# version 10
# baseline (speedup 1.0000x reference)
import os
from contextlib import ExitStack

import numpy as np

import concourse.bass as bass
import concourse.mybir as mybir
from concourse.bass_utils import run_bass_kernel_spmd

F32 = mybir.dt.float32
BF16 = mybir.dt.bfloat16
AF = mybir.ActivationFunctionType
ALU = mybir.AluOpType
AX = mybir.AxisListType

ENGINES = ("sync", "tensor", "vector", "scalar", "gpsimd")
NEG = -1.0e30
EPS = 1.0e-5
D = 1024
DT = 8
NE = 8
DFF0 = 4096
F1 = 1408
F2 = 2816
NH = 16
HD = 64
ACOLS = 72704


class Buf:
    __slots__ = ("name", "w", "readers")

    def __init__(self, name, w=None, readers=None):
        self.name = name
        self.w = w
        self.readers = list(readers or [])


class Prog:
    def __init__(self, nc, es):
        self.nc = nc
        self.es = es
        self.recs = {e: [] for e in ENGINES}
        self.sems = {}
        self.cnt = {}
        self.pending = {e: [] for e in ENGINES}
        self.eng_sem = {e: "S" + e for e in ENGINES}
        for e in ENGINES:
            self._sem(self.eng_sem[e])
        self.DMA_R = 8
        self.dma_rot = {}
        for q in ("dmaq", "dmag", "dmas"):
            self.dma_rot[q] = 0
            for i in range(self.DMA_R):
                self._sem(f"{q}{i}")

    def _sem(self, name):
        if name not in self.sems:
            self.sems[name] = self.es.enter_context(self.nc.semaphore(name))
            self.cnt[name] = 0
        return self.sems[name]

    def emit(self, eng, fn, reads=(), writes=(), war=(), sem="", amt=1,
             last=True, first=True, extra_waits=()):
        waits = set(extra_waits)
        if first:
            for b in reads:
                if b.w:
                    waits.add(b.w)
            for b in list(writes) + list(war):
                if b.w:
                    waits.add(b.w)
                for r in b.readers:
                    waits.add(r)
        else:
            for b in reads:
                if b.w:
                    waits.add(b.w)
        s = self.eng_sem[eng] if sem == "" else sem
        inc = None
        if s is not None:
            self.cnt[s] += amt
            inc = (s, amt)
            tok = (s, self.cnt[s])
            for b in writes:
                if last:
                    b.w = tok
                    b.readers = []
            for b in reads:
                b.readers.append(tok)
            if sem == "":
                for kind, b in self.pending[eng]:
                    if kind == "r":
                        b.readers.append(tok)
                    else:
                        b.w = tok
                        b.readers = []
                self.pending[eng] = []
        else:
            for b in reads:
                self.pending[eng].append(("r", b))
            for b in writes:
                self.pending[eng].append(("w", b))
        self.recs[eng].append((sorted(waits), fn, inc))

    def dma(self, eng, dst, src, reads=(), writes=(), war=()):
        q = {"sync": "dmaq", "gpsimd": "dmag", "scalar": "dmas"}[eng]
        i = self.dma_rot[q]
        self.dma_rot[q] = (i + 1) % self.DMA_R
        s = f"{q}{i}"
        waits = []
        if self.cnt[s] > 0:
            waits.append((s, self.cnt[s]))
        self.emit(eng, lambda e: e.dma_start(dst, src), reads=reads,
                  writes=writes, war=war, sem=s, amt=16, extra_waits=waits)

    def build(self, block):
        known = {e: {} for e in ENGINES}

        def run(e_obj, eng):
            k = known[eng]
            for waits, fn, inc in self.recs[eng]:
                for (s, cnt) in waits:
                    if k.get(s, -1) < cnt:
                        e_obj.wait_ge(self.sems[s], cnt)
                        k[s] = cnt
                ins = fn(e_obj)
                if inc is not None:
                    s, amt = inc
                    ins.then_inc(self.sems[s], amt)

        for eng in ENGINES:
            if self.recs[eng]:
                getattr(block, eng)(lambda e_obj, eng=eng: run(e_obj, eng))

    def barrier_rec(self, eng, sems):
        allsems = []
        for s in sems:
            if s in self.dma_rot:
                allsems += [f"{s}{i}" for i in range(self.DMA_R)]
            else:
                allsems.append(s)
        toks = [(s, self.cnt[s]) for s in allsems if self.cnt[s] > 0]
        self.recs[eng].append((toks, lambda e: None, None))


class Arena:
    def __init__(self, t, ncols):
        self.t = t
        self.free = [(0, ncols, Buf("arena0"))]

    def alloc(self, ncols, name):
        ncols = (ncols + 63) & ~63
        for i, (c0, n, b) in enumerate(self.free):
            if n >= ncols:
                if n == ncols:
                    self.free.pop(i)
                else:
                    self.free[i] = (c0 + ncols, n - ncols,
                                    Buf(b.name, w=b.w, readers=list(b.readers)))
                nb = Buf(name, w=b.w, readers=list(b.readers))
                return Region(self.t, c0, ncols, nb)
        raise MemoryError(f"arena full: {name} needs {ncols}; free={[(a, n) for a, n, _ in self.free]}")

    def freer(self, reg):
        self.free.append((reg.c0, reg.ncols, reg.buf))
        self.free.sort(key=lambda x: x[0])
        merged = []
        for c0, n, b in self.free:
            if merged and merged[-1][0] + merged[-1][1] == c0:
                pc0, pn, pb = merged[-1]
                nb = Buf("m", w=None, readers=[])
                for x in (pb, b):
                    if x.w:
                        nb.readers.append(x.w)
                    nb.readers += x.readers
                merged[-1] = (pc0, pn + n, nb)
            else:
                merged.append((c0, n, b))
        self.free = merged


class Region:
    def __init__(self, t, c0, ncols, buf):
        self.t = t
        self.c0 = c0
        self.ncols = ncols
        self.buf = buf

    def bf(self, p0, p1, a, b):
        return self.t[p0:p1, self.c0 + a:self.c0 + b]

    def f32(self, p0, p1, a, b):
        return self.t[p0:p1, self.c0 + 2 * a:self.c0 + 2 * b].bitcast(F32)


def fold_bufs(reg, bufs):
    for b in bufs:
        if b.w:
            reg.buf.readers.append(b.w)
        reg.buf.readers.extend(b.readers)


class Ctx:
    pass


def build_core_program(T, parts, dbg=None):
    TT = T // 128
    nc = bass.Bass("TRN2", target_bir_lowering=False)
    c = Ctx()
    c.T, c.TT = T, TT

    def dram(name, shape):
        return nc.dram_tensor(name, shape, F32, kind="ExternalInput").ap()

    c.x = dram("x", [T, D])
    c.OUT = nc.dram_tensor("OUT", [T, D], F32, kind="ExternalOutput").ap()
    if "L0" in parts:
        c.Wq, c.Wk, c.Wv = dram("Wq", [D, D]), dram("Wk", [D, D]), dram("Wv", [D, D])
        c.Wr = dram("Wr", [D, NE])
        c.W1 = dram("W1", [NE, DFF0 // 128, 128, D])
        c.W2 = dram("W2", [NE, DFF0 // 128, 128, D])
        c.W3 = dram("W3", [NE, DFF0, D])
        c.w3bf = nc.dram_tensor("w3bf", [NE * (DFF0 // 128), 128, D], BF16).ap()
        c.Bw3bf = [Buf(f"w3bf{i}") for i in range(NE * (DFF0 // 128))]
        c.w1bf = nc.dram_tensor("w1bf", [NE * (DFF0 // 128), 128, D], BF16).ap()
        c.Bw1bf = [Buf(f"w1bf{i}") for i in range(NE * (DFF0 // 128))]
        c.w2bf = nc.dram_tensor("w2bf", [NE * (DFF0 // 128), 128, D], BF16).ap()
        c.Bw2bf = [Buf(f"w2bf{i}") for i in range(NE * (DFF0 // 128))]
    if any(s.startswith("L1") for s in parts):
        c.Wqkv = dram("Wqkv", [D, 3 * D])
        c.Wgate, c.Wproj = dram("Wgate", [D, D]), dram("Wproj", [D, D])
        c.gwT = dram("gwT", [D, NE])
        c.w_gate = dram("w_gate", [NE, F1 // 128, 128, D])
        c.w_up = dram("w_up", [NE, F1 // 128, 128, D])
        c.w_down = dram("w_down", [NE, F1, D])
        c.sw_gate = dram("sw_gate", [F2 // 128, 128, D])
        c.sw_up = dram("sw_up", [F2 // 128, 128, D])
        c.sw_down = dram("sw_down", [F2, D])
        c.swdbf = nc.dram_tensor("swdbf", [F2 // 128, 128, D], BF16).ap()
        c.Bswdbf = [Buf(f"swdbf{i}") for i in range(F2 // 128)]
        c.gate_hbm = nc.dram_tensor("gate_hbm", [T, D], BF16).ap()
        c.xnorm_hbm = nc.dram_tensor("xnorm_hbm", [T, D], BF16).ap()
        if os.environ.get("KDBG"):
            c.dbg_bf = nc.dram_tensor("dbg_bf", [TT, 8, 128, 2048], BF16).ap()
            c.dbg_f = nc.dram_tensor("dbg_f", [TT, 128, 64], F32).ap()
    c.consts = nc.dram_tensor("consts", [128, 3592], F32, kind="ExternalInput").ap()
    c.crow = nc.dram_tensor("crow", [1, 12288], F32, kind="ExternalInput").ap()

    es = ExitStack()
    with es:
        arena_t = es.enter_context(nc.sbuf_tensor("arena", [128, ACOLS], BF16))
        xres_t = es.enter_context(nc.sbuf_tensor("xres", [128, TT * 1024], F32))
        P = [es.enter_context(nc.psum_tensor(f"P{i}", [128, 512], F32)) for i in range(8)]
        p = Prog(nc, es)
        block = es.enter_context(nc.Block())

        c.nc, c.p, c.P = nc, p, [b[:] for b in P]
        c.PB = [b.bitcast(BF16) for b in c.P]
        c.BP = [Buf(f"P{i}") for i in range(8)]
        c.ar = Arena(arena_t, ACOLS)
        c.xres_t = xres_t
        c.Bx = [Buf(f"x{t}") for t in range(TT)]
        c.xres = lambda t: xres_t[:, 1024 * t:1024 * (t + 1)]

        c.ident = c.ar.alloc(128, "ident")
        c.ones = c.ar.alloc(512, "ones")
        c.maskS = c.ar.alloc(256, "maskS")
        c.maskA = c.ar.alloc(128, "maskA")
        c.stats = c.ar.alloc(2 * 8 * TT, "stats")
        c.cw = c.ar.alloc(2 * 8 * TT, "cw")

        setup(c)
        load_x(c)
        if "L0" in parts:
            xnT = ln_phase(c, layer=0)
            l0b = c.ar.alloc(2 * 8, "l0br")
            p.dma("sync", l0b.f32(0, 128, 0, 8), c.consts[:, 0:8], writes=[l0b.buf])
            rtile, rfini = router_prep(c, xnT, c.Wr, with_bias=True, sigma_scale=1.0,
                                       brap=l0b.f32(0, 128, 0, 8), brbuf=l0b.buf,
                                       bank_of=lambda t: 7)
            attn0(c, xnT, qi_hook=rtile)
            rfini()
            c.ar.freer(l0b)
            moe(c, xnT, dff=DFF0, wg=c.W1, wu=c.W2, wd=c.W3,
                router_w=None, sigma_scale=1.0, bias=True, name="f0")
            fold_bufs(xnT, xnT.tbufs)
            c.ar.freer(xnT)
        if any(s.startswith("L1") for s in parts):
            sub = [s[2:] for s in parts if s.startswith("L1")][0] or "ams"
            xnT = ln_phase(c, layer=1)
            if "a" in sub:
                gate_phase(c, xnT)
            if "m" in sub:
                moe(c, xnT, dff=F1, wg=c.w_gate, wu=c.w_up, wd=c.w_down,
                    router_w=c.gwT, sigma_scale=2.5, bias=False, name="f1")
            if "s" in sub:
                swiglu(c, xnT)
            c.ar.freer(xnT)
            if "a" in sub and "g" not in sub:
                scan_phase(c)
        store_x(c)
        p.barrier_rec("sync", ["dmaq", "dmag", "dmas", "Svector", "Sscalar",
                               "Stensor", "Sgpsimd"])
        if dbg is not None:
            dbg(c)
        p.build(block)
    return nc


def mm_group(c, bank_ids, mms, tail=None):
    p = c.p
    seq = list(mms) + list(tail or [])
    n = len(seq)
    banks = [c.BP[b] for b in bank_ids]
    for i, (fn, reads) in enumerate(seq):
        p.emit("tensor", fn, reads=reads,
               writes=(banks if i == n - 1 else []),
               war=(banks if i == 0 else ()),
               sem=(None if i < n - 1 else ""),
               first=(i == 0), last=(i == n - 1))


def trans_batch(c, pairs, bank, dst_ap, dst_buf):
    p = c.p
    assert len(pairs) <= 8
    for i, (src, sbuf) in enumerate(pairs):
        dst = c.PB[bank][:, 128 * i:128 * (i + 1)]
        p.emit("tensor",
               lambda t, dst=dst, src=src: t.transpose(dst, src, c.ident.bf(0, 128, 0, 128)),
               reads=[sbuf, c.ident.buf],
               writes=([c.BP[bank]] if i == len(pairs) - 1 else []),
               war=([c.BP[bank]] if i == 0 else ()),
               sem=(None if i < len(pairs) - 1 else ""),
               first=(i == 0), last=(i == len(pairs) - 1))
    n = len(pairs)
    src_ap = c.PB[bank][:, 0:128 * n]
    if dst_ap.ndim == 3:
        src_ap = src_ap.rearrange("p (d c) -> p d c", c=128)
    p.emit("vector", lambda v: v.tensor_copy(dst_ap, src_ap),
           reads=[c.BP[bank]], writes=[dst_buf])


def setup(c):
    p = c.p
    p.emit("gpsimd", lambda g: g.memset(c.ones.bf(0, 128, 0, 512), 1.0),
           writes=[c.ones.buf])
    idb = c.ident
    p.emit("gpsimd", lambda g: g.iota(idb.bf(0, 128, 0, 128), [[1, 128]], base=0,
                                      channel_multiplier=-1,
                                      allow_small_or_imprecise_dtypes=True),
           writes=[idb.buf])
    p.emit("vector", lambda v: v.tensor_scalar(out=idb.bf(0, 128, 0, 128),
                                               in0=idb.bf(0, 128, 0, 128),
                                               scalar1=0.0, scalar2=None,
                                               op0=ALU.is_equal),
           reads=[idb.buf], writes=[idb.buf])
    ms = c.maskS
    p.emit("gpsimd", lambda g: g.iota(ms.f32(0, 128, 0, 128), [[1, 128]], base=0,
                                      channel_multiplier=-1,
                                      allow_small_or_imprecise_dtypes=True),
           writes=[ms.buf])
    p.emit("vector", lambda v: v.tensor_scalar(out=ms.f32(0, 128, 0, 128),
                                               in0=ms.f32(0, 128, 0, 128),
                                               scalar1=0.0, scalar2=NEG,
                                               op0=ALU.is_gt, op1=ALU.mult),
           reads=[ms.buf], writes=[ms.buf])
    ma = c.maskA
    p.emit("gpsimd", lambda g: g.iota(ma.bf(0, 128, 0, 128), [[1, 128]], base=0,
                                      channel_multiplier=-1,
                                      allow_small_or_imprecise_dtypes=True),
           writes=[ma.buf])
    p.emit("vector", lambda v: v.tensor_scalar(out=ma.bf(0, 128, 0, 128),
                                               in0=ma.bf(0, 128, 0, 128),
                                               scalar1=0.0, scalar2=None,
                                               op0=ALU.is_ge),
           reads=[ma.buf], writes=[ma.buf])


def load_x(c):
    for t in range(c.TT):
        c.p.dma("sync", c.xres(t), c.x[128 * t:128 * (t + 1), :], writes=[c.Bx[t]])


def store_x(c):
    for t in range(c.TT):
        c.p.dma("sync", c.OUT[128 * t:128 * (t + 1), :], c.xres(t), reads=[c.Bx[t]])


def stream_wslabs(c, src, n_slabs, cols, name):
    reg = c.ar.alloc(n_slabs * cols, name)
    reg.sbufs = []
    for d in range(n_slabs):
        b = Buf(f"{name}_{d}", w=reg.buf.w, readers=list(reg.buf.readers))
        reg.sbufs.append(b)
        c.p.dma("gpsimd", reg.bf(0, 128, d * cols, (d + 1) * cols),
                src[128 * d:128 * (d + 1), :], writes=[b])
    return reg


def free_wslabs(c, reg):
    fold_bufs(reg, reg.sbufs)
    c.ar.freer(reg)


def xn_tiles(xnT, t0, t1):
    return xnT.tbufs[t0 // 128:(t1 + 127) // 128]


def xnT_dst(c, xnT, t):
    return bass.AP(xnT.t, xnT.c0 + 128 * t,
                   [[xnT.t.shape[1], 128], [c.T, DT], [1, 128]])


def ln_phase(c, layer):
    p, ar, T, TT = c.p, c.ar, c.T, c.TT
    xnT = ar.alloc(DT * T, f"xnT{layer}")
    xnT.tbufs = [Buf(f"xnT{layer}_{t}") for t in range(TT)]
    for b in xnT.tbufs:
        b.w = xnT.buf.w
        b.readers = list(xnT.buf.readers)
    scr = ar.alloc((6 * 1024 if layer == 1 else 3 * 1024 + 128), f"lnscr{layer}")
    Bscr = scr.buf
    st = c.stats
    if layer == 1:
        gb = ar.alloc(2 * 2048, "lngb")
        p.dma("sync", gb.f32(0, 128, 0, 2048), c.consts[:, 520:2568], writes=[gb.buf])
        xnrm = ar.alloc(2 * 1024, "xnrmtile")
        xnrmb = [Buf("xr0"), Buf("xr1")]
        c.Bxn = [Buf(f"xn{t}") for t in range(TT)]
    for t in range(TT):
        bn6 = scr.f32(0, 128, 1536, 1548)
        mv = scr.f32(0, 128, 1548, 1550)
        for hb in range(2):
            p.emit("vector", lambda v, t=t, bn6=bn6, hb=hb: v.bn_stats(
                bn6[:, 6 * hb:6 * (hb + 1)], c.xres(t)[:, 512 * hb:512 * (hb + 1)]),
                   reads=[c.Bx[t]], writes=[Bscr])
        p.emit("vector", lambda v, bn6=bn6, mv=mv: v.bn_aggr(mv, bn6),
               reads=[Bscr], writes=[Bscr])
        m = st.f32(0, 128, t, t + 1)
        r = st.f32(0, 128, TT + t, TT + t + 1)
        sd = st.f32(0, 128, 4 * TT, 4 * TT + 1)
        p.emit("vector", lambda v, m=m, mv=mv: v.tensor_copy(m, mv[:, 0:1]),
               reads=[Bscr], writes=[st.buf])
        p.emit("scalar", lambda s, sd=sd, mv=mv: s.activation(sd, mv[:, 1:2], AF.Sqrt,
                                                              bias=np.float32(EPS), scale=1.0),
               reads=[Bscr], writes=[st.buf])
        p.emit("vector", lambda v, r=r, sd=sd: v.reciprocal(r, sd),
               reads=[st.buf], writes=[st.buf])
        xn = scr.bf(0, 128, 0, 1024)
        if layer == 0:
            p.emit("vector", lambda v, t=t, xn=xn, m=m, r=r: v.tensor_scalar(
                out=xn, in0=c.xres(t), scalar1=m, scalar2=r,
                op0=ALU.subtract, op1=ALU.mult),
                   reads=[c.Bx[t], st.buf], writes=[Bscr])
        else:
            xf = scr.f32(0, 128, 512, 1536)
            p.emit("vector", lambda v, t=t, xf=xf, m=m, r=r: v.tensor_scalar(
                out=xf, in0=c.xres(t), scalar1=m, scalar2=r,
                op0=ALU.subtract, op1=ALU.mult),
                   reads=[c.Bx[t], st.buf], writes=[Bscr])
            for hb in range(2):
                p.emit("vector", lambda v, bn6=bn6, xf=xf, hb=hb: v.bn_stats(
                    bn6[:, 6 * hb:6 * (hb + 1)], xf[:, 512 * hb:512 * (hb + 1)]),
                       reads=[Bscr], writes=[Bscr])
            p.emit("vector", lambda v, bn6=bn6, mv=mv: v.bn_aggr(mv, bn6),
                   reads=[Bscr], writes=[Bscr])
            sd2 = st.f32(0, 128, 4 * TT + 1, 4 * TT + 2)
            r2 = st.f32(0, 128, 4 * TT + 2, 4 * TT + 3)
            p.emit("scalar", lambda s, sd2=sd2, mv=mv: s.activation(
                sd2, mv[:, 1:2], AF.Sqrt, bias=np.float32(EPS), scale=1.0),
                   reads=[Bscr], writes=[st.buf])
            p.emit("vector", lambda v, r2=r2, sd2=sd2: v.reciprocal(r2, sd2),
                   reads=[st.buf], writes=[st.buf])
            xm = scr.f32(0, 128, 1550, 2574)
            p.emit("vector", lambda v, xm=xm, xf=xf, mv=mv, r2=r2: v.tensor_scalar(
                out=xm, in0=xf, scalar1=mv[:, 0:1], scalar2=r2,
                op0=ALU.subtract, op1=ALU.mult),
                   reads=[Bscr, st.buf], writes=[Bscr])
            p.emit("vector", lambda v, xm=xm, gb=gb: v.tensor_mul(
                xm, xm, gb.f32(0, 128, 0, 1024)),
                   reads=[Bscr, gb.buf], writes=[Bscr])
            gi = t % 2
            p.emit("vector", lambda v, xm=xm, gb=gb, gi=gi: v.tensor_add(
                xnrm.bf(0, 128, 1024 * gi, 1024 * (gi + 1)), xm,
                gb.f32(0, 128, 1024, 2048)),
                   reads=[Bscr, gb.buf], writes=[xnrmb[gi]])
            p.dma("sync", c.xnorm_hbm[128 * t:128 * (t + 1), :],
                  xnrm.bf(0, 128, 1024 * gi, 1024 * (gi + 1)),
                  reads=[xnrmb[gi]], writes=[c.Bxn[t]])
            p.emit("vector", lambda v, xn=xn, xf=xf: v.tensor_copy(xn, xf),
                   reads=[Bscr], writes=[Bscr])
        trans_batch(c, [(xn[:, 128 * d:128 * (d + 1)], Bscr) for d in range(DT)],
                    7, xnT_dst(c, xnT, t), xnT.tbufs[t])
    if layer == 1:
        fold_bufs(xnrm, xnrmb)
        c.ar.freer(xnrm)
        c.ar.freer(gb)
    c.ar.freer(scr)
    return xnT


def attn0(c, xnT, qi_hook=None):
    p, ar, T, TT = c.p, c.ar, c.T, c.TT
    kT = ar.alloc(DT * T, "kT")
    vtok = ar.alloc(TT * 1024, "vtok")
    nchunk = (T + 511) // 512
    wk = stream_wslabs(c, c.Wk, DT, 1024, "wk")
    for do in range(DT):
        for ch in range(nchunk):
            t0, t1 = 512 * ch, min(512 * (ch + 1), T)
            bank = (do * nchunk + ch) % 4
            mm_group(c, [bank],
                     [(lambda t, do=do, d=d, t0=t0, t1=t1, bank=bank: t.matmul(
                         c.P[bank][:, 0:t1 - t0],
                         wk.bf(0, 128, 1024 * d + 128 * do, 1024 * d + 128 * (do + 1)),
                         xnT.bf(0, 128, T * d + t0, T * d + t1),
                         start=(d == 0), stop=(d == DT - 1)),
                       [wk.sbufs[d]] + xn_tiles(xnT, t0, t1))
                      for d in range(DT)])
            p.emit("scalar", lambda s, do=do, t0=t0, t1=t1, bank=bank: s.activation(
                kT.bf(0, 128, T * do + t0, T * do + t1), c.P[bank][:, 0:t1 - t0],
                AF.Copy, bias=0.0, scale=1.0),
                   reads=[c.BP[bank]], writes=[kT.buf])
    free_wslabs(c, wk)
    wv = stream_wslabs(c, c.Wv, DT, 1024, "wv")
    for t in range(TT):
        b0 = 4 + (t % 2) * 2
        mm_group(c, [b0, b0 + 1],
                 [(lambda tt, t=t, d=d, h=h, b0=b0: tt.matmul(
                     c.P[b0 + h][:, 0:512],
                     xnT.bf(0, 128, T * d + 128 * t, T * d + 128 * (t + 1)),
                     wv.bf(0, 128, 1024 * d + 512 * h, 1024 * d + 512 * (h + 1)),
                     start=(d == 0), stop=(d == DT - 1)), [xnT.tbufs[t], wv.sbufs[d]])
                  for d in range(DT) for h in range(2)])
        for h in range(2):
            p.emit("scalar", lambda s, t=t, h=h, b0=b0: s.activation(
                vtok.bf(0, 128, 1024 * t + 512 * h, 1024 * t + 512 * (h + 1)),
                c.P[b0 + h][:, 0:512], AF.Copy, bias=0.0, scale=1.0),
                   reads=[c.BP[b0 + h]], writes=[vtok.buf])
    free_wslabs(c, wv)
    wq = stream_wslabs(c, c.Wq, DT, 1024, "wq")
    qT = ar.alloc(DT * 128, "qTqi")
    SstA = [ar.alloc(2 * T, "Sstrip0"), ar.alloc(2 * T, "Sstrip1")]
    Pst = ar.alloc(T, "Pstrip")
    PTst = ar.alloc(T, "PTstrip")
    smA = [ar.alloc(64, "softsmall0"), ar.alloc(64, "softsmall1")]
    for qi in range(TT):
        Sst = SstA[qi % 2]
        sm = smA[qi % 2]
        for do in range(DT):
            bank = do % 2
            mm_group(c, [bank],
                     [(lambda t, do=do, d=d, qi=qi, bank=bank: t.matmul(
                         c.P[bank][:, 0:128],
                         wq.bf(0, 128, 1024 * d + 128 * do, 1024 * d + 128 * (do + 1)),
                         xnT.bf(0, 128, T * d + 128 * qi, T * d + 128 * (qi + 1)),
                         start=(d == 0), stop=(d == DT - 1)), [wq.sbufs[d], xnT.tbufs[qi]])
                      for d in range(DT)])
            p.emit("scalar", lambda s, do=do, bank=bank: s.activation(
                qT.bf(0, 128, 128 * do, 128 * (do + 1)), c.P[bank][:, 0:128],
                AF.Copy, bias=0.0, scale=1.0),
                   reads=[c.BP[bank]], writes=[qT.buf])
        w = 128 * (qi + 1)
        chunks = []
        s0 = 0
        while s0 < 128 * qi:
            s1 = min(s0 + 512, 128 * qi)
            chunks.append((s0, s1, False))
            s0 = s1
        chunks.append((128 * qi, w, True))
        for ci, (s0, s1, diag) in enumerate(chunks):
            bank = 2 + ci % 2
            mm_group(c, [bank],
                     [(lambda t, d=d, s0=s0, s1=s1, bank=bank: t.matmul(
                         c.P[bank][:, 0:s1 - s0],
                         qT.bf(0, 128, 128 * d, 128 * (d + 1)),
                         kT.bf(0, 128, T * d + s0, T * d + s1),
                         start=(d == 0), stop=(d == DT - 1)), [qT.buf, kT.buf])
                      for d in range(DT)])
            if not diag:
                p.emit("scalar", lambda s, s0=s0, s1=s1, bank=bank: s.activation(
                    Sst.f32(0, 128, s0, s1), c.P[bank][:, 0:s1 - s0],
                    AF.Copy, bias=0.0, scale=np.float32(0.03125)),
                       reads=[c.BP[bank]], writes=[Sst.buf])
            else:
                p.emit("vector", lambda v, s0=s0, s1=s1, bank=bank: v.scalar_tensor_tensor(
                    out=Sst.f32(0, 128, s0, s1), in0=c.P[bank][:, 0:s1 - s0],
                    scalar=0.03125, in1=c.maskS.f32(0, 128, 0, s1 - s0),
                    op0=ALU.mult, op1=ALU.add),
                       reads=[c.BP[bank], c.maskS.buf], writes=[Sst.buf])
        negmax = sm.f32(0, 128, 0, 1)
        rowsum = sm.f32(0, 128, 1, 2)
        recip = sm.f32(0, 128, 2, 3)
        p.emit("vector", lambda v, w=w, negmax=negmax: v.tensor_reduce(
            negmax, Sst.f32(0, 128, 0, w), axis=AX.X, op=ALU.max, negate=True),
               reads=[Sst.buf], writes=[sm.buf])
        p.emit("scalar", lambda s, w=w, negmax=negmax, rowsum=rowsum: s.activation(
            Pst.bf(0, 128, 0, w), Sst.f32(0, 128, 0, w), AF.Exp,
            bias=negmax, scale=1.0, accum_out=rowsum),
               reads=[Sst.buf, sm.buf], writes=[Pst.buf, sm.buf])
        p.emit("vector", lambda v, recip=recip, rowsum=rowsum: v.reciprocal(recip, rowsum),
               reads=[sm.buf], writes=[sm.buf])
        for g0 in range(0, qi + 1, 8):
            g1 = min(g0 + 8, qi + 1)
            dst = bass.AP(PTst.t, PTst.c0 + 128 * g0,
                          [[PTst.t.shape[1], 128], [1, 128 * (g1 - g0)]])
            trans_batch(c, [(Pst.bf(0, 128, 128 * s, 128 * (s + 1)), Pst.buf)
                            for s in range(g0, g1)], 6, dst, PTst.buf)
        for h in range(2):
            bank = 4 + h
            mm_group(c, [bank],
                     [(lambda t, s=s, h=h, qi=qi, bank=bank: t.matmul(
                         c.P[bank][:, 0:512],
                         PTst.bf(0, 128, 128 * s, 128 * (s + 1)),
                         vtok.bf(0, 128, 1024 * s + 512 * h, 1024 * s + 512 * (h + 1)),
                         start=(s == 0), stop=(s == qi)), [PTst.buf, vtok.buf])
                      for s in range(qi + 1)])
            p.emit("vector", lambda v, qi=qi, h=h, recip=recip, bank=bank:
                   v.scalar_tensor_tensor(
                out=c.xres(qi)[:, 512 * h:512 * (h + 1)], in0=c.P[bank][:, 0:512],
                scalar=recip, in1=c.xres(qi)[:, 512 * h:512 * (h + 1)],
                op0=ALU.mult, op1=ALU.add),
                   reads=[c.BP[bank], sm.buf, c.Bx[qi]], writes=[c.Bx[qi]])
        if qi_hook is not None:
            qi_hook(qi)
    free_wslabs(c, wq)
    for r in (qT, SstA[0], SstA[1], Pst, PTst, smA[0], smA[1], kT, vtok):
        c.ar.freer(r)


def router_prep(c, xnT, router_w, with_bias, sigma_scale, brap=None, brbuf=None,
                bank_of=None):
    p, ar, T, TT = c.p, c.ar, c.T, c.TT
    rw = stream_wslabs(c, router_w, DT, NE, "rw")
    scr = ar.alloc(128, "rscr")
    B = scr.buf

    def tile(t):
        bank = (t % 4) if bank_of is None else bank_of(t)
        mm_group(c, [bank],
                 [(lambda tt, t=t, d=d, bank=bank: tt.matmul(
                     c.P[bank][:, 0:NE],
                     xnT.bf(0, 128, T * d + 128 * t, T * d + 128 * (t + 1)),
                     rw.bf(0, 128, NE * d, NE * (d + 1)),
                     start=(d == 0), stop=(d == DT - 1)), [xnT.tbufs[t], rw.sbufs[d]])
                  for d in range(DT)])
        lg = scr.f32(0, 128, 0, NE)
        if with_bias:
            p.emit("vector", lambda v, lg=lg, bank=bank: v.tensor_add(
                lg, c.P[bank][:, 0:NE], brap),
                   reads=[c.BP[bank], brbuf], writes=[B])
        else:
            p.emit("vector", lambda v, lg=lg, bank=bank: v.tensor_copy(
                lg, c.P[bank][:, 0:NE]),
                   reads=[c.BP[bank]], writes=[B])
        m1 = scr.f32(0, 128, NE, NE + 1)
        m2 = scr.f32(0, 128, NE + 1, NE + 2)
        w1 = scr.f32(0, 128, NE + 2, NE + 3)
        w2 = scr.f32(0, 128, NE + 3, NE + 4)
        d12 = scr.f32(0, 128, NE + 4, NE + 5)
        eq1 = scr.f32(0, 128, 16, 24)
        eq2 = scr.f32(0, 128, 24, 32)
        l2 = scr.f32(0, 128, 32, 40)
        p.emit("vector", lambda v, m1=m1, lg=lg: v.tensor_reduce(
            m1, lg, axis=AX.X, op=ALU.max), reads=[B], writes=[B])
        p.emit("vector", lambda v, eq1=eq1, lg=lg, m1=m1: v.tensor_scalar(
            out=eq1, in0=lg, scalar1=m1, scalar2=None, op0=ALU.is_equal),
               reads=[B], writes=[B])
        p.emit("vector", lambda v, l2=l2, eq1=eq1, lg=lg: v.scalar_tensor_tensor(
            out=l2, in0=eq1, scalar=NEG, in1=lg, op0=ALU.mult, op1=ALU.add),
               reads=[B], writes=[B])
        p.emit("vector", lambda v, m2=m2, l2=l2: v.tensor_reduce(
            m2, l2, axis=AX.X, op=ALU.max), reads=[B], writes=[B])
        p.emit("vector", lambda v, eq2=eq2, l2=l2, m2=m2: v.tensor_scalar(
            out=eq2, in0=l2, scalar1=m2, scalar2=None, op0=ALU.is_equal),
               reads=[B], writes=[B])
        p.emit("vector", lambda v, d12=d12, m2=m2, m1=m1: v.tensor_scalar(
            out=d12, in0=m2, scalar1=m1, scalar2=None, op0=ALU.subtract),
               reads=[B], writes=[B])
        p.emit("scalar", lambda s, d12=d12: s.activation(d12, d12, AF.Exp,
                                                         bias=0.0, scale=1.0),
               reads=[B], writes=[B])
        p.emit("vector", lambda v, d12=d12: v.tensor_scalar(
            out=d12, in0=d12, scalar1=1.0, scalar2=None, op0=ALU.add),
               reads=[B], writes=[B])
        p.emit("vector", lambda v, w1=w1, d12=d12: v.reciprocal(w1, d12),
               reads=[B], writes=[B])
        p.emit("vector", lambda v, w1=w1: v.tensor_scalar(
            out=w1, in0=w1, scalar1=sigma_scale, scalar2=None, op0=ALU.mult),
               reads=[B], writes=[B])
        p.emit("vector", lambda v, w2=w2, w1=w1: v.tensor_scalar(
            out=w2, in0=w1, scalar1=-1.0, scalar2=sigma_scale,
            op0=ALU.mult, op1=ALU.add), reads=[B], writes=[B])
        cwc = c.cw.f32(0, 128, NE * t, NE * (t + 1))
        p.emit("vector", lambda v, eq2=eq2, w2=w2: v.tensor_scalar(
            out=eq2, in0=eq2, scalar1=w2, scalar2=None, op0=ALU.mult),
               reads=[B], writes=[B])
        p.emit("vector", lambda v, cwc=cwc, eq1=eq1, w1=w1, eq2=eq2:
               v.scalar_tensor_tensor(
            out=cwc, in0=eq1, scalar=w1, in1=eq2, op0=ALU.mult, op1=ALU.add),
               reads=[B], writes=[c.cw.buf])

    def fini():
        free_wslabs(c, rw)
        c.ar.freer(scr)
    return tile, fini


def router_phase(c, xnT, router_w, with_bias, sigma_scale, brap=None, brbuf=None):
    tile, fini = router_prep(c, xnT, router_w, with_bias, sigma_scale, brap, brbuf)
    for t in range(c.TT):
        tile(t)
    fini()


def moe(c, xnT, dff, wg, wu, wd, router_w, sigma_scale, bias, name):
    p, ar, T, TT = c.p, c.ar, c.T, c.TT
    l0c = None
    if bias:
        l0c = ar.alloc(2 * 520, name + "_bias")
        p.dma("sync", l0c.f32(0, 128, 0, 8), c.consts[:, 0:8], writes=[l0c.buf])
        p.dma("sync", l0c.f32(0, 128, 8, 520), c.consts[:, 8:520],
              writes=[l0c.buf], war=[l0c.buf])
        b3row = ar.alloc(1024, name + "_b3")
    if router_w is not None:
        router_phase(c, xnT, router_w, with_bias=bias, sigma_scale=sigma_scale,
                     brap=(l0c.f32(0, 128, 0, 8) if bias else None),
                     brbuf=(l0c.buf if bias else None))
    hscr = ar.alloc(2 * 512, name + "_hscr")
    nj = dff // 128
    CH = min(1024, T)
    nch = T // CH
    h = ar.alloc(nj * CH, name + "_h")
    wbuf = [ar.alloc(1024, f"{name}_w{i}") for i in range(8)]
    w3buf = [ar.alloc(1024, f"{name}_w3{i}") for i in range(4)]
    wd_res = None
    if dff == F1:
        wd_res = ar.alloc(nj * 1024, name + "_wdres")
    nsub = max(CH // 512, 1)
    sub = CH // nsub
    for e in range(NE):
        if bias:
            p.dma("gpsimd", b3row.bf(0, 1, 0, 1024),
                  c.crow[0:1, 1024 * e:1024 * (e + 1)],
                  writes=[b3row.buf], war=[b3row.buf])
        for ch in range(nch):
            t0 = CH * ch
            for j in range(nj):
                if wd_res is not None and ch == 0 and j == 2:
                    for k in range(nj):
                        wsrc = bass.AP(wd.tensor, wd.offset + e * dff * D + 128 * k * D,
                                       [[D, 128], [1, D]])
                        c.p.dma("gpsimd", wd_res.bf(0, 128, 1024 * k, 1024 * (k + 1)),
                                wsrc, writes=[wd_res.buf], war=[wd_res.buf])
                wb1 = wbuf[(j % 4) * 2]
                wb2 = wbuf[(j % 4) * 2 + 1]
                src1 = bass.AP(wg.tensor, wg.offset + (e * nj + j) * 128 * D,
                               [[D, 128], [1, D]])
                src2 = bass.AP(wu.tensor, wu.offset + (e * nj + j) * 128 * D,
                               [[D, 128], [1, D]])
                if nch > 1 and dff == DFF0:
                    if ch == 0:
                        p.dma("gpsimd", wb1.bf(0, 128, 0, 1024), src1,
                              writes=[wb1.buf], war=[wb1.buf])
                        p.dma("sync", c.w1bf[e * nj + j], wb1.bf(0, 128, 0, 1024),
                              reads=[wb1.buf], writes=[c.Bw1bf[e * nj + j]])
                        p.dma("gpsimd", wb2.bf(0, 128, 0, 1024), src2,
                              writes=[wb2.buf], war=[wb2.buf])
                        p.dma("sync", c.w2bf[e * nj + j], wb2.bf(0, 128, 0, 1024),
                              reads=[wb2.buf], writes=[c.Bw2bf[e * nj + j]])
                    else:
                        p.dma("gpsimd", wb1.bf(0, 128, 0, 1024), c.w1bf[e * nj + j],
                              reads=[c.Bw1bf[e * nj + j]], writes=[wb1.buf], war=[wb1.buf])
                        p.dma("scalar", wb2.bf(0, 128, 0, 1024), c.w2bf[e * nj + j],
                              reads=[c.Bw2bf[e * nj + j]], writes=[wb2.buf], war=[wb2.buf])
                else:
                    p.dma("gpsimd", wb1.bf(0, 128, 0, 1024), src1,
                          writes=[wb1.buf], war=[wb1.buf])
                    p.dma("gpsimd", wb2.bf(0, 128, 0, 1024), src2,
                          writes=[wb2.buf], war=[wb2.buf])
                bk = 0 if (j % 2 == 0) else 4
                banks = [bk + mat * nsub + si for mat in range(2) for si in range(nsub)]
                mms = []
                for d in range(DT):
                    for mat, wb in ((0, wb1), (1, wb2)):
                        for si in range(nsub):
                            bank = bk + mat * nsub + si
                            mms.append((lambda t, d=d, si=si, wb=wb, bank=bank, t0=t0:
                                        t.matmul(
                                c.P[bank][:, 0:sub],
                                wb.bf(0, 128, 128 * d, 128 * (d + 1)),
                                xnT.bf(0, 128, T * d + t0 + sub * si,
                                       T * d + t0 + sub * (si + 1)),
                                start=(d == 0), stop=(d == DT - 1)),
                                        [wb.buf] + xn_tiles(xnT, t0, t0 + CH)))
                mm_group(c, banks, mms)
                for si in range(nsub):
                    b1k = bk + si
                    b2k = bk + nsub + si
                    if bias:
                        b1ap = l0c.f32(0, 128, 8 + 32 * e + j, 9 + 32 * e + j)
                        b2ap = l0c.f32(0, 128, 264 + 32 * e + j, 265 + 32 * e + j)
                        p.emit("scalar", lambda s, b1k=b1k, b1ap=b1ap:
                               s.activation(hscr.f32(0, 128, 0, sub),
                                            c.P[b1k][:, 0:sub], AF.Silu,
                                            bias=b1ap, scale=1.0),
                               reads=[c.BP[b1k], l0c.buf], writes=[hscr.buf])
                        p.emit("vector", lambda v, j=j, si=si, b2k=b2k, b2ap=b2ap:
                               v.scalar_tensor_tensor(
                                   out=h.bf(0, 128, CH * j + sub * si,
                                            CH * j + sub * (si + 1)),
                                   in0=c.P[b2k][:, 0:sub], scalar=b2ap,
                                   in1=hscr.f32(0, 128, 0, sub),
                                   op0=ALU.add, op1=ALU.mult),
                               reads=[c.BP[b2k], hscr.buf, l0c.buf], writes=[h.buf])
                    else:
                        p.emit("scalar", lambda s, b1k=b1k: s.activation(
                            hscr.f32(0, 128, 0, sub), c.P[b1k][:, 0:sub],
                            AF.Silu, bias=0.0, scale=1.0),
                               reads=[c.BP[b1k]], writes=[hscr.buf])
                        p.emit("vector", lambda v, j=j, si=si, b2k=b2k: v.tensor_mul(
                            h.bf(0, 128, CH * j + sub * si, CH * j + sub * (si + 1)),
                            hscr.f32(0, 128, 0, sub), c.P[b2k][:, 0:sub]),
                               reads=[c.BP[b2k], hscr.buf], writes=[h.buf])
            PTOK = min(512, CH)
            npass = CH // PTOK
            ntt = PTOK // 128
            for ps in range(npass):
                pt0 = t0 + PTOK * ps
                mms = {b: [] for b in range(2 * ntt)}
                first_use = (ch == 0 and ps == 0)
                for k in range(nj):
                    if wd_res is None:
                        w3b = w3buf[k % 4]
                        cb = c.Bw3bf[e * nj + k]
                        if first_use:
                            p.dma("gpsimd", w3b.bf(0, 128, 0, 1024),
                                  bass.AP(wd.tensor, wd.offset + e * dff * D + 128 * k * D,
                                          [[D, 128], [1, D]]),
                                  writes=[w3b.buf], war=[w3b.buf])
                            p.dma("sync", c.w3bf[e * nj + k], w3b.bf(0, 128, 0, 1024),
                                  reads=[w3b.buf], writes=[cb])
                        else:
                            p.dma("scalar", w3b.bf(0, 128, 0, 1024),
                                  c.w3bf[e * nj + k],
                                  reads=[cb], writes=[w3b.buf], war=[w3b.buf])
                    else:
                        w3b = None
                    for tt4 in range(ntt):
                        for hf in range(2):
                            bank = tt4 * 2 + hf
                            if w3b is not None:
                                wap = w3b.bf(0, 128, 512 * hf, 512 * (hf + 1))
                                wbb = w3b.buf
                            else:
                                wap = wd_res.bf(0, 128, 1024 * k + 512 * hf,
                                                1024 * k + 512 * (hf + 1))
                                wbb = wd_res.buf
                            last_k = (k == nj - 1) and not bias
                            k_last = (tt4 == ntt - 1 and hf == 1)
                            p.emit("tensor", lambda t, k=k, tt4=tt4, wap=wap, bank=bank,
                                   pt0=pt0, t0=t0, last_k=last_k: t.matmul(
                                c.P[bank][:, 0:512],
                                h.bf(0, 128, CH * k + (pt0 - t0) + 128 * tt4,
                                     CH * k + (pt0 - t0) + 128 * (tt4 + 1)),
                                wap, start=(k == 0), stop=last_k),
                                   reads=[h.buf, wbb],
                                   writes=([c.BP[bank]] if last_k else []),
                                   war=([c.BP[bank]] if k == 0 else ()),
                                   sem=("" if (last_k or k_last) else None),
                                   first=(k == 0), last=last_k)
                if bias:
                    for tt4 in range(ntt):
                        for hf in range(2):
                            bank = tt4 * 2 + hf
                            p.emit("tensor", lambda t, hf=hf, bank=bank: t.matmul(
                                c.P[bank][:, 0:512],
                                c.ones.bf(0, 1, 0, 128),
                                b3row.bf(0, 1, 512 * hf, 512 * (hf + 1)),
                                start=False, stop=True),
                                   reads=[c.ones.buf, b3row.buf],
                                   writes=[c.BP[bank]], first=False, last=True)
                for tt4 in range(ntt):
                    tglob = pt0 // 128 + tt4
                    cwap = c.cw.f32(0, 128, NE * tglob + e, NE * tglob + e + 1)
                    for hf in range(2):
                        bank = tt4 * 2 + hf
                        p.emit("vector", lambda v, tglob=tglob, hf=hf, bank=bank,
                               cwap=cwap: v.scalar_tensor_tensor(
                            out=c.xres(tglob)[:, 512 * hf:512 * (hf + 1)],
                            in0=c.P[bank][:, 0:512], scalar=cwap,
                            in1=c.xres(tglob)[:, 512 * hf:512 * (hf + 1)],
                            op0=ALU.mult, op1=ALU.add),
                               reads=[c.BP[bank], c.cw.buf, c.Bx[tglob]],
                               writes=[c.Bx[tglob]])
    for r in [h, hscr] + wbuf + w3buf:
        c.ar.freer(r)
    if wd_res is not None:
        c.ar.freer(wd_res)
    if bias:
        c.ar.freer(l0c)
        c.ar.freer(b3row)


def gate_phase(c, xnT):
    p, ar, T, TT = c.p, c.ar, c.T, c.TT
    c.Bgh = [Buf(f"gh{t}") for t in range(TT)]
    wgt = stream_wslabs(c, c.Wgate, DT, 1024, "wgate")
    bg = ar.alloc(2 * 1024, "bgate")
    p.dma("sync", bg.f32(0, 128, 0, 1024), c.consts[:, 2568:3592], writes=[bg.buf])
    scr = ar.alloc(2 * 1032, "gatescr")
    gt = ar.alloc(2 * 1024, "gtile")
    gtb = [Buf("gt0"), Buf("gt1")]
    for t in range(TT):
        b0 = (t % 2) * 2
        mm_group(c, [b0, b0 + 1],
                 [(lambda tt, t=t, d=d, h=h, b0=b0: tt.matmul(
                     c.P[b0 + h][:, 0:512],
                     xnT.bf(0, 128, T * d + 128 * t, T * d + 128 * (t + 1)),
                     wgt.bf(0, 128, 1024 * d + 512 * h, 1024 * d + 512 * (h + 1)),
                     start=(d == 0), stop=(d == DT - 1)), [xnT.tbufs[t], wgt.sbufs[d]])
                  for d in range(DT) for h in range(2)])
        g0 = scr.f32(0, 128, 0, 1024)
        for h in range(2):
            p.emit("vector", lambda v, h=h, b0=b0, g0=g0: v.tensor_add(
                g0[:, 512 * h:512 * (h + 1)], c.P[b0 + h][:, 0:512],
                bg.f32(0, 128, 512 * h, 512 * (h + 1))),
                   reads=[c.BP[b0 + h], bg.buf], writes=[scr.buf])
        p.emit("scalar", lambda s, g0=g0: s.activation(g0, g0, AF.Sigmoid,
                                                       bias=0.0, scale=1.0),
               reads=[scr.buf], writes=[scr.buf])
        rs = scr.f32(0, 128, 1024, 1025)
        p.emit("vector", lambda v, rs=rs, g0=g0: v.tensor_reduce(
            rs, g0, axis=AX.X, op=ALU.add), reads=[scr.buf], writes=[scr.buf])
        p.emit("vector", lambda v, rs=rs: v.tensor_scalar(
            out=rs, in0=rs, scalar1=1.0 / D, scalar2=EPS, op0=ALU.mult, op1=ALU.add),
               reads=[scr.buf], writes=[scr.buf])
        p.emit("vector", lambda v, rs=rs: v.reciprocal(rs, rs),
               reads=[scr.buf], writes=[scr.buf])
        gi = t % 2
        p.emit("vector", lambda v, gi=gi, g0=g0, rs=rs: v.tensor_scalar(
            out=gt.bf(0, 128, 1024 * gi, 1024 * (gi + 1)), in0=g0, scalar1=rs,
            scalar2=None, op0=ALU.mult),
               reads=[scr.buf], writes=[gtb[gi]])
        p.dma("sync", c.gate_hbm[128 * t:128 * (t + 1), :],
              gt.bf(0, 128, 1024 * gi, 1024 * (gi + 1)), reads=[gtb[gi]],
              writes=[c.Bgh[t]])
    fold_bufs(gt, gtb)
    free_wslabs(c, wgt)
    for r in (bg, scr, gt):
        c.ar.freer(r)


def scan_phase(c):
    p, ar, T, TT = c.p, c.ar, c.T, c.TT
    wqkv = stream_wslabs(c, c.Wqkv, DT, 3072, "wqkv")
    wproj = stream_wslabs(c, c.Wproj, DT, 1024, "wproj")
    brow = ar.alloc(4096, "brow")
    p.dma("gpsimd", brow.bf(0, 1, 0, 4096), c.crow[0:1, 8192:12288], writes=[brow.buf])
    kvacc = ar.alloc(2 * 1024, "kvacc")
    ksacc = ar.alloc(2 * 16, "ksacc")
    kvblk = ar.alloc(1024, "kvblk")
    ksblk = ar.alloc(64, "ksblk")
    for reg, n, f in ((kvacc, 1024, 1), (ksacc, 16, 1)):
        p.emit("gpsimd", lambda g, reg=reg, n=n: g.memset(reg.f32(0, 64, 0, n), 0.0),
               writes=[reg.buf])
    p.emit("gpsimd", lambda g: g.memset(kvblk.bf(0, 128, 0, 1024), 0.0),
           writes=[kvblk.buf])
    p.emit("gpsimd", lambda g: g.memset(ksblk.bf(0, 128, 0, 64), 0.0),
           writes=[ksblk.buf])
    scr = ar.alloc(2 * 2048, "sscr")
    xnc = ar.alloc(1024, "xnc")
    xncT = ar.alloc(2048, "xncT")
    q2 = ar.alloc(1024, "q2tok")
    k2 = ar.alloc(1024, "k2tok")
    vv = ar.alloc(1024, "vchunk")
    q2T = ar.alloc(1024, "q2T")
    k2T = ar.alloc(1024, "k2T")
    at = ar.alloc(16 * 128, "at")
    outc = ar.alloc(1024, "outc")
    outT = ar.alloc(1024, "outT")
    dsc = ar.alloc(64, "densc")
    gtc = ar.alloc(2 * 1024, "gchunk")
    gtcb = [Buf("gc0"), Buf("gc1")]
    xnc2 = ar.alloc(2 * 1024, "xnchunk")
    xncb = [Buf("xc0"), Buf("xc1")]
    nch_lim = int(os.environ.get("KN_CH", "1000000"))
    xncTb = [Buf("xncT0"), Buf("xncT1")]

    def prefetch_chunk(ch):
        gi = ch % 2
        p.dma("sync", gtc.bf(0, 128, 1024 * gi, 1024 * (gi + 1)),
              c.gate_hbm[128 * ch:128 * (ch + 1), :], reads=[c.Bgh[ch]],
              writes=[gtcb[gi]], war=[gtcb[gi]])
        p.dma("sync", xnc2.bf(0, 128, 1024 * gi, 1024 * (gi + 1)),
              c.xnorm_hbm[128 * ch:128 * (ch + 1), :], reads=[c.Bxn[ch]],
              writes=[xncb[gi]], war=[xncb[gi]])
        trans_batch(c, [(xnc2.bf(0, 128, 1024 * gi + 128 * d, 1024 * gi + 128 * (d + 1)),
                         xncb[gi]) for d in range(DT)], 7,
                    xncT.bf(0, 128, 1024 * gi, 1024 * (gi + 1)), xncTb[gi])

    if min(TT, nch_lim) > 0:
        prefetch_chunk(0)
    for ch in range(min(TT, nch_lim)):
        gi = ch % 2
        gap = gtc.bf(0, 128, 1024 * gi, 1024 * (gi + 1))
        gbuf = gtcb[gi]
        for half in range(2):
            mms = []
            for d in range(DT):
                for s3 in range(3):
                    col = 1536 * half + 512 * s3
                    mms.append((lambda t, d=d, col=col, s3=s3, gi=gi: t.matmul(
                        c.P[s3][:, 0:512],
                        xncT.bf(0, 128, 1024 * gi + 128 * d, 1024 * gi + 128 * (d + 1)),
                        wqkv.bf(0, 128, 3072 * d + col, 3072 * d + col + 512),
                        start=(d == 0), stop=False), [xncTb[gi], wqkv.sbufs[d]]))
            tail = [(lambda t, half=half, s3=s3: t.matmul(
                c.P[s3][:, 0:512], c.ones.bf(0, 1, 0, 128),
                brow.bf(0, 1, 1536 * half + 512 * s3, 1536 * half + 512 * (s3 + 1)),
                start=False, stop=True), [c.ones.buf, brow.buf]) for s3 in range(3)]
            mm_group(c, [0, 1, 2], mms, tail=tail)
            if half == 0:
                qf = scr.f32(0, 128, 0, 1024)
                for s3 in range(2):
                    p.emit("vector", lambda v, s3=s3, qf=qf, gap=gap: v.tensor_mul(
                        qf[:, 512 * s3:512 * (s3 + 1)], c.P[s3][:, 0:512],
                        gap[:, 512 * s3:512 * (s3 + 1)]),
                           reads=[c.BP[s3], gbuf], writes=[scr.buf])
                p.emit("vector", lambda v, qf=qf: v.scalar_tensor_tensor(
                    out=qf, in0=qf, scalar=0.0, in1=qf, op0=ALU.max, op1=ALU.mult),
                       reads=[scr.buf], writes=[scr.buf])
                p.emit("vector", lambda v, qf=qf: v.tensor_scalar(
                    out=q2.bf(0, 128, 0, 1024), in0=qf, scalar1=EPS, scalar2=None,
                    op0=ALU.add), reads=[scr.buf], writes=[q2.buf])
                kf = scr.f32(0, 128, 1024, 1536)
                p.emit("vector", lambda v, kf=kf, gap=gap: v.tensor_mul(
                    kf, c.P[2][:, 0:512], gap[:, 0:512]),
                       reads=[c.BP[2], gbuf], writes=[scr.buf])
            else:
                kf2 = scr.f32(0, 128, 1536, 2048)
                p.emit("vector", lambda v, kf2=kf2, gap=gap: v.tensor_mul(
                    kf2, c.P[0][:, 0:512], gap[:, 512:1024]),
                       reads=[c.BP[0], gbuf], writes=[scr.buf])
                kfull = scr.f32(0, 128, 1024, 2048)
                p.emit("vector", lambda v, kfull=kfull: v.scalar_tensor_tensor(
                    out=kfull, in0=kfull, scalar=0.0, in1=kfull,
                    op0=ALU.max, op1=ALU.mult),
                       reads=[scr.buf], writes=[scr.buf])
                p.emit("vector", lambda v, kfull=kfull: v.tensor_scalar(
                    out=k2.bf(0, 128, 0, 1024), in0=kfull, scalar1=EPS, scalar2=None,
                    op0=ALU.add), reads=[scr.buf], writes=[k2.buf])
                for s3 in range(2):
                    p.emit("scalar", lambda s, s3=s3: s.activation(
                        vv.bf(0, 128, 512 * s3, 512 * (s3 + 1)),
                        c.P[1 + s3][:, 0:512], AF.Copy, bias=0.0, scale=1.0),
                           reads=[c.BP[1 + s3]], writes=[vv.buf])
        if ch + 1 < min(TT, nch_lim):
            prefetch_chunk(ch + 1)
        trans_batch(c, [(q2.bf(0, 128, 128 * d, 128 * (d + 1)), q2.buf)
                        for d in range(DT)], 7, q2T.bf(0, 128, 0, 1024), q2T.buf)
        trans_batch(c, [(k2.bf(0, 128, 128 * d, 128 * (d + 1)), k2.buf)
                        for d in range(DT)], 7, k2T.bf(0, 128, 0, 1024), k2T.buf)
        for hp in range(NH // 2):
            for odd in range(2):
                hh = 2 * hp + odd
                p.emit("vector", lambda v, hp=hp, odd=odd, hh=hh: v.tensor_copy(
                    kvblk.bf(64 * odd, 64 * odd + 64, 128 * hp + 64 * odd,
                             128 * hp + 64 * odd + 64),
                    kvacc.f32(0, 64, 64 * hh, 64 * (hh + 1))),
                       reads=[kvacc.buf], writes=[kvblk.buf])
                p.emit("vector", lambda v, hh=hh, odd=odd: v.tensor_copy(
                    ksblk.bf(64 * odd, 64 * odd + 64, hh, hh + 1),
                    ksacc.f32(0, 64, hh, hh + 1)),
                       reads=[ksacc.buf], writes=[ksblk.buf])
        for rnd in range(2):
            for i in range(8):
                hh = 8 * rnd + i
                base = 64 * (hh % 2)
                p.emit("tensor", lambda t, hh=hh, i=i, base=base: t.matmul(
                    c.P[i][:, 0:128],
                    k2T.bf(base, base + 64, 128 * (hh // 2), 128 * (hh // 2) + 128),
                    q2T.bf(base, base + 64, 128 * (hh // 2), 128 * (hh // 2) + 128),
                    start=True, stop=True),
                       reads=[k2T.buf, q2T.buf],
                       writes=[c.BP[i]], war=[c.BP[i]],
                       first=True, last=True)
            for i in range(8):
                hh = 8 * rnd + i
                p.emit("vector", lambda v, hh=hh, i=i: v.tensor_mul(
                    at.bf(0, 128, 128 * hh, 128 * (hh + 1)),
                    c.P[i][:, 0:128],
                    c.maskA.bf(0, 128, 0, 128)),
                       reads=[c.BP[i], c.maskA.buf], writes=[at.buf])
        for half in range(2):
            bank = 2 + half
            mms = [(lambda t, hh=8 * half + i, i=i, bank=bank: t.matmul(
                c.P[bank][:, 64 * i:64 * (i + 1)],
                at.bf(0, 128, 128 * hh, 128 * (hh + 1)),
                vv.bf(0, 128, 64 * hh, 64 * (hh + 1)),
                start=(i == 0), stop=False), [at.buf, vv.buf]) for i in range(8)]
            tail = [(lambda t, hp=4 * half + j, j=j, bank=bank: t.matmul(
                c.P[bank][:, 128 * j:128 * (j + 1)],
                q2T.bf(0, 128, 128 * hp, 128 * (hp + 1)),
                kvblk.bf(0, 128, 128 * hp, 128 * (hp + 1)),
                start=False, stop=(j == 3)), [q2T.buf, kvblk.buf]) for j in range(4)]
            mm_group(c, [bank], mms, tail=tail)
        mms = [(lambda t, hh=hh: t.matmul(
            c.P[4][:, hh:hh + 1],
            at.bf(0, 128, 128 * hh, 128 * (hh + 1)),
            c.ones.bf(0, 128, 0, 1),
            start=(hh == 0), stop=False), [at.buf, c.ones.buf]) for hh in range(NH)]
        tail = [(lambda t, hp=hp: t.matmul(
            c.P[4][:, 2 * hp:2 * hp + 2],
            q2T.bf(0, 128, 128 * hp, 128 * (hp + 1)),
            ksblk.bf(0, 128, 2 * hp, 2 * hp + 2),
            start=False, stop=(hp == NH // 2 - 1)), [q2T.buf, ksblk.buf])
            for hp in range(NH // 2)]
        mm_group(c, [4], mms, tail=tail)
        dn = dsc.f32(0, 128, 0, 16)
        p.emit("vector", lambda v, dn=dn: v.tensor_scalar(
            out=dn, in0=c.P[4][:, 0:16], scalar1=EPS, scalar2=None, op0=ALU.add),
               reads=[c.BP[4]], writes=[dsc.buf])
        p.emit("vector", lambda v, dn=dn: v.reciprocal(dn, dn),
               reads=[dsc.buf], writes=[dsc.buf])
        for hh in range(NH):
            bank = 2 + hh // 8
            p.emit("vector", lambda v, hh=hh, bank=bank, dn=dn: v.tensor_scalar(
                out=outc.bf(0, 128, 64 * hh, 64 * (hh + 1)),
                in0=c.P[bank][:, 64 * (hh % 8):64 * (hh % 8 + 1)],
                scalar1=dn[:, hh:hh + 1], scalar2=None, op0=ALU.mult),
                   reads=[c.BP[bank], dsc.buf], writes=[outc.buf])
        if os.environ.get("KDBG"):
            for nm, reg, ncol in ((0, q2, 1024), (1, k2, 1024), (2, vv, 1024),
                                  (3, outc, 1024), (4, xnc, 1024), (5, q2T, 1024),
                                  (6, kvblk, 1024)):
                p.dma("sync", c.dbg_bf[ch, nm, :, 0:ncol], reg.bf(0, 128, 0, ncol),
                      reads=[reg.buf])
            p.dma("sync", c.dbg_bf[ch, 7, :, 0:2048], at.bf(0, 128, 0, 2048),
                  reads=[at.buf])
            p.dma("sync", c.dbg_f[ch, :, 0:16], dsc.f32(0, 128, 0, 16),
                  reads=[dsc.buf])
        for half in range(2):
            bank = 5 + half
            mm_group(c, [bank],
                     [(lambda t, hh=8 * half + i, i=i, bank=bank: t.matmul(
                         c.P[bank][0:64, 64 * i:64 * (i + 1)],
                         k2.bf(0, 128, 64 * hh, 64 * (hh + 1)),
                         vv.bf(0, 128, 64 * hh, 64 * (hh + 1)),
                         start=(i == 0), stop=(i == 7)), [k2.buf, vv.buf])
                      for i in range(8)])
        mm_group(c, [4],
                 [(lambda t, hh=hh: t.matmul(
                     c.P[4][0:64, 32 + hh:33 + hh],
                     k2.bf(0, 128, 64 * hh, 64 * (hh + 1)),
                     c.ones.bf(0, 128, 0, 1),
                     start=(hh == 0), stop=(hh == NH - 1)), [k2.buf, c.ones.buf])
                  for hh in range(NH)])
        for half in range(2):
            bank = 5 + half
            p.emit("vector", lambda v, half=half, bank=bank: v.tensor_add(
                kvacc.f32(0, 64, 512 * half, 512 * (half + 1)),
                kvacc.f32(0, 64, 512 * half, 512 * (half + 1)),
                c.P[bank][0:64, 0:512]),
                   reads=[c.BP[bank], kvacc.buf], writes=[kvacc.buf])
        p.emit("vector", lambda v: v.tensor_add(
            ksacc.f32(0, 64, 0, 16), ksacc.f32(0, 64, 0, 16), c.P[4][0:64, 32:48]),
               reads=[c.BP[4], ksacc.buf], writes=[ksacc.buf])
        trans_batch(c, [(outc.bf(0, 128, 128 * d, 128 * (d + 1)), outc.buf)
                        for d in range(DT)], 7, outT.bf(0, 128, 0, 1024), outT.buf)
        mms = []
        for d in range(DT):
            for h in range(2):
                mms.append((lambda t, d=d, h=h: t.matmul(
                    c.P[h][:, 0:512],
                    outT.bf(0, 128, 128 * d, 128 * (d + 1)),
                    wproj.bf(0, 128, 1024 * d + 512 * h, 1024 * d + 512 * (h + 1)),
                    start=(d == 0), stop=False), [outT.buf, wproj.sbufs[d]]))
        tail = [(lambda t, h=h: t.matmul(
            c.P[h][:, 0:512], c.ones.bf(0, 1, 0, 128),
            brow.bf(0, 1, 3072 + 512 * h, 3072 + 512 * (h + 1)),
            start=False, stop=True), [c.ones.buf, brow.buf]) for h in range(2)]
        mm_group(c, [0, 1], mms, tail=tail)
        for h in range(2):
            p.emit("vector", lambda v, ch=ch, h=h: v.tensor_add(
                c.xres(ch)[:, 512 * h:512 * (h + 1)],
                c.xres(ch)[:, 512 * h:512 * (h + 1)], c.P[h][:, 0:512]),
                   reads=[c.BP[h], c.Bx[ch]], writes=[c.Bx[ch]])
    fold_bufs(gtc, gtcb)
    fold_bufs(xnc2, xncb)
    fold_bufs(xncT, xncTb)
    free_wslabs(c, wqkv)
    free_wslabs(c, wproj)
    for r in (brow, kvacc, ksacc, kvblk, ksblk, scr, xnc, xncT,
              q2, k2, vv, q2T, k2T, at, outc, outT, dsc, gtc, xnc2):
        c.ar.freer(r)


def rebuild_xnT(c):
    p, ar, T, TT = c.p, c.ar, c.T, c.TT
    xnT = ar.alloc(DT * T, "xnT1r")
    scr = ar.alloc(1024, "rbscr")
    st = c.stats
    for t in range(TT):
        m = st.f32(0, 128, t, t + 1)
        r = st.f32(0, 128, TT + t, TT + t + 1)
        xn = scr.bf(0, 128, 0, 1024)
        p.emit("vector", lambda v, t=t, xn=xn, m=m, r=r: v.tensor_scalar(
            out=xn, in0=c.xres(t), scalar1=m, scalar2=r,
            op0=ALU.subtract, op1=ALU.mult),
               reads=[c.Bx[t], st.buf], writes=[scr.buf])
        trans_batch(c, [(xn[:, 128 * d:128 * (d + 1)], scr.buf) for d in range(DT)],
                    7, xnT_dst(c, xnT, t), xnT.buf)
    c.ar.freer(scr)
    return xnT


def swiglu(c, xnT):
    p, ar, T, TT = c.p, c.ar, c.T, c.TT
    njh = F2 // 2 // 128
    hreg = ar.alloc(njh * T, "swh")
    wbuf = [ar.alloc(1024, f"sww{i}") for i in range(8)]
    w3buf = [ar.alloc(1024, f"sww3{i}") for i in range(4)]
    hscr = ar.alloc(2 * 512, "swhscr")
    nseg = max(T // 512, 1)
    seglen = min(512, T)
    for fh in range(2):
        j0 = fh * njh
        for j in range(njh):
            wb1 = wbuf[(j % 4) * 2]
            wb2 = wbuf[(j % 4) * 2 + 1]
            src1 = bass.AP(c.sw_gate.tensor, c.sw_gate.offset + (j0 + j) * 128 * D,
                           [[D, 128], [1, D]])
            src2 = bass.AP(c.sw_up.tensor, c.sw_up.offset + (j0 + j) * 128 * D,
                           [[D, 128], [1, D]])
            p.dma("gpsimd", wb1.bf(0, 128, 0, 1024), src1,
                  writes=[wb1.buf], war=[wb1.buf])
            p.dma("gpsimd", wb2.bf(0, 128, 0, 1024), src2,
                  writes=[wb2.buf], war=[wb2.buf])
            bk = 0 if (j % 2 == 0) else 4
            for seg in range(nseg):
                g0, g1 = seglen * seg, seglen * (seg + 1)
                b1k = bk + (seg % 2)
                b2k = bk + 2 + (seg % 2)
                for bank, wb in ((b1k, wb1), (b2k, wb2)):
                    mm_group(c, [bank],
                             [(lambda t, d=d, g0=g0, g1=g1, wb=wb, bank=bank: t.matmul(
                                 c.P[bank][:, 0:g1 - g0],
                                 wb.bf(0, 128, 128 * d, 128 * (d + 1)),
                                 xnT.bf(0, 128, T * d + g0, T * d + g1),
                                 start=(d == 0), stop=(d == DT - 1)),
                               [wb.buf] + xn_tiles(xnT, g0, g1)) for d in range(DT)])
                p.emit("scalar", lambda s, g1=g1, g0=g0, b1k=b1k: s.activation(
                    hscr.f32(0, 128, 0, g1 - g0), c.P[b1k][:, 0:g1 - g0],
                    AF.Silu, bias=0.0, scale=1.0),
                       reads=[c.BP[b1k]], writes=[hscr.buf])
                p.emit("vector", lambda v, j=j, g0=g0, g1=g1, b2k=b2k: v.tensor_mul(
                    hreg.bf(0, 128, T * j + g0, T * j + g1),
                    hscr.f32(0, 128, 0, g1 - g0), c.P[b2k][:, 0:g1 - g0]),
                       reads=[c.BP[b2k], hscr.buf], writes=[hreg.buf])
        PTOK = min(512, T)
        npass = T // PTOK
        ntt = PTOK // 128
        for ps in range(npass):
            pt0 = PTOK * ps
            for k in range(njh):
                w3b = w3buf[k % 4]
                cb = c.Bswdbf[j0 + k]
                if ps == 0:
                    p.dma("gpsimd", w3b.bf(0, 128, 0, 1024),
                          bass.AP(c.sw_down.tensor,
                                  c.sw_down.offset + 128 * (j0 + k) * D,
                                  [[D, 128], [1, D]]),
                          writes=[w3b.buf], war=[w3b.buf])
                    p.dma("sync", c.swdbf[j0 + k], w3b.bf(0, 128, 0, 1024),
                          reads=[w3b.buf], writes=[cb])
                else:
                    p.dma("scalar", w3b.bf(0, 128, 0, 1024), c.swdbf[j0 + k],
                          reads=[cb], writes=[w3b.buf], war=[w3b.buf])
                for tt4 in range(ntt):
                    for hf in range(2):
                        bank = tt4 * 2 + hf
                        lk = (k == njh - 1)
                        k_last = (tt4 == ntt - 1 and hf == 1)
                        p.emit("tensor", lambda t, k=k, tt4=tt4, hf=hf, w3b=w3b,
                               bank=bank, pt0=pt0: t.matmul(
                            c.P[bank][:, 0:512],
                            hreg.bf(0, 128, T * k + pt0 + 128 * tt4,
                                    T * k + pt0 + 128 * (tt4 + 1)),
                            w3b.bf(0, 128, 512 * hf, 512 * (hf + 1)),
                            start=(k == 0), stop=(k == njh - 1)),
                               reads=[hreg.buf, w3b.buf],
                               writes=([c.BP[bank]] if lk else []),
                               war=([c.BP[bank]] if k == 0 else ()),
                               sem=("" if (lk or k_last) else None),
                               first=(k == 0), last=lk)
            for tt4 in range(ntt):
                tglob = pt0 // 128 + tt4
                for hf in range(2):
                    bank = tt4 * 2 + hf
                    p.emit("vector", lambda v, tglob=tglob, hf=hf, bank=bank:
                           v.tensor_add(
                        c.xres(tglob)[:, 512 * hf:512 * (hf + 1)],
                        c.xres(tglob)[:, 512 * hf:512 * (hf + 1)],
                        c.P[bank][:, 0:512]),
                           reads=[c.BP[bank], c.Bx[tglob]], writes=[c.Bx[tglob]])
    for r in [hreg, hscr] + wbuf + w3buf:
        c.ar.freer(r)


def _consts_np(inputs, parts):
    l1 = any(s.startswith("L1") for s in parts)
    cst = np.zeros((128, 3592), np.float32)
    crow = np.zeros((1, 12288), np.float32)
    if "L0" in parts:
        cst[:, 0:8] = inputs["L0_ffn_br"][None, :]
        cst[:, 8:264] = np.asarray(inputs["L0_ffn_b1"]).reshape(NE, 32, 128).transpose(2, 0, 1).reshape(128, 256)
        cst[:, 264:520] = np.asarray(inputs["L0_ffn_b2"]).reshape(NE, 32, 128).transpose(2, 0, 1).reshape(128, 256)
        crow[0, 0:8192] = np.asarray(inputs["L0_ffn_b3"]).reshape(-1)
    if l1:
        cst[:, 520:1544] = inputs["L1_attn_ln_g"][None, :]
        cst[:, 1544:2568] = inputs["L1_attn_ln_b"][None, :]
        cst[:, 2568:3592] = inputs["L1_attn_b_gate"][None, :]
        crow[0, 8192:11264] = np.asarray(inputs["L1_attn_b_qkv"])
        crow[0, 11264:12288] = np.asarray(inputs["L1_attn_b_proj"])
    return cst, crow


def _in_map(inputs, xcore, parts):
    cst, crow = _consts_np(inputs, parts)
    m = {"x": xcore, "consts": cst, "crow": crow}
    def _packj(w, nj):
        w = np.asarray(w)
        E = w.shape[0]
        return w.reshape(E, DT, 128, nj, 128).transpose(0, 3, 2, 1, 4).reshape(E, nj, 128, DT * 128)

    if "L0" in parts:
        m["Wq"] = inputs["L0_attn_Wq"]
        m["Wk"] = inputs["L0_attn_Wk"]
        m["Wv"] = inputs["L0_attn_Wv"]
        m["Wr"] = inputs["L0_ffn_Wr"]
        m["W1"] = _packj(inputs["L0_ffn_W1"], DFF0 // 128)
        m["W2"] = _packj(inputs["L0_ffn_W2"], DFF0 // 128)
        m["W3"] = inputs["L0_ffn_W3"]
    if any(s.startswith("L1") for s in parts):
        m["Wqkv"] = inputs["L1_attn_W_qkv"]
        m["Wgate"] = inputs["L1_attn_W_gate"]
        m["Wproj"] = inputs["L1_attn_W_proj"]
        m["gwT"] = np.asarray(inputs["L1_ffn_gate_w"]).T
        m["w_gate"] = _packj(inputs["L1_ffn_w_gate"], F1 // 128)
        m["w_up"] = _packj(inputs["L1_ffn_w_up"], F1 // 128)
        m["w_down"] = inputs["L1_ffn_w_down"]
        m["sw_gate"] = _packj(np.asarray(inputs["L1_ffn_sw_gate"])[None], F2 // 128)[0]
        m["sw_up"] = _packj(np.asarray(inputs["L1_ffn_sw_up"])[None], F2 // 128)[0]
        m["sw_down"] = inputs["L1_ffn_sw_down"]
    return {k: np.ascontiguousarray(np.asarray(v, dtype=np.float32)) for k, v in m.items()}


def _tune_compiler():
    pass


def kernel(**inputs):
    _tune_compiler()
    x = np.asarray(inputs["L0_attn_x"], dtype=np.float32)
    B, T, _ = x.shape
    nc = build_core_program(T, ("L0", "L1"))
    in_maps = [_in_map(inputs, x[i], ("L0", "L1")) for i in range(B)]
    res = run_bass_kernel_spmd(nc, in_maps, core_ids=list(range(B)))
    out = np.stack([np.asarray(r["OUT"]) for r in res.results], axis=0)
    return out.astype(np.float32)


# version 11
# speedup vs baseline: 1.1191x; 1.1191x over previous
import os
from contextlib import ExitStack

import numpy as np

import concourse.bass as bass
import concourse.mybir as mybir
from concourse.bass_utils import run_bass_kernel_spmd

F32 = mybir.dt.float32
BF16 = mybir.dt.bfloat16
AF = mybir.ActivationFunctionType
ALU = mybir.AluOpType
AX = mybir.AxisListType

ENGINES = ("sync", "tensor", "vector", "scalar", "gpsimd")
NEG = -1.0e30
EPS = 1.0e-5
D = 1024
DT = 8
NE = 8
DFF0 = 4096
F1 = 1408
F2 = 2816
NH = 16
HD = 64
ACOLS = 72704


class Buf:
    __slots__ = ("name", "w", "readers")

    def __init__(self, name, w=None, readers=None):
        self.name = name
        self.w = w
        self.readers = list(readers or [])


class Prog:
    def __init__(self, nc, es):
        self.nc = nc
        self.es = es
        self.recs = {e: [] for e in ENGINES}
        self.sems = {}
        self.cnt = {}
        self.pending = {e: [] for e in ENGINES}
        self.eng_sem = {e: "S" + e for e in ENGINES}
        for e in ENGINES:
            self._sem(self.eng_sem[e])
        self.DMA_R = 8
        self.dma_rot = {}
        for q in ("dmaq", "dmag", "dmas"):
            self.dma_rot[q] = 0
            for i in range(self.DMA_R):
                self._sem(f"{q}{i}")

    def _sem(self, name):
        if name not in self.sems:
            self.sems[name] = self.es.enter_context(self.nc.semaphore(name))
            self.cnt[name] = 0
        return self.sems[name]

    def emit(self, eng, fn, reads=(), writes=(), war=(), sem="", amt=1,
             last=True, first=True, extra_waits=()):
        waits = set(extra_waits)
        if first:
            for b in reads:
                if b.w:
                    waits.add(b.w)
            for b in list(writes) + list(war):
                if b.w:
                    waits.add(b.w)
                for r in b.readers:
                    waits.add(r)
        else:
            for b in reads:
                if b.w:
                    waits.add(b.w)
        s = self.eng_sem[eng] if sem == "" else sem
        inc = None
        if s is not None:
            self.cnt[s] += amt
            inc = (s, amt)
            tok = (s, self.cnt[s])
            for b in writes:
                if last:
                    b.w = tok
                    b.readers = []
            for b in reads:
                b.readers.append(tok)
            if sem == "":
                for kind, b in self.pending[eng]:
                    if kind == "r":
                        b.readers.append(tok)
                    else:
                        b.w = tok
                        b.readers = []
                self.pending[eng] = []
        else:
            for b in reads:
                self.pending[eng].append(("r", b))
            for b in writes:
                self.pending[eng].append(("w", b))
        self.recs[eng].append((sorted(waits), fn, inc))

    def dma(self, eng, dst, src, reads=(), writes=(), war=()):
        q = {"sync": "dmaq", "gpsimd": "dmag", "scalar": "dmas"}[eng]
        i = self.dma_rot[q]
        self.dma_rot[q] = (i + 1) % self.DMA_R
        s = f"{q}{i}"
        waits = []
        if self.cnt[s] > 0:
            waits.append((s, self.cnt[s]))
        self.emit(eng, lambda e: e.dma_start(dst, src), reads=reads,
                  writes=writes, war=war, sem=s, amt=16, extra_waits=waits)

    def build(self, block):
        known = {e: {} for e in ENGINES}

        def run(e_obj, eng):
            k = known[eng]
            for waits, fn, inc in self.recs[eng]:
                for (s, cnt) in waits:
                    if k.get(s, -1) < cnt:
                        e_obj.wait_ge(self.sems[s], cnt)
                        k[s] = cnt
                ins = fn(e_obj)
                if inc is not None:
                    s, amt = inc
                    ins.then_inc(self.sems[s], amt)

        for eng in ENGINES:
            if self.recs[eng]:
                getattr(block, eng)(lambda e_obj, eng=eng: run(e_obj, eng))

    def barrier_rec(self, eng, sems):
        allsems = []
        for s in sems:
            if s in self.dma_rot:
                allsems += [f"{s}{i}" for i in range(self.DMA_R)]
            else:
                allsems.append(s)
        toks = [(s, self.cnt[s]) for s in allsems if self.cnt[s] > 0]
        self.recs[eng].append((toks, lambda e: None, None))


class Arena:
    def __init__(self, t, ncols):
        self.t = t
        self.free = [(0, ncols, Buf("arena0"))]

    def alloc(self, ncols, name):
        ncols = (ncols + 63) & ~63
        for i, (c0, n, b) in enumerate(self.free):
            if n >= ncols:
                if n == ncols:
                    self.free.pop(i)
                else:
                    self.free[i] = (c0 + ncols, n - ncols,
                                    Buf(b.name, w=b.w, readers=list(b.readers)))
                nb = Buf(name, w=b.w, readers=list(b.readers))
                return Region(self.t, c0, ncols, nb)
        raise MemoryError(f"arena full: {name} needs {ncols}; free={[(a, n) for a, n, _ in self.free]}")

    def freer(self, reg):
        self.free.append((reg.c0, reg.ncols, reg.buf))
        self.free.sort(key=lambda x: x[0])
        merged = []
        for c0, n, b in self.free:
            if merged and merged[-1][0] + merged[-1][1] == c0:
                pc0, pn, pb = merged[-1]
                nb = Buf("m", w=None, readers=[])
                for x in (pb, b):
                    if x.w:
                        nb.readers.append(x.w)
                    nb.readers += x.readers
                merged[-1] = (pc0, pn + n, nb)
            else:
                merged.append((c0, n, b))
        self.free = merged


class Region:
    def __init__(self, t, c0, ncols, buf):
        self.t = t
        self.c0 = c0
        self.ncols = ncols
        self.buf = buf

    def bf(self, p0, p1, a, b):
        return self.t[p0:p1, self.c0 + a:self.c0 + b]

    def f32(self, p0, p1, a, b):
        return self.t[p0:p1, self.c0 + 2 * a:self.c0 + 2 * b].bitcast(F32)


def fold_bufs(reg, bufs):
    for b in bufs:
        if b.w:
            reg.buf.readers.append(b.w)
        reg.buf.readers.extend(b.readers)


class Ctx:
    pass


def build_core_program(T, parts, dbg=None):
    TT = T // 128
    nc = bass.Bass("TRN2", target_bir_lowering=False)
    c = Ctx()
    c.T, c.TT = T, TT

    def dram(name, shape):
        return nc.dram_tensor(name, shape, F32, kind="ExternalInput").ap()

    c.x = dram("x", [T, D])
    c.OUT = nc.dram_tensor("OUT", [T, D], F32, kind="ExternalOutput").ap()
    if "L0" in parts:
        c.Wq, c.Wk, c.Wv = dram("Wq", [D, D]), dram("Wk", [D, D]), dram("Wv", [D, D])
        c.Wr = dram("Wr", [D, NE])
        c.W1 = dram("W1", [NE, DFF0 // 128, 128, D])
        c.W2 = dram("W2", [NE, DFF0 // 128, 128, D])
        c.W3 = dram("W3", [NE, DFF0, D])
        c.w3bf = nc.dram_tensor("w3bf", [NE * (DFF0 // 128), 128, D], BF16).ap()
        c.Bw3bf = [Buf(f"w3bf{i}") for i in range(NE * (DFF0 // 128))]
        c.w1bf = nc.dram_tensor("w1bf", [NE * (DFF0 // 128), 128, D], BF16).ap()
        c.Bw1bf = [Buf(f"w1bf{i}") for i in range(NE * (DFF0 // 128))]
        c.w2bf = nc.dram_tensor("w2bf", [NE * (DFF0 // 128), 128, D], BF16).ap()
        c.Bw2bf = [Buf(f"w2bf{i}") for i in range(NE * (DFF0 // 128))]
    if any(s.startswith("L1") for s in parts):
        c.Wqkv = dram("Wqkv", [D, 3 * D])
        c.Wgate, c.Wproj = dram("Wgate", [D, D]), dram("Wproj", [D, D])
        c.gwT = dram("gwT", [D, NE])
        c.w_gate = dram("w_gate", [NE, F1 // 128, 128, D])
        c.w_up = dram("w_up", [NE, F1 // 128, 128, D])
        c.w_down = dram("w_down", [NE, F1, D])
        c.sw_gate = dram("sw_gate", [F2 // 128, 128, D])
        c.sw_up = dram("sw_up", [F2 // 128, 128, D])
        c.sw_down = dram("sw_down", [F2, D])
        c.swdbf = nc.dram_tensor("swdbf", [F2 // 128, 128, D], BF16).ap()
        c.Bswdbf = [Buf(f"swdbf{i}") for i in range(F2 // 128)]
        c.gate_hbm = nc.dram_tensor("gate_hbm", [T, D], BF16).ap()
        c.xnorm_hbm = nc.dram_tensor("xnorm_hbm", [T, D], BF16).ap()
        if os.environ.get("KDBG"):
            c.dbg_bf = nc.dram_tensor("dbg_bf", [TT, 8, 128, 2048], BF16).ap()
            c.dbg_f = nc.dram_tensor("dbg_f", [TT, 128, 64], F32).ap()
    c.consts = nc.dram_tensor("consts", [128, 3592], F32, kind="ExternalInput").ap()
    c.crow = nc.dram_tensor("crow", [1, 12288], F32, kind="ExternalInput").ap()

    es = ExitStack()
    with es:
        arena_t = es.enter_context(nc.sbuf_tensor("arena", [128, ACOLS], BF16))
        xres_t = es.enter_context(nc.sbuf_tensor("xres", [128, TT * 1024], F32))
        P = [es.enter_context(nc.psum_tensor(f"P{i}", [128, 512], F32)) for i in range(8)]
        p = Prog(nc, es)
        block = es.enter_context(nc.Block())

        c.nc, c.p, c.P = nc, p, [b[:] for b in P]
        c.PB = [b.bitcast(BF16) for b in c.P]
        c.BP = [Buf(f"P{i}") for i in range(8)]
        c.ar = Arena(arena_t, ACOLS)
        c.xres_t = xres_t
        c.Bx = [Buf(f"x{t}") for t in range(TT)]
        c.xres = lambda t: xres_t[:, 1024 * t:1024 * (t + 1)]

        c.ident = c.ar.alloc(128, "ident")
        c.ones = c.ar.alloc(512, "ones")
        c.maskS = c.ar.alloc(256, "maskS")
        c.maskA = c.ar.alloc(128, "maskA")
        c.stats = c.ar.alloc(2 * 8 * TT, "stats")
        c.cw = c.ar.alloc(2 * 8 * TT, "cw")

        setup(c)
        load_x(c)
        if "L0" in parts:
            xnT = ln_phase(c, layer=0)
            l0b = c.ar.alloc(2 * 8, "l0br")
            p.dma("sync", l0b.f32(0, 128, 0, 8), c.consts[:, 0:8], writes=[l0b.buf])
            rtile, rfini = router_prep(c, xnT, c.Wr, with_bias=True, sigma_scale=1.0,
                                       brap=l0b.f32(0, 128, 0, 8), brbuf=l0b.buf,
                                       bank_of=lambda t: 7)
            attn0(c, xnT, qi_hook=rtile)
            rfini()
            c.ar.freer(l0b)
            moe(c, xnT, dff=DFF0, wg=c.W1, wu=c.W2, wd=c.W3,
                router_w=None, sigma_scale=1.0, bias=True, name="f0")
            fold_bufs(xnT, xnT.tbufs)
            c.ar.freer(xnT)
        if any(s.startswith("L1") for s in parts):
            sub = [s[2:] for s in parts if s.startswith("L1")][0] or "ams"
            xnT = ln_phase(c, layer=1)
            if "a" in sub:
                gate_phase(c, xnT)
            if "m" in sub:
                moe(c, xnT, dff=F1, wg=c.w_gate, wu=c.w_up, wd=c.w_down,
                    router_w=c.gwT, sigma_scale=2.5, bias=False, name="f1")
            if "s" in sub:
                swiglu(c, xnT)
            c.ar.freer(xnT)
            if "a" in sub and "g" not in sub:
                scan_phase(c)
        store_x(c)
        p.barrier_rec("sync", ["dmaq", "dmag", "dmas", "Svector", "Sscalar",
                               "Stensor", "Sgpsimd"])
        if dbg is not None:
            dbg(c)
        p.build(block)
    return nc


def mm_group(c, bank_ids, mms, tail=None):
    p = c.p
    seq = list(mms) + list(tail or [])
    n = len(seq)
    banks = [c.BP[b] for b in bank_ids]
    for i, (fn, reads) in enumerate(seq):
        p.emit("tensor", fn, reads=reads,
               writes=(banks if i == n - 1 else []),
               war=(banks if i == 0 else ()),
               sem=(None if i < n - 1 else ""),
               first=(i == 0), last=(i == n - 1))


def trans_batch(c, pairs, bank, dst_ap, dst_buf):
    p = c.p
    assert len(pairs) <= 8
    for i, (src, sbuf) in enumerate(pairs):
        dst = c.PB[bank][:, 128 * i:128 * (i + 1)]
        p.emit("tensor",
               lambda t, dst=dst, src=src: t.transpose(dst, src, c.ident.bf(0, 128, 0, 128)),
               reads=[sbuf, c.ident.buf],
               writes=([c.BP[bank]] if i == len(pairs) - 1 else []),
               war=([c.BP[bank]] if i == 0 else ()),
               sem=(None if i < len(pairs) - 1 else ""),
               first=(i == 0), last=(i == len(pairs) - 1))
    n = len(pairs)
    src_ap = c.PB[bank][:, 0:128 * n]
    if dst_ap.ndim == 3:
        src_ap = src_ap.rearrange("p (d c) -> p d c", c=128)
    p.emit("vector", lambda v: v.tensor_copy(dst_ap, src_ap),
           reads=[c.BP[bank]], writes=[dst_buf])


def setup(c):
    p = c.p
    p.emit("gpsimd", lambda g: g.memset(c.ones.bf(0, 128, 0, 512), 1.0),
           writes=[c.ones.buf])
    idb = c.ident
    p.emit("gpsimd", lambda g: g.iota(idb.bf(0, 128, 0, 128), [[1, 128]], base=0,
                                      channel_multiplier=-1,
                                      allow_small_or_imprecise_dtypes=True),
           writes=[idb.buf])
    p.emit("vector", lambda v: v.tensor_scalar(out=idb.bf(0, 128, 0, 128),
                                               in0=idb.bf(0, 128, 0, 128),
                                               scalar1=0.0, scalar2=None,
                                               op0=ALU.is_equal),
           reads=[idb.buf], writes=[idb.buf])
    ms = c.maskS
    p.emit("gpsimd", lambda g: g.iota(ms.f32(0, 128, 0, 128), [[1, 128]], base=0,
                                      channel_multiplier=-1,
                                      allow_small_or_imprecise_dtypes=True),
           writes=[ms.buf])
    p.emit("vector", lambda v: v.tensor_scalar(out=ms.f32(0, 128, 0, 128),
                                               in0=ms.f32(0, 128, 0, 128),
                                               scalar1=0.0, scalar2=NEG,
                                               op0=ALU.is_gt, op1=ALU.mult),
           reads=[ms.buf], writes=[ms.buf])
    ma = c.maskA
    p.emit("gpsimd", lambda g: g.iota(ma.bf(0, 128, 0, 128), [[1, 128]], base=0,
                                      channel_multiplier=-1,
                                      allow_small_or_imprecise_dtypes=True),
           writes=[ma.buf])
    p.emit("vector", lambda v: v.tensor_scalar(out=ma.bf(0, 128, 0, 128),
                                               in0=ma.bf(0, 128, 0, 128),
                                               scalar1=0.0, scalar2=None,
                                               op0=ALU.is_ge),
           reads=[ma.buf], writes=[ma.buf])


def load_x(c):
    for t in range(c.TT):
        c.p.dma("sync", c.xres(t), c.x[128 * t:128 * (t + 1), :], writes=[c.Bx[t]])


def store_x(c):
    for t in range(c.TT):
        c.p.dma("sync", c.OUT[128 * t:128 * (t + 1), :], c.xres(t), reads=[c.Bx[t]])


def stream_wslabs(c, src, n_slabs, cols, name):
    reg = c.ar.alloc(n_slabs * cols, name)
    reg.sbufs = []
    for d in range(n_slabs):
        b = Buf(f"{name}_{d}", w=reg.buf.w, readers=list(reg.buf.readers))
        reg.sbufs.append(b)
        c.p.dma("gpsimd", reg.bf(0, 128, d * cols, (d + 1) * cols),
                src[128 * d:128 * (d + 1), :], writes=[b])
    return reg


def free_wslabs(c, reg):
    fold_bufs(reg, reg.sbufs)
    c.ar.freer(reg)


def xn_tiles(xnT, t0, t1):
    return xnT.tbufs[t0 // 128:(t1 + 127) // 128]


def xnT_dst(c, xnT, t):
    return bass.AP(xnT.t, xnT.c0 + 128 * t,
                   [[xnT.t.shape[1], 128], [c.T, DT], [1, 128]])


def ln_phase(c, layer):
    p, ar, T, TT = c.p, c.ar, c.T, c.TT
    xnT = ar.alloc(DT * T, f"xnT{layer}")
    xnT.tbufs = [Buf(f"xnT{layer}_{t}") for t in range(TT)]
    for b in xnT.tbufs:
        b.w = xnT.buf.w
        b.readers = list(xnT.buf.readers)
    scr = ar.alloc((6 * 1024 if layer == 1 else 3 * 1024 + 128), f"lnscr{layer}")
    Bscr = scr.buf
    st = c.stats
    if layer == 1:
        gb = ar.alloc(2 * 2048, "lngb")
        p.dma("sync", gb.f32(0, 128, 0, 2048), c.consts[:, 520:2568], writes=[gb.buf])
        xnrm = ar.alloc(2 * 1024, "xnrmtile")
        xnrmb = [Buf("xr0"), Buf("xr1")]
        c.Bxn = [Buf(f"xn{t}") for t in range(TT)]
    for t in range(TT):
        bn6 = scr.f32(0, 128, 1536, 1548)
        mv = scr.f32(0, 128, 1548, 1550)
        for hb in range(2):
            p.emit("vector", lambda v, t=t, bn6=bn6, hb=hb: v.bn_stats(
                bn6[:, 6 * hb:6 * (hb + 1)], c.xres(t)[:, 512 * hb:512 * (hb + 1)]),
                   reads=[c.Bx[t]], writes=[Bscr])
        p.emit("vector", lambda v, bn6=bn6, mv=mv: v.bn_aggr(mv, bn6),
               reads=[Bscr], writes=[Bscr])
        m = st.f32(0, 128, t, t + 1)
        r = st.f32(0, 128, TT + t, TT + t + 1)
        sd = st.f32(0, 128, 4 * TT, 4 * TT + 1)
        p.emit("vector", lambda v, m=m, mv=mv: v.tensor_copy(m, mv[:, 0:1]),
               reads=[Bscr], writes=[st.buf])
        p.emit("scalar", lambda s, sd=sd, mv=mv: s.activation(sd, mv[:, 1:2], AF.Sqrt,
                                                              bias=np.float32(EPS), scale=1.0),
               reads=[Bscr], writes=[st.buf])
        p.emit("vector", lambda v, r=r, sd=sd: v.reciprocal(r, sd),
               reads=[st.buf], writes=[st.buf])
        xn = scr.bf(0, 128, 0, 1024)
        if layer == 0:
            p.emit("vector", lambda v, t=t, xn=xn, m=m, r=r: v.tensor_scalar(
                out=xn, in0=c.xres(t), scalar1=m, scalar2=r,
                op0=ALU.subtract, op1=ALU.mult),
                   reads=[c.Bx[t], st.buf], writes=[Bscr])
        else:
            xf = scr.f32(0, 128, 512, 1536)
            p.emit("vector", lambda v, t=t, xf=xf, m=m, r=r: v.tensor_scalar(
                out=xf, in0=c.xres(t), scalar1=m, scalar2=r,
                op0=ALU.subtract, op1=ALU.mult),
                   reads=[c.Bx[t], st.buf], writes=[Bscr])
            for hb in range(2):
                p.emit("vector", lambda v, bn6=bn6, xf=xf, hb=hb: v.bn_stats(
                    bn6[:, 6 * hb:6 * (hb + 1)], xf[:, 512 * hb:512 * (hb + 1)]),
                       reads=[Bscr], writes=[Bscr])
            p.emit("vector", lambda v, bn6=bn6, mv=mv: v.bn_aggr(mv, bn6),
                   reads=[Bscr], writes=[Bscr])
            sd2 = st.f32(0, 128, 4 * TT + 1, 4 * TT + 2)
            r2 = st.f32(0, 128, 4 * TT + 2, 4 * TT + 3)
            p.emit("scalar", lambda s, sd2=sd2, mv=mv: s.activation(
                sd2, mv[:, 1:2], AF.Sqrt, bias=np.float32(EPS), scale=1.0),
                   reads=[Bscr], writes=[st.buf])
            p.emit("vector", lambda v, r2=r2, sd2=sd2: v.reciprocal(r2, sd2),
                   reads=[st.buf], writes=[st.buf])
            xm = scr.f32(0, 128, 1550, 2574)
            p.emit("vector", lambda v, xm=xm, xf=xf, mv=mv, r2=r2: v.tensor_scalar(
                out=xm, in0=xf, scalar1=mv[:, 0:1], scalar2=r2,
                op0=ALU.subtract, op1=ALU.mult),
                   reads=[Bscr, st.buf], writes=[Bscr])
            p.emit("vector", lambda v, xm=xm, gb=gb: v.tensor_mul(
                xm, xm, gb.f32(0, 128, 0, 1024)),
                   reads=[Bscr, gb.buf], writes=[Bscr])
            gi = t % 2
            p.emit("vector", lambda v, xm=xm, gb=gb, gi=gi: v.tensor_add(
                xnrm.bf(0, 128, 1024 * gi, 1024 * (gi + 1)), xm,
                gb.f32(0, 128, 1024, 2048)),
                   reads=[Bscr, gb.buf], writes=[xnrmb[gi]])
            p.dma("sync", c.xnorm_hbm[128 * t:128 * (t + 1), :],
                  xnrm.bf(0, 128, 1024 * gi, 1024 * (gi + 1)),
                  reads=[xnrmb[gi]], writes=[c.Bxn[t]])
            p.emit("vector", lambda v, xn=xn, xf=xf: v.tensor_copy(xn, xf),
                   reads=[Bscr], writes=[Bscr])
        trans_batch(c, [(xn[:, 128 * d:128 * (d + 1)], Bscr) for d in range(DT)],
                    7, xnT_dst(c, xnT, t), xnT.tbufs[t])
    if layer == 1:
        fold_bufs(xnrm, xnrmb)
        c.ar.freer(xnrm)
        c.ar.freer(gb)
    c.ar.freer(scr)
    return xnT


def attn0(c, xnT, qi_hook=None):
    p, ar, T, TT = c.p, c.ar, c.T, c.TT
    kT = ar.alloc(DT * T, "kT")
    vtok = ar.alloc(TT * 1024, "vtok")
    nchunk = (T + 511) // 512
    wk = stream_wslabs(c, c.Wk, DT, 1024, "wk")
    for do in range(DT):
        for ch in range(nchunk):
            t0, t1 = 512 * ch, min(512 * (ch + 1), T)
            bank = (do * nchunk + ch) % 4
            mm_group(c, [bank],
                     [(lambda t, do=do, d=d, t0=t0, t1=t1, bank=bank: t.matmul(
                         c.P[bank][:, 0:t1 - t0],
                         wk.bf(0, 128, 1024 * d + 128 * do, 1024 * d + 128 * (do + 1)),
                         xnT.bf(0, 128, T * d + t0, T * d + t1),
                         start=(d == 0), stop=(d == DT - 1)),
                       [wk.sbufs[d]] + xn_tiles(xnT, t0, t1))
                      for d in range(DT)])
            p.emit("scalar", lambda s, do=do, t0=t0, t1=t1, bank=bank: s.activation(
                kT.bf(0, 128, T * do + t0, T * do + t1), c.P[bank][:, 0:t1 - t0],
                AF.Copy, bias=0.0, scale=1.0),
                   reads=[c.BP[bank]], writes=[kT.buf])
    free_wslabs(c, wk)
    wv = stream_wslabs(c, c.Wv, DT, 1024, "wv")
    for t in range(TT):
        b0 = 4 + (t % 2) * 2
        mm_group(c, [b0, b0 + 1],
                 [(lambda tt, t=t, d=d, h=h, b0=b0: tt.matmul(
                     c.P[b0 + h][:, 0:512],
                     xnT.bf(0, 128, T * d + 128 * t, T * d + 128 * (t + 1)),
                     wv.bf(0, 128, 1024 * d + 512 * h, 1024 * d + 512 * (h + 1)),
                     start=(d == 0), stop=(d == DT - 1)), [xnT.tbufs[t], wv.sbufs[d]])
                  for d in range(DT) for h in range(2)])
        for h in range(2):
            p.emit("scalar", lambda s, t=t, h=h, b0=b0: s.activation(
                vtok.bf(0, 128, 1024 * t + 512 * h, 1024 * t + 512 * (h + 1)),
                c.P[b0 + h][:, 0:512], AF.Copy, bias=0.0, scale=1.0),
                   reads=[c.BP[b0 + h]], writes=[vtok.buf])
    free_wslabs(c, wv)
    wq = stream_wslabs(c, c.Wq, DT, 1024, "wq")
    qT = ar.alloc(DT * 128, "qTqi")
    SstA = [ar.alloc(2 * T, "Sstrip0"), ar.alloc(2 * T, "Sstrip1")]
    Pst = ar.alloc(T, "Pstrip")
    PTst = ar.alloc(T, "PTstrip")
    smA = [ar.alloc(64, "softsmall0"), ar.alloc(64, "softsmall1")]
    for qi in range(TT):
        Sst = SstA[qi % 2]
        sm = smA[qi % 2]
        for do in range(DT):
            bank = do % 2
            mm_group(c, [bank],
                     [(lambda t, do=do, d=d, qi=qi, bank=bank: t.matmul(
                         c.P[bank][:, 0:128],
                         wq.bf(0, 128, 1024 * d + 128 * do, 1024 * d + 128 * (do + 1)),
                         xnT.bf(0, 128, T * d + 128 * qi, T * d + 128 * (qi + 1)),
                         start=(d == 0), stop=(d == DT - 1)), [wq.sbufs[d], xnT.tbufs[qi]])
                      for d in range(DT)])
            p.emit("scalar", lambda s, do=do, bank=bank: s.activation(
                qT.bf(0, 128, 128 * do, 128 * (do + 1)), c.P[bank][:, 0:128],
                AF.Copy, bias=0.0, scale=1.0),
                   reads=[c.BP[bank]], writes=[qT.buf])
        w = 128 * (qi + 1)
        chunks = []
        s0 = 0
        while s0 < 128 * qi:
            s1 = min(s0 + 512, 128 * qi)
            chunks.append((s0, s1, False))
            s0 = s1
        chunks.append((128 * qi, w, True))
        for ci, (s0, s1, diag) in enumerate(chunks):
            bank = 2 + ci % 2
            mm_group(c, [bank],
                     [(lambda t, d=d, s0=s0, s1=s1, bank=bank: t.matmul(
                         c.P[bank][:, 0:s1 - s0],
                         qT.bf(0, 128, 128 * d, 128 * (d + 1)),
                         kT.bf(0, 128, T * d + s0, T * d + s1),
                         start=(d == 0), stop=(d == DT - 1)), [qT.buf, kT.buf])
                      for d in range(DT)])
            if not diag:
                p.emit("scalar", lambda s, s0=s0, s1=s1, bank=bank: s.activation(
                    Sst.f32(0, 128, s0, s1), c.P[bank][:, 0:s1 - s0],
                    AF.Copy, bias=0.0, scale=np.float32(0.03125)),
                       reads=[c.BP[bank]], writes=[Sst.buf])
            else:
                p.emit("vector", lambda v, s0=s0, s1=s1, bank=bank: v.scalar_tensor_tensor(
                    out=Sst.f32(0, 128, s0, s1), in0=c.P[bank][:, 0:s1 - s0],
                    scalar=0.03125, in1=c.maskS.f32(0, 128, 0, s1 - s0),
                    op0=ALU.mult, op1=ALU.add),
                       reads=[c.BP[bank], c.maskS.buf], writes=[Sst.buf])
        negmax = sm.f32(0, 128, 0, 1)
        rowsum = sm.f32(0, 128, 1, 2)
        recip = sm.f32(0, 128, 2, 3)
        p.emit("vector", lambda v, w=w, negmax=negmax: v.tensor_reduce(
            negmax, Sst.f32(0, 128, 0, w), axis=AX.X, op=ALU.max, negate=True),
               reads=[Sst.buf], writes=[sm.buf])
        p.emit("scalar", lambda s, w=w, negmax=negmax, rowsum=rowsum: s.activation(
            Pst.bf(0, 128, 0, w), Sst.f32(0, 128, 0, w), AF.Exp,
            bias=negmax, scale=1.0, accum_out=rowsum),
               reads=[Sst.buf, sm.buf], writes=[Pst.buf, sm.buf])
        p.emit("vector", lambda v, recip=recip, rowsum=rowsum: v.reciprocal(recip, rowsum),
               reads=[sm.buf], writes=[sm.buf])
        for g0 in range(0, qi + 1, 8):
            g1 = min(g0 + 8, qi + 1)
            dst = bass.AP(PTst.t, PTst.c0 + 128 * g0,
                          [[PTst.t.shape[1], 128], [1, 128 * (g1 - g0)]])
            trans_batch(c, [(Pst.bf(0, 128, 128 * s, 128 * (s + 1)), Pst.buf)
                            for s in range(g0, g1)], 6, dst, PTst.buf)
        for h in range(2):
            bank = 4 + h
            mm_group(c, [bank],
                     [(lambda t, s=s, h=h, qi=qi, bank=bank: t.matmul(
                         c.P[bank][:, 0:512],
                         PTst.bf(0, 128, 128 * s, 128 * (s + 1)),
                         vtok.bf(0, 128, 1024 * s + 512 * h, 1024 * s + 512 * (h + 1)),
                         start=(s == 0), stop=(s == qi)), [PTst.buf, vtok.buf])
                      for s in range(qi + 1)])
            p.emit("vector", lambda v, qi=qi, h=h, recip=recip, bank=bank:
                   v.scalar_tensor_tensor(
                out=c.xres(qi)[:, 512 * h:512 * (h + 1)], in0=c.P[bank][:, 0:512],
                scalar=recip, in1=c.xres(qi)[:, 512 * h:512 * (h + 1)],
                op0=ALU.mult, op1=ALU.add),
                   reads=[c.BP[bank], sm.buf, c.Bx[qi]], writes=[c.Bx[qi]])
        if qi_hook is not None:
            qi_hook(qi)
    free_wslabs(c, wq)
    for r in (qT, SstA[0], SstA[1], Pst, PTst, smA[0], smA[1], kT, vtok):
        c.ar.freer(r)


def router_prep(c, xnT, router_w, with_bias, sigma_scale, brap=None, brbuf=None,
                bank_of=None):
    p, ar, T, TT = c.p, c.ar, c.T, c.TT
    rw = stream_wslabs(c, router_w, DT, NE, "rw")
    scr = ar.alloc(128, "rscr")
    B = scr.buf

    def tile(t):
        bank = (t % 4) if bank_of is None else bank_of(t)
        mm_group(c, [bank],
                 [(lambda tt, t=t, d=d, bank=bank: tt.matmul(
                     c.P[bank][:, 0:NE],
                     xnT.bf(0, 128, T * d + 128 * t, T * d + 128 * (t + 1)),
                     rw.bf(0, 128, NE * d, NE * (d + 1)),
                     start=(d == 0), stop=(d == DT - 1)), [xnT.tbufs[t], rw.sbufs[d]])
                  for d in range(DT)])
        lg = scr.f32(0, 128, 0, NE)
        if with_bias:
            p.emit("vector", lambda v, lg=lg, bank=bank: v.tensor_add(
                lg, c.P[bank][:, 0:NE], brap),
                   reads=[c.BP[bank], brbuf], writes=[B])
        else:
            p.emit("vector", lambda v, lg=lg, bank=bank: v.tensor_copy(
                lg, c.P[bank][:, 0:NE]),
                   reads=[c.BP[bank]], writes=[B])
        m1 = scr.f32(0, 128, NE, NE + 1)
        m2 = scr.f32(0, 128, NE + 1, NE + 2)
        w1 = scr.f32(0, 128, NE + 2, NE + 3)
        w2 = scr.f32(0, 128, NE + 3, NE + 4)
        d12 = scr.f32(0, 128, NE + 4, NE + 5)
        eq1 = scr.f32(0, 128, 16, 24)
        eq2 = scr.f32(0, 128, 24, 32)
        l2 = scr.f32(0, 128, 32, 40)
        p.emit("vector", lambda v, m1=m1, lg=lg: v.tensor_reduce(
            m1, lg, axis=AX.X, op=ALU.max), reads=[B], writes=[B])
        p.emit("vector", lambda v, eq1=eq1, lg=lg, m1=m1: v.tensor_scalar(
            out=eq1, in0=lg, scalar1=m1, scalar2=None, op0=ALU.is_equal),
               reads=[B], writes=[B])
        p.emit("vector", lambda v, l2=l2, eq1=eq1, lg=lg: v.scalar_tensor_tensor(
            out=l2, in0=eq1, scalar=NEG, in1=lg, op0=ALU.mult, op1=ALU.add),
               reads=[B], writes=[B])
        p.emit("vector", lambda v, m2=m2, l2=l2: v.tensor_reduce(
            m2, l2, axis=AX.X, op=ALU.max), reads=[B], writes=[B])
        p.emit("vector", lambda v, eq2=eq2, l2=l2, m2=m2: v.tensor_scalar(
            out=eq2, in0=l2, scalar1=m2, scalar2=None, op0=ALU.is_equal),
               reads=[B], writes=[B])
        p.emit("vector", lambda v, d12=d12, m2=m2, m1=m1: v.tensor_scalar(
            out=d12, in0=m2, scalar1=m1, scalar2=None, op0=ALU.subtract),
               reads=[B], writes=[B])
        p.emit("scalar", lambda s, d12=d12: s.activation(d12, d12, AF.Exp,
                                                         bias=0.0, scale=1.0),
               reads=[B], writes=[B])
        p.emit("vector", lambda v, d12=d12: v.tensor_scalar(
            out=d12, in0=d12, scalar1=1.0, scalar2=None, op0=ALU.add),
               reads=[B], writes=[B])
        p.emit("vector", lambda v, w1=w1, d12=d12: v.reciprocal(w1, d12),
               reads=[B], writes=[B])
        p.emit("vector", lambda v, w1=w1: v.tensor_scalar(
            out=w1, in0=w1, scalar1=sigma_scale, scalar2=None, op0=ALU.mult),
               reads=[B], writes=[B])
        p.emit("vector", lambda v, w2=w2, w1=w1: v.tensor_scalar(
            out=w2, in0=w1, scalar1=-1.0, scalar2=sigma_scale,
            op0=ALU.mult, op1=ALU.add), reads=[B], writes=[B])
        cwc = c.cw.f32(0, 128, NE * t, NE * (t + 1))
        p.emit("vector", lambda v, eq2=eq2, w2=w2: v.tensor_scalar(
            out=eq2, in0=eq2, scalar1=w2, scalar2=None, op0=ALU.mult),
               reads=[B], writes=[B])
        p.emit("vector", lambda v, cwc=cwc, eq1=eq1, w1=w1, eq2=eq2:
               v.scalar_tensor_tensor(
            out=cwc, in0=eq1, scalar=w1, in1=eq2, op0=ALU.mult, op1=ALU.add),
               reads=[B], writes=[c.cw.buf])

    def fini():
        free_wslabs(c, rw)
        c.ar.freer(scr)
    return tile, fini


def router_phase(c, xnT, router_w, with_bias, sigma_scale, brap=None, brbuf=None):
    tile, fini = router_prep(c, xnT, router_w, with_bias, sigma_scale, brap, brbuf)
    for t in range(c.TT):
        tile(t)
    fini()


def moe(c, xnT, dff, wg, wu, wd, router_w, sigma_scale, bias, name):
    p, ar, T, TT = c.p, c.ar, c.T, c.TT
    l0c = None
    if bias:
        l0c = ar.alloc(2 * 520, name + "_bias")
        p.dma("sync", l0c.f32(0, 128, 0, 8), c.consts[:, 0:8], writes=[l0c.buf])
        p.dma("sync", l0c.f32(0, 128, 8, 520), c.consts[:, 8:520],
              writes=[l0c.buf], war=[l0c.buf])
        b3row = ar.alloc(1024, name + "_b3")
    if router_w is not None:
        router_phase(c, xnT, router_w, with_bias=bias, sigma_scale=sigma_scale,
                     brap=(l0c.f32(0, 128, 0, 8) if bias else None),
                     brbuf=(l0c.buf if bias else None))
    hscr = ar.alloc(2 * 512, name + "_hscr")
    nj = dff // 128
    CH = min(1024, T)
    nch = T // CH
    h = ar.alloc(nj * CH, name + "_h")
    wbuf = [ar.alloc(1024, f"{name}_w{i}") for i in range(8)]
    w3buf = [ar.alloc(1024, f"{name}_w3{i}") for i in range(4)]
    wd_res = None
    if dff == F1:
        wd_res = ar.alloc(nj * 1024, name + "_wdres")
    nsub = max(CH // 512, 1)
    sub = CH // nsub
    for e in range(NE):
        if bias:
            p.dma("gpsimd", b3row.bf(0, 1, 0, 1024),
                  c.crow[0:1, 1024 * e:1024 * (e + 1)],
                  writes=[b3row.buf], war=[b3row.buf])
        for ch in range(nch):
            t0 = CH * ch
            for j in range(nj):
                if wd_res is not None and ch == 0 and j == 2:
                    for k in range(nj):
                        wsrc = bass.AP(wd.tensor, wd.offset + e * dff * D + 128 * k * D,
                                       [[D, 128], [1, D]])
                        c.p.dma("gpsimd", wd_res.bf(0, 128, 1024 * k, 1024 * (k + 1)),
                                wsrc, writes=[wd_res.buf], war=[wd_res.buf])
                wb1 = wbuf[(j % 4) * 2]
                wb2 = wbuf[(j % 4) * 2 + 1]
                src1 = bass.AP(wg.tensor, wg.offset + (e * nj + j) * 128 * D,
                               [[D, 128], [1, D]])
                src2 = bass.AP(wu.tensor, wu.offset + (e * nj + j) * 128 * D,
                               [[D, 128], [1, D]])
                if nch > 1 and dff == DFF0:
                    if ch == 0:
                        p.dma("gpsimd", wb1.bf(0, 128, 0, 1024), src1,
                              writes=[wb1.buf], war=[wb1.buf])
                        p.dma("sync", c.w1bf[e * nj + j], wb1.bf(0, 128, 0, 1024),
                              reads=[wb1.buf], writes=[c.Bw1bf[e * nj + j]])
                        p.dma("gpsimd", wb2.bf(0, 128, 0, 1024), src2,
                              writes=[wb2.buf], war=[wb2.buf])
                        p.dma("sync", c.w2bf[e * nj + j], wb2.bf(0, 128, 0, 1024),
                              reads=[wb2.buf], writes=[c.Bw2bf[e * nj + j]])
                    else:
                        p.dma("gpsimd", wb1.bf(0, 128, 0, 1024), c.w1bf[e * nj + j],
                              reads=[c.Bw1bf[e * nj + j]], writes=[wb1.buf], war=[wb1.buf])
                        p.dma("scalar", wb2.bf(0, 128, 0, 1024), c.w2bf[e * nj + j],
                              reads=[c.Bw2bf[e * nj + j]], writes=[wb2.buf], war=[wb2.buf])
                else:
                    p.dma("gpsimd", wb1.bf(0, 128, 0, 1024), src1,
                          writes=[wb1.buf], war=[wb1.buf])
                    p.dma("gpsimd", wb2.bf(0, 128, 0, 1024), src2,
                          writes=[wb2.buf], war=[wb2.buf])
                bk = 0 if (j % 2 == 0) else 4
                for mat, wb in ((0, wb1), (1, wb2)):
                    for si in range(nsub):
                        bank = bk + mat * nsub + si
                        mm_group(c, [bank],
                                 [(lambda t, d=d, t0=t0, si=si, wb=wb, bank=bank: t.matmul(
                                     c.P[bank][:, 0:sub],
                                     wb.bf(0, 128, 128 * d, 128 * (d + 1)),
                                     xnT.bf(0, 128, T * d + t0 + sub * si,
                                            T * d + t0 + sub * (si + 1)),
                                     start=(d == 0), stop=(d == DT - 1)),
                                   [wb.buf] + xn_tiles(xnT, t0 + sub * si,
                                                       t0 + sub * (si + 1)))
                                  for d in range(DT)])
                for si in range(nsub):
                    b1k = bk + si
                    b2k = bk + nsub + si
                    if bias:
                        b1ap = l0c.f32(0, 128, 8 + 32 * e + j, 9 + 32 * e + j)
                        b2ap = l0c.f32(0, 128, 264 + 32 * e + j, 265 + 32 * e + j)
                        p.emit("scalar", lambda s, b1k=b1k, b1ap=b1ap:
                               s.activation(hscr.f32(0, 128, 0, sub),
                                            c.P[b1k][:, 0:sub], AF.Silu,
                                            bias=b1ap, scale=1.0),
                               reads=[c.BP[b1k], l0c.buf], writes=[hscr.buf])
                        p.emit("vector", lambda v, j=j, si=si, b2k=b2k, b2ap=b2ap:
                               v.scalar_tensor_tensor(
                                   out=h.bf(0, 128, CH * j + sub * si,
                                            CH * j + sub * (si + 1)),
                                   in0=c.P[b2k][:, 0:sub], scalar=b2ap,
                                   in1=hscr.f32(0, 128, 0, sub),
                                   op0=ALU.add, op1=ALU.mult),
                               reads=[c.BP[b2k], hscr.buf, l0c.buf], writes=[h.buf])
                    else:
                        p.emit("scalar", lambda s, b1k=b1k: s.activation(
                            hscr.f32(0, 128, 0, sub), c.P[b1k][:, 0:sub],
                            AF.Silu, bias=0.0, scale=1.0),
                               reads=[c.BP[b1k]], writes=[hscr.buf])
                        p.emit("vector", lambda v, j=j, si=si, b2k=b2k: v.tensor_mul(
                            h.bf(0, 128, CH * j + sub * si, CH * j + sub * (si + 1)),
                            hscr.f32(0, 128, 0, sub), c.P[b2k][:, 0:sub]),
                               reads=[c.BP[b2k], hscr.buf], writes=[h.buf])
            PTOK = min(512, CH)
            npass = CH // PTOK
            ntt = PTOK // 128
            for ps in range(npass):
                pt0 = t0 + PTOK * ps
                mms = {b: [] for b in range(2 * ntt)}
                first_use = (ch == 0 and ps == 0)
                for k in range(nj):
                    if wd_res is None:
                        w3b = w3buf[k % 4]
                        cb = c.Bw3bf[e * nj + k]
                        if first_use:
                            p.dma("gpsimd", w3b.bf(0, 128, 0, 1024),
                                  bass.AP(wd.tensor, wd.offset + e * dff * D + 128 * k * D,
                                          [[D, 128], [1, D]]),
                                  writes=[w3b.buf], war=[w3b.buf])
                            p.dma("sync", c.w3bf[e * nj + k], w3b.bf(0, 128, 0, 1024),
                                  reads=[w3b.buf], writes=[cb])
                        else:
                            p.dma("scalar", w3b.bf(0, 128, 0, 1024),
                                  c.w3bf[e * nj + k],
                                  reads=[cb], writes=[w3b.buf], war=[w3b.buf])
                    else:
                        w3b = None
                    for tt4 in range(ntt):
                        for hf in range(2):
                            bank = tt4 * 2 + hf
                            if w3b is not None:
                                wap = w3b.bf(0, 128, 512 * hf, 512 * (hf + 1))
                                wbb = w3b.buf
                            else:
                                wap = wd_res.bf(0, 128, 1024 * k + 512 * hf,
                                                1024 * k + 512 * (hf + 1))
                                wbb = wd_res.buf
                            last_k = (k == nj - 1) and not bias
                            k_last = (tt4 == ntt - 1 and hf == 1)
                            p.emit("tensor", lambda t, k=k, tt4=tt4, wap=wap, bank=bank,
                                   pt0=pt0, t0=t0, last_k=last_k: t.matmul(
                                c.P[bank][:, 0:512],
                                h.bf(0, 128, CH * k + (pt0 - t0) + 128 * tt4,
                                     CH * k + (pt0 - t0) + 128 * (tt4 + 1)),
                                wap, start=(k == 0), stop=last_k),
                                   reads=[h.buf, wbb],
                                   writes=([c.BP[bank]] if last_k else []),
                                   war=([c.BP[bank]] if k == 0 else ()),
                                   sem=("" if (last_k or k_last) else None),
                                   first=(k == 0), last=last_k)
                if bias:
                    for tt4 in range(ntt):
                        for hf in range(2):
                            bank = tt4 * 2 + hf
                            p.emit("tensor", lambda t, hf=hf, bank=bank: t.matmul(
                                c.P[bank][:, 0:512],
                                c.ones.bf(0, 1, 0, 128),
                                b3row.bf(0, 1, 512 * hf, 512 * (hf + 1)),
                                start=False, stop=True),
                                   reads=[c.ones.buf, b3row.buf],
                                   writes=[c.BP[bank]], first=False, last=True)
                for tt4 in range(ntt):
                    tglob = pt0 // 128 + tt4
                    cwap = c.cw.f32(0, 128, NE * tglob + e, NE * tglob + e + 1)
                    for hf in range(2):
                        bank = tt4 * 2 + hf
                        p.emit("vector", lambda v, tglob=tglob, hf=hf, bank=bank,
                               cwap=cwap: v.scalar_tensor_tensor(
                            out=c.xres(tglob)[:, 512 * hf:512 * (hf + 1)],
                            in0=c.P[bank][:, 0:512], scalar=cwap,
                            in1=c.xres(tglob)[:, 512 * hf:512 * (hf + 1)],
                            op0=ALU.mult, op1=ALU.add),
                               reads=[c.BP[bank], c.cw.buf, c.Bx[tglob]],
                               writes=[c.Bx[tglob]])
    for r in [h, hscr] + wbuf + w3buf:
        c.ar.freer(r)
    if wd_res is not None:
        c.ar.freer(wd_res)
    if bias:
        c.ar.freer(l0c)
        c.ar.freer(b3row)


def gate_phase(c, xnT):
    p, ar, T, TT = c.p, c.ar, c.T, c.TT
    c.Bgh = [Buf(f"gh{t}") for t in range(TT)]
    wgt = stream_wslabs(c, c.Wgate, DT, 1024, "wgate")
    bg = ar.alloc(2 * 1024, "bgate")
    p.dma("sync", bg.f32(0, 128, 0, 1024), c.consts[:, 2568:3592], writes=[bg.buf])
    scr = ar.alloc(2 * 1032, "gatescr")
    gt = ar.alloc(2 * 1024, "gtile")
    gtb = [Buf("gt0"), Buf("gt1")]
    for t in range(TT):
        b0 = (t % 2) * 2
        mm_group(c, [b0, b0 + 1],
                 [(lambda tt, t=t, d=d, h=h, b0=b0: tt.matmul(
                     c.P[b0 + h][:, 0:512],
                     xnT.bf(0, 128, T * d + 128 * t, T * d + 128 * (t + 1)),
                     wgt.bf(0, 128, 1024 * d + 512 * h, 1024 * d + 512 * (h + 1)),
                     start=(d == 0), stop=(d == DT - 1)), [xnT.tbufs[t], wgt.sbufs[d]])
                  for d in range(DT) for h in range(2)])
        g0 = scr.f32(0, 128, 0, 1024)
        for h in range(2):
            p.emit("vector", lambda v, h=h, b0=b0, g0=g0: v.tensor_add(
                g0[:, 512 * h:512 * (h + 1)], c.P[b0 + h][:, 0:512],
                bg.f32(0, 128, 512 * h, 512 * (h + 1))),
                   reads=[c.BP[b0 + h], bg.buf], writes=[scr.buf])
        p.emit("scalar", lambda s, g0=g0: s.activation(g0, g0, AF.Sigmoid,
                                                       bias=0.0, scale=1.0),
               reads=[scr.buf], writes=[scr.buf])
        rs = scr.f32(0, 128, 1024, 1025)
        p.emit("vector", lambda v, rs=rs, g0=g0: v.tensor_reduce(
            rs, g0, axis=AX.X, op=ALU.add), reads=[scr.buf], writes=[scr.buf])
        p.emit("vector", lambda v, rs=rs: v.tensor_scalar(
            out=rs, in0=rs, scalar1=1.0 / D, scalar2=EPS, op0=ALU.mult, op1=ALU.add),
               reads=[scr.buf], writes=[scr.buf])
        p.emit("vector", lambda v, rs=rs: v.reciprocal(rs, rs),
               reads=[scr.buf], writes=[scr.buf])
        gi = t % 2
        p.emit("vector", lambda v, gi=gi, g0=g0, rs=rs: v.tensor_scalar(
            out=gt.bf(0, 128, 1024 * gi, 1024 * (gi + 1)), in0=g0, scalar1=rs,
            scalar2=None, op0=ALU.mult),
               reads=[scr.buf], writes=[gtb[gi]])
        p.dma("sync", c.gate_hbm[128 * t:128 * (t + 1), :],
              gt.bf(0, 128, 1024 * gi, 1024 * (gi + 1)), reads=[gtb[gi]],
              writes=[c.Bgh[t]])
    fold_bufs(gt, gtb)
    free_wslabs(c, wgt)
    for r in (bg, scr, gt):
        c.ar.freer(r)


def scan_phase(c):
    p, ar, T, TT = c.p, c.ar, c.T, c.TT
    wqkv = stream_wslabs(c, c.Wqkv, DT, 3072, "wqkv")
    wproj = stream_wslabs(c, c.Wproj, DT, 1024, "wproj")
    brow = ar.alloc(4096, "brow")
    p.dma("gpsimd", brow.bf(0, 1, 0, 4096), c.crow[0:1, 8192:12288], writes=[brow.buf])
    kvacc = ar.alloc(2 * 1024, "kvacc")
    ksacc = ar.alloc(2 * 16, "ksacc")
    kvblk = ar.alloc(1024, "kvblk")
    ksblk = ar.alloc(64, "ksblk")
    for reg, n, f in ((kvacc, 1024, 1), (ksacc, 16, 1)):
        p.emit("gpsimd", lambda g, reg=reg, n=n: g.memset(reg.f32(0, 64, 0, n), 0.0),
               writes=[reg.buf])
    p.emit("gpsimd", lambda g: g.memset(kvblk.bf(0, 128, 0, 1024), 0.0),
           writes=[kvblk.buf])
    p.emit("gpsimd", lambda g: g.memset(ksblk.bf(0, 128, 0, 64), 0.0),
           writes=[ksblk.buf])
    scr = ar.alloc(2 * 2048, "sscr")
    xnc = ar.alloc(1024, "xnc")
    xncT = ar.alloc(2048, "xncT")
    q2 = ar.alloc(1024, "q2tok")
    k2 = ar.alloc(1024, "k2tok")
    vv = ar.alloc(1024, "vchunk")
    q2T = ar.alloc(1024, "q2T")
    k2T = ar.alloc(1024, "k2T")
    at = ar.alloc(16 * 128, "at")
    outc = ar.alloc(1024, "outc")
    outT = ar.alloc(1024, "outT")
    dsc = ar.alloc(64, "densc")
    gtc = ar.alloc(2 * 1024, "gchunk")
    gtcb = [Buf("gc0"), Buf("gc1")]
    xnc2 = ar.alloc(2 * 1024, "xnchunk")
    xncb = [Buf("xc0"), Buf("xc1")]
    nch_lim = int(os.environ.get("KN_CH", "1000000"))
    xncTb = [Buf("xncT0"), Buf("xncT1")]

    def prefetch_chunk(ch):
        gi = ch % 2
        p.dma("sync", gtc.bf(0, 128, 1024 * gi, 1024 * (gi + 1)),
              c.gate_hbm[128 * ch:128 * (ch + 1), :], reads=[c.Bgh[ch]],
              writes=[gtcb[gi]], war=[gtcb[gi]])
        p.dma("sync", xnc2.bf(0, 128, 1024 * gi, 1024 * (gi + 1)),
              c.xnorm_hbm[128 * ch:128 * (ch + 1), :], reads=[c.Bxn[ch]],
              writes=[xncb[gi]], war=[xncb[gi]])
        trans_batch(c, [(xnc2.bf(0, 128, 1024 * gi + 128 * d, 1024 * gi + 128 * (d + 1)),
                         xncb[gi]) for d in range(DT)], 7,
                    xncT.bf(0, 128, 1024 * gi, 1024 * (gi + 1)), xncTb[gi])

    if min(TT, nch_lim) > 0:
        prefetch_chunk(0)
    for ch in range(min(TT, nch_lim)):
        gi = ch % 2
        gap = gtc.bf(0, 128, 1024 * gi, 1024 * (gi + 1))
        gbuf = gtcb[gi]
        for half in range(2):
            mms = []
            for d in range(DT):
                for s3 in range(3):
                    col = 1536 * half + 512 * s3
                    mms.append((lambda t, d=d, col=col, s3=s3, gi=gi: t.matmul(
                        c.P[s3][:, 0:512],
                        xncT.bf(0, 128, 1024 * gi + 128 * d, 1024 * gi + 128 * (d + 1)),
                        wqkv.bf(0, 128, 3072 * d + col, 3072 * d + col + 512),
                        start=(d == 0), stop=False), [xncTb[gi], wqkv.sbufs[d]]))
            tail = [(lambda t, half=half, s3=s3: t.matmul(
                c.P[s3][:, 0:512], c.ones.bf(0, 1, 0, 128),
                brow.bf(0, 1, 1536 * half + 512 * s3, 1536 * half + 512 * (s3 + 1)),
                start=False, stop=True), [c.ones.buf, brow.buf]) for s3 in range(3)]
            mm_group(c, [0, 1, 2], mms, tail=tail)
            if half == 0:
                qf = scr.f32(0, 128, 0, 1024)
                for s3 in range(2):
                    p.emit("vector", lambda v, s3=s3, qf=qf, gap=gap: v.tensor_mul(
                        qf[:, 512 * s3:512 * (s3 + 1)], c.P[s3][:, 0:512],
                        gap[:, 512 * s3:512 * (s3 + 1)]),
                           reads=[c.BP[s3], gbuf], writes=[scr.buf])
                p.emit("vector", lambda v, qf=qf: v.scalar_tensor_tensor(
                    out=qf, in0=qf, scalar=0.0, in1=qf, op0=ALU.max, op1=ALU.mult),
                       reads=[scr.buf], writes=[scr.buf])
                p.emit("vector", lambda v, qf=qf: v.tensor_scalar(
                    out=q2.bf(0, 128, 0, 1024), in0=qf, scalar1=EPS, scalar2=None,
                    op0=ALU.add), reads=[scr.buf], writes=[q2.buf])
                kf = scr.f32(0, 128, 1024, 1536)
                p.emit("vector", lambda v, kf=kf, gap=gap: v.tensor_mul(
                    kf, c.P[2][:, 0:512], gap[:, 0:512]),
                       reads=[c.BP[2], gbuf], writes=[scr.buf])
            else:
                kf2 = scr.f32(0, 128, 1536, 2048)
                p.emit("vector", lambda v, kf2=kf2, gap=gap: v.tensor_mul(
                    kf2, c.P[0][:, 0:512], gap[:, 512:1024]),
                       reads=[c.BP[0], gbuf], writes=[scr.buf])
                kfull = scr.f32(0, 128, 1024, 2048)
                p.emit("vector", lambda v, kfull=kfull: v.scalar_tensor_tensor(
                    out=kfull, in0=kfull, scalar=0.0, in1=kfull,
                    op0=ALU.max, op1=ALU.mult),
                       reads=[scr.buf], writes=[scr.buf])
                p.emit("vector", lambda v, kfull=kfull: v.tensor_scalar(
                    out=k2.bf(0, 128, 0, 1024), in0=kfull, scalar1=EPS, scalar2=None,
                    op0=ALU.add), reads=[scr.buf], writes=[k2.buf])
                for s3 in range(2):
                    p.emit("scalar", lambda s, s3=s3: s.activation(
                        vv.bf(0, 128, 512 * s3, 512 * (s3 + 1)),
                        c.P[1 + s3][:, 0:512], AF.Copy, bias=0.0, scale=1.0),
                           reads=[c.BP[1 + s3]], writes=[vv.buf])
        if ch + 1 < min(TT, nch_lim):
            prefetch_chunk(ch + 1)
        trans_batch(c, [(q2.bf(0, 128, 128 * d, 128 * (d + 1)), q2.buf)
                        for d in range(DT)], 7, q2T.bf(0, 128, 0, 1024), q2T.buf)
        trans_batch(c, [(k2.bf(0, 128, 128 * d, 128 * (d + 1)), k2.buf)
                        for d in range(DT)], 7, k2T.bf(0, 128, 0, 1024), k2T.buf)
        for hp in range(NH // 2):
            for odd in range(2):
                hh = 2 * hp + odd
                p.emit("vector", lambda v, hp=hp, odd=odd, hh=hh: v.tensor_copy(
                    kvblk.bf(64 * odd, 64 * odd + 64, 128 * hp + 64 * odd,
                             128 * hp + 64 * odd + 64),
                    kvacc.f32(0, 64, 64 * hh, 64 * (hh + 1))),
                       reads=[kvacc.buf], writes=[kvblk.buf])
                p.emit("vector", lambda v, hh=hh, odd=odd: v.tensor_copy(
                    ksblk.bf(64 * odd, 64 * odd + 64, hh, hh + 1),
                    ksacc.f32(0, 64, hh, hh + 1)),
                       reads=[ksacc.buf], writes=[ksblk.buf])
        for rnd in range(2):
            for i in range(8):
                hh = 8 * rnd + i
                base = 64 * (hh % 2)
                p.emit("tensor", lambda t, hh=hh, i=i, base=base: t.matmul(
                    c.P[i][:, 0:128],
                    k2T.bf(base, base + 64, 128 * (hh // 2), 128 * (hh // 2) + 128),
                    q2T.bf(base, base + 64, 128 * (hh // 2), 128 * (hh // 2) + 128),
                    start=True, stop=True),
                       reads=[k2T.buf, q2T.buf],
                       writes=[c.BP[i]], war=[c.BP[i]],
                       first=True, last=True)
            for i in range(8):
                hh = 8 * rnd + i
                p.emit("vector", lambda v, hh=hh, i=i: v.tensor_mul(
                    at.bf(0, 128, 128 * hh, 128 * (hh + 1)),
                    c.P[i][:, 0:128],
                    c.maskA.bf(0, 128, 0, 128)),
                       reads=[c.BP[i], c.maskA.buf], writes=[at.buf])
        for half in range(2):
            bank = 2 + half
            mms = [(lambda t, hh=8 * half + i, i=i, bank=bank: t.matmul(
                c.P[bank][:, 64 * i:64 * (i + 1)],
                at.bf(0, 128, 128 * hh, 128 * (hh + 1)),
                vv.bf(0, 128, 64 * hh, 64 * (hh + 1)),
                start=(i == 0), stop=False), [at.buf, vv.buf]) for i in range(8)]
            tail = [(lambda t, hp=4 * half + j, j=j, bank=bank: t.matmul(
                c.P[bank][:, 128 * j:128 * (j + 1)],
                q2T.bf(0, 128, 128 * hp, 128 * (hp + 1)),
                kvblk.bf(0, 128, 128 * hp, 128 * (hp + 1)),
                start=False, stop=(j == 3)), [q2T.buf, kvblk.buf]) for j in range(4)]
            mm_group(c, [bank], mms, tail=tail)
        mms = [(lambda t, hh=hh: t.matmul(
            c.P[4][:, hh:hh + 1],
            at.bf(0, 128, 128 * hh, 128 * (hh + 1)),
            c.ones.bf(0, 128, 0, 1),
            start=(hh == 0), stop=False), [at.buf, c.ones.buf]) for hh in range(NH)]
        tail = [(lambda t, hp=hp: t.matmul(
            c.P[4][:, 2 * hp:2 * hp + 2],
            q2T.bf(0, 128, 128 * hp, 128 * (hp + 1)),
            ksblk.bf(0, 128, 2 * hp, 2 * hp + 2),
            start=False, stop=(hp == NH // 2 - 1)), [q2T.buf, ksblk.buf])
            for hp in range(NH // 2)]
        mm_group(c, [4], mms, tail=tail)
        dn = dsc.f32(0, 128, 0, 16)
        p.emit("vector", lambda v, dn=dn: v.tensor_scalar(
            out=dn, in0=c.P[4][:, 0:16], scalar1=EPS, scalar2=None, op0=ALU.add),
               reads=[c.BP[4]], writes=[dsc.buf])
        p.emit("vector", lambda v, dn=dn: v.reciprocal(dn, dn),
               reads=[dsc.buf], writes=[dsc.buf])
        for hh in range(NH):
            bank = 2 + hh // 8
            p.emit("vector", lambda v, hh=hh, bank=bank, dn=dn: v.tensor_scalar(
                out=outc.bf(0, 128, 64 * hh, 64 * (hh + 1)),
                in0=c.P[bank][:, 64 * (hh % 8):64 * (hh % 8 + 1)],
                scalar1=dn[:, hh:hh + 1], scalar2=None, op0=ALU.mult),
                   reads=[c.BP[bank], dsc.buf], writes=[outc.buf])
        if os.environ.get("KDBG"):
            for nm, reg, ncol in ((0, q2, 1024), (1, k2, 1024), (2, vv, 1024),
                                  (3, outc, 1024), (4, xnc, 1024), (5, q2T, 1024),
                                  (6, kvblk, 1024)):
                p.dma("sync", c.dbg_bf[ch, nm, :, 0:ncol], reg.bf(0, 128, 0, ncol),
                      reads=[reg.buf])
            p.dma("sync", c.dbg_bf[ch, 7, :, 0:2048], at.bf(0, 128, 0, 2048),
                  reads=[at.buf])
            p.dma("sync", c.dbg_f[ch, :, 0:16], dsc.f32(0, 128, 0, 16),
                  reads=[dsc.buf])
        for half in range(2):
            bank = 5 + half
            mm_group(c, [bank],
                     [(lambda t, hh=8 * half + i, i=i, bank=bank: t.matmul(
                         c.P[bank][0:64, 64 * i:64 * (i + 1)],
                         k2.bf(0, 128, 64 * hh, 64 * (hh + 1)),
                         vv.bf(0, 128, 64 * hh, 64 * (hh + 1)),
                         start=(i == 0), stop=(i == 7)), [k2.buf, vv.buf])
                      for i in range(8)])
        mm_group(c, [4],
                 [(lambda t, hh=hh: t.matmul(
                     c.P[4][0:64, 32 + hh:33 + hh],
                     k2.bf(0, 128, 64 * hh, 64 * (hh + 1)),
                     c.ones.bf(0, 128, 0, 1),
                     start=(hh == 0), stop=(hh == NH - 1)), [k2.buf, c.ones.buf])
                  for hh in range(NH)])
        for half in range(2):
            bank = 5 + half
            p.emit("vector", lambda v, half=half, bank=bank: v.tensor_add(
                kvacc.f32(0, 64, 512 * half, 512 * (half + 1)),
                kvacc.f32(0, 64, 512 * half, 512 * (half + 1)),
                c.P[bank][0:64, 0:512]),
                   reads=[c.BP[bank], kvacc.buf], writes=[kvacc.buf])
        p.emit("vector", lambda v: v.tensor_add(
            ksacc.f32(0, 64, 0, 16), ksacc.f32(0, 64, 0, 16), c.P[4][0:64, 32:48]),
               reads=[c.BP[4], ksacc.buf], writes=[ksacc.buf])
        trans_batch(c, [(outc.bf(0, 128, 128 * d, 128 * (d + 1)), outc.buf)
                        for d in range(DT)], 7, outT.bf(0, 128, 0, 1024), outT.buf)
        mms = []
        for d in range(DT):
            for h in range(2):
                mms.append((lambda t, d=d, h=h: t.matmul(
                    c.P[h][:, 0:512],
                    outT.bf(0, 128, 128 * d, 128 * (d + 1)),
                    wproj.bf(0, 128, 1024 * d + 512 * h, 1024 * d + 512 * (h + 1)),
                    start=(d == 0), stop=False), [outT.buf, wproj.sbufs[d]]))
        tail = [(lambda t, h=h: t.matmul(
            c.P[h][:, 0:512], c.ones.bf(0, 1, 0, 128),
            brow.bf(0, 1, 3072 + 512 * h, 3072 + 512 * (h + 1)),
            start=False, stop=True), [c.ones.buf, brow.buf]) for h in range(2)]
        mm_group(c, [0, 1], mms, tail=tail)
        for h in range(2):
            p.emit("vector", lambda v, ch=ch, h=h: v.tensor_add(
                c.xres(ch)[:, 512 * h:512 * (h + 1)],
                c.xres(ch)[:, 512 * h:512 * (h + 1)], c.P[h][:, 0:512]),
                   reads=[c.BP[h], c.Bx[ch]], writes=[c.Bx[ch]])
    fold_bufs(gtc, gtcb)
    fold_bufs(xnc2, xncb)
    fold_bufs(xncT, xncTb)
    free_wslabs(c, wqkv)
    free_wslabs(c, wproj)
    for r in (brow, kvacc, ksacc, kvblk, ksblk, scr, xnc, xncT,
              q2, k2, vv, q2T, k2T, at, outc, outT, dsc, gtc, xnc2):
        c.ar.freer(r)


def rebuild_xnT(c):
    p, ar, T, TT = c.p, c.ar, c.T, c.TT
    xnT = ar.alloc(DT * T, "xnT1r")
    scr = ar.alloc(1024, "rbscr")
    st = c.stats
    for t in range(TT):
        m = st.f32(0, 128, t, t + 1)
        r = st.f32(0, 128, TT + t, TT + t + 1)
        xn = scr.bf(0, 128, 0, 1024)
        p.emit("vector", lambda v, t=t, xn=xn, m=m, r=r: v.tensor_scalar(
            out=xn, in0=c.xres(t), scalar1=m, scalar2=r,
            op0=ALU.subtract, op1=ALU.mult),
               reads=[c.Bx[t], st.buf], writes=[scr.buf])
        trans_batch(c, [(xn[:, 128 * d:128 * (d + 1)], scr.buf) for d in range(DT)],
                    7, xnT_dst(c, xnT, t), xnT.buf)
    c.ar.freer(scr)
    return xnT


def swiglu(c, xnT):
    p, ar, T, TT = c.p, c.ar, c.T, c.TT
    njh = F2 // 2 // 128
    hreg = ar.alloc(njh * T, "swh")
    wbuf = [ar.alloc(1024, f"sww{i}") for i in range(8)]
    w3buf = [ar.alloc(1024, f"sww3{i}") for i in range(4)]
    hscr = ar.alloc(2 * 512, "swhscr")
    nseg = max(T // 512, 1)
    seglen = min(512, T)
    for fh in range(2):
        j0 = fh * njh
        for j in range(njh):
            wb1 = wbuf[(j % 4) * 2]
            wb2 = wbuf[(j % 4) * 2 + 1]
            src1 = bass.AP(c.sw_gate.tensor, c.sw_gate.offset + (j0 + j) * 128 * D,
                           [[D, 128], [1, D]])
            src2 = bass.AP(c.sw_up.tensor, c.sw_up.offset + (j0 + j) * 128 * D,
                           [[D, 128], [1, D]])
            p.dma("gpsimd", wb1.bf(0, 128, 0, 1024), src1,
                  writes=[wb1.buf], war=[wb1.buf])
            p.dma("gpsimd", wb2.bf(0, 128, 0, 1024), src2,
                  writes=[wb2.buf], war=[wb2.buf])
            bk = 0 if (j % 2 == 0) else 4
            for seg in range(nseg):
                g0, g1 = seglen * seg, seglen * (seg + 1)
                b1k = bk + (seg % 2)
                b2k = bk + 2 + (seg % 2)
                for bank, wb in ((b1k, wb1), (b2k, wb2)):
                    mm_group(c, [bank],
                             [(lambda t, d=d, g0=g0, g1=g1, wb=wb, bank=bank: t.matmul(
                                 c.P[bank][:, 0:g1 - g0],
                                 wb.bf(0, 128, 128 * d, 128 * (d + 1)),
                                 xnT.bf(0, 128, T * d + g0, T * d + g1),
                                 start=(d == 0), stop=(d == DT - 1)),
                               [wb.buf] + xn_tiles(xnT, g0, g1)) for d in range(DT)])
                p.emit("scalar", lambda s, g1=g1, g0=g0, b1k=b1k: s.activation(
                    hscr.f32(0, 128, 0, g1 - g0), c.P[b1k][:, 0:g1 - g0],
                    AF.Silu, bias=0.0, scale=1.0),
                       reads=[c.BP[b1k]], writes=[hscr.buf])
                p.emit("vector", lambda v, j=j, g0=g0, g1=g1, b2k=b2k: v.tensor_mul(
                    hreg.bf(0, 128, T * j + g0, T * j + g1),
                    hscr.f32(0, 128, 0, g1 - g0), c.P[b2k][:, 0:g1 - g0]),
                       reads=[c.BP[b2k], hscr.buf], writes=[hreg.buf])
        PTOK = min(512, T)
        npass = T // PTOK
        ntt = PTOK // 128
        for ps in range(npass):
            pt0 = PTOK * ps
            for k in range(njh):
                w3b = w3buf[k % 4]
                cb = c.Bswdbf[j0 + k]
                if ps == 0:
                    p.dma("gpsimd", w3b.bf(0, 128, 0, 1024),
                          bass.AP(c.sw_down.tensor,
                                  c.sw_down.offset + 128 * (j0 + k) * D,
                                  [[D, 128], [1, D]]),
                          writes=[w3b.buf], war=[w3b.buf])
                    p.dma("sync", c.swdbf[j0 + k], w3b.bf(0, 128, 0, 1024),
                          reads=[w3b.buf], writes=[cb])
                else:
                    p.dma("scalar", w3b.bf(0, 128, 0, 1024), c.swdbf[j0 + k],
                          reads=[cb], writes=[w3b.buf], war=[w3b.buf])
                for tt4 in range(ntt):
                    for hf in range(2):
                        bank = tt4 * 2 + hf
                        lk = (k == njh - 1)
                        k_last = (tt4 == ntt - 1 and hf == 1)
                        p.emit("tensor", lambda t, k=k, tt4=tt4, hf=hf, w3b=w3b,
                               bank=bank, pt0=pt0: t.matmul(
                            c.P[bank][:, 0:512],
                            hreg.bf(0, 128, T * k + pt0 + 128 * tt4,
                                    T * k + pt0 + 128 * (tt4 + 1)),
                            w3b.bf(0, 128, 512 * hf, 512 * (hf + 1)),
                            start=(k == 0), stop=(k == njh - 1)),
                               reads=[hreg.buf, w3b.buf],
                               writes=([c.BP[bank]] if lk else []),
                               war=([c.BP[bank]] if k == 0 else ()),
                               sem=("" if (lk or k_last) else None),
                               first=(k == 0), last=lk)
            for tt4 in range(ntt):
                tglob = pt0 // 128 + tt4
                for hf in range(2):
                    bank = tt4 * 2 + hf
                    p.emit("vector", lambda v, tglob=tglob, hf=hf, bank=bank:
                           v.tensor_add(
                        c.xres(tglob)[:, 512 * hf:512 * (hf + 1)],
                        c.xres(tglob)[:, 512 * hf:512 * (hf + 1)],
                        c.P[bank][:, 0:512]),
                           reads=[c.BP[bank], c.Bx[tglob]], writes=[c.Bx[tglob]])
    for r in [hreg, hscr] + wbuf + w3buf:
        c.ar.freer(r)


def _consts_np(inputs, parts):
    l1 = any(s.startswith("L1") for s in parts)
    cst = np.zeros((128, 3592), np.float32)
    crow = np.zeros((1, 12288), np.float32)
    if "L0" in parts:
        cst[:, 0:8] = inputs["L0_ffn_br"][None, :]
        cst[:, 8:264] = np.asarray(inputs["L0_ffn_b1"]).reshape(NE, 32, 128).transpose(2, 0, 1).reshape(128, 256)
        cst[:, 264:520] = np.asarray(inputs["L0_ffn_b2"]).reshape(NE, 32, 128).transpose(2, 0, 1).reshape(128, 256)
        crow[0, 0:8192] = np.asarray(inputs["L0_ffn_b3"]).reshape(-1)
    if l1:
        cst[:, 520:1544] = inputs["L1_attn_ln_g"][None, :]
        cst[:, 1544:2568] = inputs["L1_attn_ln_b"][None, :]
        cst[:, 2568:3592] = inputs["L1_attn_b_gate"][None, :]
        crow[0, 8192:11264] = np.asarray(inputs["L1_attn_b_qkv"])
        crow[0, 11264:12288] = np.asarray(inputs["L1_attn_b_proj"])
    return cst, crow


def _in_map(inputs, xcore, parts):
    cst, crow = _consts_np(inputs, parts)
    m = {"x": xcore, "consts": cst, "crow": crow}
    def _packj(w, nj):
        w = np.asarray(w)
        E = w.shape[0]
        return w.reshape(E, DT, 128, nj, 128).transpose(0, 3, 2, 1, 4).reshape(E, nj, 128, DT * 128)

    if "L0" in parts:
        m["Wq"] = inputs["L0_attn_Wq"]
        m["Wk"] = inputs["L0_attn_Wk"]
        m["Wv"] = inputs["L0_attn_Wv"]
        m["Wr"] = inputs["L0_ffn_Wr"]
        m["W1"] = _packj(inputs["L0_ffn_W1"], DFF0 // 128)
        m["W2"] = _packj(inputs["L0_ffn_W2"], DFF0 // 128)
        m["W3"] = inputs["L0_ffn_W3"]
    if any(s.startswith("L1") for s in parts):
        m["Wqkv"] = inputs["L1_attn_W_qkv"]
        m["Wgate"] = inputs["L1_attn_W_gate"]
        m["Wproj"] = inputs["L1_attn_W_proj"]
        m["gwT"] = np.asarray(inputs["L1_ffn_gate_w"]).T
        m["w_gate"] = _packj(inputs["L1_ffn_w_gate"], F1 // 128)
        m["w_up"] = _packj(inputs["L1_ffn_w_up"], F1 // 128)
        m["w_down"] = inputs["L1_ffn_w_down"]
        m["sw_gate"] = _packj(np.asarray(inputs["L1_ffn_sw_gate"])[None], F2 // 128)[0]
        m["sw_up"] = _packj(np.asarray(inputs["L1_ffn_sw_up"])[None], F2 // 128)[0]
        m["sw_down"] = inputs["L1_ffn_sw_down"]
    return {k: np.ascontiguousarray(np.asarray(v, dtype=np.float32)) for k, v in m.items()}


def _tune_compiler():
    pass


def kernel(**inputs):
    _tune_compiler()
    x = np.asarray(inputs["L0_attn_x"], dtype=np.float32)
    B, T, _ = x.shape
    nc = build_core_program(T, ("L0", "L1"))
    in_maps = [_in_map(inputs, x[i], ("L0", "L1")) for i in range(B)]
    res = run_bass_kernel_spmd(nc, in_maps, core_ids=list(range(B)))
    out = np.stack([np.asarray(r["OUT"]) for r in res.results], axis=0)
    return out.astype(np.float32)


# version 12
# speedup vs baseline: 1.2030x; 1.0750x over previous
import os
from contextlib import ExitStack

import numpy as np

import concourse.bass as bass
import concourse.mybir as mybir
from concourse.bass_utils import run_bass_kernel_spmd

F32 = mybir.dt.float32
BF16 = mybir.dt.bfloat16
AF = mybir.ActivationFunctionType
ALU = mybir.AluOpType
AX = mybir.AxisListType

ENGINES = ("sync", "tensor", "vector", "scalar", "gpsimd")
NEG = -1.0e30
EPS = 1.0e-5
D = 1024
DT = 8
NE = 8
DFF0 = 4096
F1 = 1408
F2 = 2816
NH = 16
HD = 64
ACOLS = 72704


class Buf:
    __slots__ = ("name", "w", "readers")

    def __init__(self, name, w=None, readers=None):
        self.name = name
        self.w = w
        self.readers = list(readers or [])


class Prog:
    def __init__(self, nc, es):
        self.nc = nc
        self.es = es
        self.recs = {e: [] for e in ENGINES}
        self.sems = {}
        self.cnt = {}
        self.pending = {e: [] for e in ENGINES}
        self.eng_sem = {e: "S" + e for e in ENGINES}
        for e in ENGINES:
            self._sem(self.eng_sem[e])
        self.DMA_R = 8
        self.dma_rot = {}
        for q in ("dmaq", "dmag", "dmas"):
            self.dma_rot[q] = 0
            for i in range(self.DMA_R):
                self._sem(f"{q}{i}")

    def _sem(self, name):
        if name not in self.sems:
            self.sems[name] = self.es.enter_context(self.nc.semaphore(name))
            self.cnt[name] = 0
        return self.sems[name]

    def emit(self, eng, fn, reads=(), writes=(), war=(), sem="", amt=1,
             last=True, first=True, extra_waits=()):
        waits = set(extra_waits)
        if first:
            for b in reads:
                if b.w:
                    waits.add(b.w)
            for b in list(writes) + list(war):
                if b.w:
                    waits.add(b.w)
                for r in b.readers:
                    waits.add(r)
        else:
            for b in reads:
                if b.w:
                    waits.add(b.w)
        s = self.eng_sem[eng] if sem == "" else sem
        inc = None
        if s is not None:
            self.cnt[s] += amt
            inc = (s, amt)
            tok = (s, self.cnt[s])
            for b in writes:
                if last:
                    b.w = tok
                    b.readers = []
            for b in reads:
                b.readers.append(tok)
            if sem == "":
                for kind, b in self.pending[eng]:
                    if kind == "r":
                        b.readers.append(tok)
                    else:
                        b.w = tok
                        b.readers = []
                self.pending[eng] = []
        else:
            for b in reads:
                self.pending[eng].append(("r", b))
            for b in writes:
                self.pending[eng].append(("w", b))
        self.recs[eng].append((sorted(waits), fn, inc))

    def dma(self, eng, dst, src, reads=(), writes=(), war=()):
        q = {"sync": "dmaq", "gpsimd": "dmag", "scalar": "dmas"}[eng]
        i = self.dma_rot[q]
        self.dma_rot[q] = (i + 1) % self.DMA_R
        s = f"{q}{i}"
        waits = []
        if self.cnt[s] > 0:
            waits.append((s, self.cnt[s]))
        self.emit(eng, lambda e: e.dma_start(dst, src), reads=reads,
                  writes=writes, war=war, sem=s, amt=16, extra_waits=waits)

    def build(self, block):
        known = {e: {} for e in ENGINES}

        def run(e_obj, eng):
            k = known[eng]
            for waits, fn, inc in self.recs[eng]:
                for (s, cnt) in waits:
                    if k.get(s, -1) < cnt:
                        e_obj.wait_ge(self.sems[s], cnt)
                        k[s] = cnt
                ins = fn(e_obj)
                if inc is not None:
                    s, amt = inc
                    ins.then_inc(self.sems[s], amt)

        for eng in ENGINES:
            if self.recs[eng]:
                getattr(block, eng)(lambda e_obj, eng=eng: run(e_obj, eng))

    def barrier_rec(self, eng, sems):
        allsems = []
        for s in sems:
            if s in self.dma_rot:
                allsems += [f"{s}{i}" for i in range(self.DMA_R)]
            else:
                allsems.append(s)
        toks = [(s, self.cnt[s]) for s in allsems if self.cnt[s] > 0]
        self.recs[eng].append((toks, lambda e: None, None))


class Arena:
    def __init__(self, t, ncols):
        self.t = t
        self.free = [(0, ncols, Buf("arena0"))]

    def alloc(self, ncols, name):
        ncols = (ncols + 63) & ~63
        for i, (c0, n, b) in enumerate(self.free):
            if n >= ncols:
                if n == ncols:
                    self.free.pop(i)
                else:
                    self.free[i] = (c0 + ncols, n - ncols,
                                    Buf(b.name, w=b.w, readers=list(b.readers)))
                nb = Buf(name, w=b.w, readers=list(b.readers))
                return Region(self.t, c0, ncols, nb)
        raise MemoryError(f"arena full: {name} needs {ncols}; free={[(a, n) for a, n, _ in self.free]}")

    def freer(self, reg):
        self.free.append((reg.c0, reg.ncols, reg.buf))
        self.free.sort(key=lambda x: x[0])
        merged = []
        for c0, n, b in self.free:
            if merged and merged[-1][0] + merged[-1][1] == c0:
                pc0, pn, pb = merged[-1]
                nb = Buf("m", w=None, readers=[])
                for x in (pb, b):
                    if x.w:
                        nb.readers.append(x.w)
                    nb.readers += x.readers
                merged[-1] = (pc0, pn + n, nb)
            else:
                merged.append((c0, n, b))
        self.free = merged


class Region:
    def __init__(self, t, c0, ncols, buf):
        self.t = t
        self.c0 = c0
        self.ncols = ncols
        self.buf = buf

    def bf(self, p0, p1, a, b):
        return self.t[p0:p1, self.c0 + a:self.c0 + b]

    def f32(self, p0, p1, a, b):
        return self.t[p0:p1, self.c0 + 2 * a:self.c0 + 2 * b].bitcast(F32)


def fold_bufs(reg, bufs):
    for b in bufs:
        if b.w:
            reg.buf.readers.append(b.w)
        reg.buf.readers.extend(b.readers)


class Ctx:
    pass


def build_core_program(T, parts, dbg=None):
    TT = T // 128
    nc = bass.Bass("TRN2", target_bir_lowering=False)
    c = Ctx()
    c.T, c.TT = T, TT

    def dram(name, shape):
        return nc.dram_tensor(name, shape, F32, kind="ExternalInput").ap()

    c.x = dram("x", [T, D])
    c.OUT = nc.dram_tensor("OUT", [T, D], F32, kind="ExternalOutput").ap()
    if "L0" in parts:
        c.Wq, c.Wk, c.Wv = dram("Wq", [D, D]), dram("Wk", [D, D]), dram("Wv", [D, D])
        c.Wr = dram("Wr", [D, NE])
        c.W1 = dram("W1", [NE, DFF0 // 128, 128, D])
        c.W2 = dram("W2", [NE, DFF0 // 128, 128, D])
        c.W3 = dram("W3", [NE, DFF0, D])
        c.w3bf = nc.dram_tensor("w3bf", [NE * (DFF0 // 128), 128, D], BF16).ap()
        c.Bw3bf = [Buf(f"w3bf{i}") for i in range(NE * (DFF0 // 128))]
        c.w1bf = nc.dram_tensor("w1bf", [NE * (DFF0 // 128), 128, D], BF16).ap()
        c.Bw1bf = [Buf(f"w1bf{i}") for i in range(NE * (DFF0 // 128))]
        c.w2bf = nc.dram_tensor("w2bf", [NE * (DFF0 // 128), 128, D], BF16).ap()
        c.Bw2bf = [Buf(f"w2bf{i}") for i in range(NE * (DFF0 // 128))]
    if any(s.startswith("L1") for s in parts):
        c.Wqkv = dram("Wqkv", [D, 3 * D])
        c.Wgate, c.Wproj = dram("Wgate", [D, D]), dram("Wproj", [D, D])
        c.gwT = dram("gwT", [D, NE])
        c.w_gate = dram("w_gate", [NE, F1 // 128, 128, D])
        c.w_up = dram("w_up", [NE, F1 // 128, 128, D])
        c.w_down = dram("w_down", [NE, F1, D])
        c.sw_gate = dram("sw_gate", [F2 // 128, 128, D])
        c.sw_up = dram("sw_up", [F2 // 128, 128, D])
        c.sw_down = dram("sw_down", [F2, D])
        c.swdbf = nc.dram_tensor("swdbf", [F2 // 128, 128, D], BF16).ap()
        c.Bswdbf = [Buf(f"swdbf{i}") for i in range(F2 // 128)]
        c.gate_hbm = nc.dram_tensor("gate_hbm", [T, D], BF16).ap()
        c.xnorm_hbm = nc.dram_tensor("xnorm_hbm", [T, D], BF16).ap()

    c.consts = nc.dram_tensor("consts", [128, 3592], F32, kind="ExternalInput").ap()
    c.crow = nc.dram_tensor("crow", [1, 12288], F32, kind="ExternalInput").ap()

    es = ExitStack()
    with es:
        arena_t = es.enter_context(nc.sbuf_tensor("arena", [128, ACOLS], BF16))
        xres_t = es.enter_context(nc.sbuf_tensor("xres", [128, TT * 1024], F32))
        P = [es.enter_context(nc.psum_tensor(f"P{i}", [128, 512], F32)) for i in range(8)]
        p = Prog(nc, es)
        block = es.enter_context(nc.Block())

        c.nc, c.p, c.P = nc, p, [b[:] for b in P]
        c.PB = [b.bitcast(BF16) for b in c.P]
        c.BP = [Buf(f"P{i}") for i in range(8)]
        c.ar = Arena(arena_t, ACOLS)
        c.xres_t = xres_t
        c.Bx = [Buf(f"x{t}") for t in range(TT)]
        c.xres = lambda t: xres_t[:, 1024 * t:1024 * (t + 1)]

        c.ident = c.ar.alloc(128, "ident")
        c.ones = c.ar.alloc(512, "ones")
        c.maskS = c.ar.alloc(256, "maskS")
        c.maskA = c.ar.alloc(128, "maskA")
        c.stats = c.ar.alloc(2 * 8 * TT, "stats")
        c.cw = c.ar.alloc(2 * 8 * TT, "cw")

        setup(c)
        load_x(c)
        if "L0" in parts:
            xnT = ln_phase(c, layer=0)
            l0b = c.ar.alloc(2 * 8, "l0br")
            p.dma("sync", l0b.f32(0, 128, 0, 8), c.consts[:, 0:8], writes=[l0b.buf])
            rtile, rfini = router_prep(c, xnT, c.Wr, with_bias=True, sigma_scale=1.0,
                                       brap=l0b.f32(0, 128, 0, 8), brbuf=l0b.buf,
                                       bank_of=lambda t: 7)
            attn0(c, xnT, qi_hook=rtile)
            rfini()
            c.ar.freer(l0b)
            moe(c, xnT, dff=DFF0, wg=c.W1, wu=c.W2, wd=c.W3,
                router_w=None, sigma_scale=1.0, bias=True, name="f0")
            fold_bufs(xnT, xnT.tbufs)
            c.ar.freer(xnT)
        if any(s.startswith("L1") for s in parts):
            sub = [s[2:] for s in parts if s.startswith("L1")][0] or "ams"
            xnT = ln_phase(c, layer=1)
            if "a" in sub:
                gate_phase(c, xnT)
            if "m" in sub:
                moe(c, xnT, dff=F1, wg=c.w_gate, wu=c.w_up, wd=c.w_down,
                    router_w=c.gwT, sigma_scale=2.5, bias=False, name="f1")
            if "s" in sub:
                swiglu(c, xnT)
            c.ar.freer(xnT)
            if "a" in sub and "g" not in sub:
                scan_phase(c)
        store_x(c)
        p.barrier_rec("sync", ["dmaq", "dmag", "dmas", "Svector", "Sscalar",
                               "Stensor", "Sgpsimd"])
        if dbg is not None:
            dbg(c)
        p.build(block)
    return nc


def mm_group(c, bank_ids, mms, tail=None):
    p = c.p
    seq = list(mms) + list(tail or [])
    n = len(seq)
    banks = [c.BP[b] for b in bank_ids]
    for i, (fn, reads) in enumerate(seq):
        p.emit("tensor", fn, reads=reads,
               writes=(banks if i == n - 1 else []),
               war=(banks if i == 0 else ()),
               sem=(None if i < n - 1 else ""),
               first=(i == 0), last=(i == n - 1))


def trans_batch(c, pairs, bank, dst_ap, dst_buf):
    p = c.p
    assert len(pairs) <= 8
    for i, (src, sbuf) in enumerate(pairs):
        dst = c.PB[bank][:, 128 * i:128 * (i + 1)]
        p.emit("tensor",
               lambda t, dst=dst, src=src: t.transpose(dst, src, c.ident.bf(0, 128, 0, 128)),
               reads=[sbuf, c.ident.buf],
               writes=([c.BP[bank]] if i == len(pairs) - 1 else []),
               war=([c.BP[bank]] if i == 0 else ()),
               sem=(None if i < len(pairs) - 1 else ""),
               first=(i == 0), last=(i == len(pairs) - 1))
    n = len(pairs)
    src_ap = c.PB[bank][:, 0:128 * n]
    if dst_ap.ndim == 3:
        src_ap = src_ap.rearrange("p (d c) -> p d c", c=128)
    p.emit("vector", lambda v: v.tensor_copy(dst_ap, src_ap),
           reads=[c.BP[bank]], writes=[dst_buf])


def setup(c):
    p = c.p
    p.emit("gpsimd", lambda g: g.memset(c.ones.bf(0, 128, 0, 512), 1.0),
           writes=[c.ones.buf])
    idb = c.ident
    p.emit("gpsimd", lambda g: g.iota(idb.bf(0, 128, 0, 128), [[1, 128]], base=0,
                                      channel_multiplier=-1,
                                      allow_small_or_imprecise_dtypes=True),
           writes=[idb.buf])
    p.emit("vector", lambda v: v.tensor_scalar(out=idb.bf(0, 128, 0, 128),
                                               in0=idb.bf(0, 128, 0, 128),
                                               scalar1=0.0, scalar2=None,
                                               op0=ALU.is_equal),
           reads=[idb.buf], writes=[idb.buf])
    ms = c.maskS
    p.emit("gpsimd", lambda g: g.iota(ms.f32(0, 128, 0, 128), [[1, 128]], base=0,
                                      channel_multiplier=-1,
                                      allow_small_or_imprecise_dtypes=True),
           writes=[ms.buf])
    p.emit("vector", lambda v: v.tensor_scalar(out=ms.f32(0, 128, 0, 128),
                                               in0=ms.f32(0, 128, 0, 128),
                                               scalar1=0.0, scalar2=NEG,
                                               op0=ALU.is_gt, op1=ALU.mult),
           reads=[ms.buf], writes=[ms.buf])
    ma = c.maskA
    p.emit("gpsimd", lambda g: g.iota(ma.bf(0, 128, 0, 128), [[1, 128]], base=0,
                                      channel_multiplier=-1,
                                      allow_small_or_imprecise_dtypes=True),
           writes=[ma.buf])
    p.emit("vector", lambda v: v.tensor_scalar(out=ma.bf(0, 128, 0, 128),
                                               in0=ma.bf(0, 128, 0, 128),
                                               scalar1=0.0, scalar2=None,
                                               op0=ALU.is_ge),
           reads=[ma.buf], writes=[ma.buf])


def load_x(c):
    for t in range(c.TT):
        c.p.dma("sync", c.xres(t), c.x[128 * t:128 * (t + 1), :], writes=[c.Bx[t]])


def store_x(c):
    for t in range(c.TT):
        c.p.dma("sync", c.OUT[128 * t:128 * (t + 1), :], c.xres(t), reads=[c.Bx[t]])


def stream_wslabs(c, src, n_slabs, cols, name):
    reg = c.ar.alloc(n_slabs * cols, name)
    reg.sbufs = []
    for d in range(n_slabs):
        b = Buf(f"{name}_{d}", w=reg.buf.w, readers=list(reg.buf.readers))
        reg.sbufs.append(b)
        c.p.dma("gpsimd", reg.bf(0, 128, d * cols, (d + 1) * cols),
                src[128 * d:128 * (d + 1), :], writes=[b])
    return reg


def free_wslabs(c, reg):
    fold_bufs(reg, reg.sbufs)
    c.ar.freer(reg)


def xn_tiles(xnT, t0, t1):
    return xnT.tbufs[t0 // 128:(t1 + 127) // 128]


def xnT_dst(c, xnT, t):
    return bass.AP(xnT.t, xnT.c0 + 128 * t,
                   [[xnT.t.shape[1], 128], [c.T, DT], [1, 128]])


def ln_phase(c, layer):
    p, ar, T, TT = c.p, c.ar, c.T, c.TT
    xnT = ar.alloc(DT * T, f"xnT{layer}")
    xnT.tbufs = [Buf(f"xnT{layer}_{t}") for t in range(TT)]
    for b in xnT.tbufs:
        b.w = xnT.buf.w
        b.readers = list(xnT.buf.readers)
    scr = ar.alloc((6 * 1024 if layer == 1 else 3 * 1024 + 128), f"lnscr{layer}")
    Bscr = scr.buf
    st = c.stats
    if layer == 1:
        gb = ar.alloc(2 * 2048, "lngb")
        p.dma("sync", gb.f32(0, 128, 0, 2048), c.consts[:, 520:2568], writes=[gb.buf])
        xnrm = ar.alloc(2 * 1024, "xnrmtile")
        xnrmb = [Buf("xr0"), Buf("xr1")]
        c.Bxn = [Buf(f"xn{t}") for t in range(TT)]
    for t in range(TT):
        bn6 = scr.f32(0, 128, 1536, 1548)
        mv = scr.f32(0, 128, 1548, 1550)
        for hb in range(2):
            p.emit("vector", lambda v, t=t, bn6=bn6, hb=hb: v.bn_stats(
                bn6[:, 6 * hb:6 * (hb + 1)], c.xres(t)[:, 512 * hb:512 * (hb + 1)]),
                   reads=[c.Bx[t]], writes=[Bscr])
        p.emit("vector", lambda v, bn6=bn6, mv=mv: v.bn_aggr(mv, bn6),
               reads=[Bscr], writes=[Bscr])
        m = st.f32(0, 128, t, t + 1)
        r = st.f32(0, 128, TT + t, TT + t + 1)
        sd = st.f32(0, 128, 4 * TT, 4 * TT + 1)
        p.emit("vector", lambda v, m=m, mv=mv: v.tensor_copy(m, mv[:, 0:1]),
               reads=[Bscr], writes=[st.buf])
        p.emit("scalar", lambda s, sd=sd, mv=mv: s.activation(sd, mv[:, 1:2], AF.Sqrt,
                                                              bias=np.float32(EPS), scale=1.0),
               reads=[Bscr], writes=[st.buf])
        p.emit("vector", lambda v, r=r, sd=sd: v.reciprocal(r, sd),
               reads=[st.buf], writes=[st.buf])
        xn = scr.bf(0, 128, 0, 1024)
        if layer == 0:
            p.emit("vector", lambda v, t=t, xn=xn, m=m, r=r: v.tensor_scalar(
                out=xn, in0=c.xres(t), scalar1=m, scalar2=r,
                op0=ALU.subtract, op1=ALU.mult),
                   reads=[c.Bx[t], st.buf], writes=[Bscr])
        else:
            xf = scr.f32(0, 128, 512, 1536)
            p.emit("vector", lambda v, t=t, xf=xf, m=m, r=r: v.tensor_scalar(
                out=xf, in0=c.xres(t), scalar1=m, scalar2=r,
                op0=ALU.subtract, op1=ALU.mult),
                   reads=[c.Bx[t], st.buf], writes=[Bscr])
            for hb in range(2):
                p.emit("vector", lambda v, bn6=bn6, xf=xf, hb=hb: v.bn_stats(
                    bn6[:, 6 * hb:6 * (hb + 1)], xf[:, 512 * hb:512 * (hb + 1)]),
                       reads=[Bscr], writes=[Bscr])
            p.emit("vector", lambda v, bn6=bn6, mv=mv: v.bn_aggr(mv, bn6),
                   reads=[Bscr], writes=[Bscr])
            sd2 = st.f32(0, 128, 4 * TT + 1, 4 * TT + 2)
            r2 = st.f32(0, 128, 4 * TT + 2, 4 * TT + 3)
            p.emit("scalar", lambda s, sd2=sd2, mv=mv: s.activation(
                sd2, mv[:, 1:2], AF.Sqrt, bias=np.float32(EPS), scale=1.0),
                   reads=[Bscr], writes=[st.buf])
            p.emit("vector", lambda v, r2=r2, sd2=sd2: v.reciprocal(r2, sd2),
                   reads=[st.buf], writes=[st.buf])
            xm = scr.f32(0, 128, 1550, 2574)
            p.emit("vector", lambda v, xm=xm, xf=xf, mv=mv, r2=r2: v.tensor_scalar(
                out=xm, in0=xf, scalar1=mv[:, 0:1], scalar2=r2,
                op0=ALU.subtract, op1=ALU.mult),
                   reads=[Bscr, st.buf], writes=[Bscr])
            p.emit("vector", lambda v, xm=xm, gb=gb: v.tensor_mul(
                xm, xm, gb.f32(0, 128, 0, 1024)),
                   reads=[Bscr, gb.buf], writes=[Bscr])
            gi = t % 2
            p.emit("vector", lambda v, xm=xm, gb=gb, gi=gi: v.tensor_add(
                xnrm.bf(0, 128, 1024 * gi, 1024 * (gi + 1)), xm,
                gb.f32(0, 128, 1024, 2048)),
                   reads=[Bscr, gb.buf], writes=[xnrmb[gi]])
            p.dma("sync", c.xnorm_hbm[128 * t:128 * (t + 1), :],
                  xnrm.bf(0, 128, 1024 * gi, 1024 * (gi + 1)),
                  reads=[xnrmb[gi]], writes=[c.Bxn[t]])
            p.emit("vector", lambda v, xn=xn, xf=xf: v.tensor_copy(xn, xf),
                   reads=[Bscr], writes=[Bscr])
        trans_batch(c, [(xn[:, 128 * d:128 * (d + 1)], Bscr) for d in range(DT)],
                    7, xnT_dst(c, xnT, t), xnT.tbufs[t])
    if layer == 1:
        fold_bufs(xnrm, xnrmb)
        c.ar.freer(xnrm)
        c.ar.freer(gb)
    c.ar.freer(scr)
    return xnT


def attn0(c, xnT, qi_hook=None):
    p, ar, T, TT = c.p, c.ar, c.T, c.TT
    kT = ar.alloc(DT * T, "kT")
    vtok = ar.alloc(TT * 1024, "vtok")
    nchunk = (T + 511) // 512
    wk = stream_wslabs(c, c.Wk, DT, 1024, "wk")
    for do in range(DT):
        for ch in range(nchunk):
            t0, t1 = 512 * ch, min(512 * (ch + 1), T)
            bank = (do * nchunk + ch) % 4
            mm_group(c, [bank],
                     [(lambda t, do=do, d=d, t0=t0, t1=t1, bank=bank: t.matmul(
                         c.P[bank][:, 0:t1 - t0],
                         wk.bf(0, 128, 1024 * d + 128 * do, 1024 * d + 128 * (do + 1)),
                         xnT.bf(0, 128, T * d + t0, T * d + t1),
                         start=(d == 0), stop=(d == DT - 1)),
                       [wk.sbufs[d]] + xn_tiles(xnT, t0, t1))
                      for d in range(DT)])
            p.emit("scalar", lambda s, do=do, t0=t0, t1=t1, bank=bank: s.activation(
                kT.bf(0, 128, T * do + t0, T * do + t1), c.P[bank][:, 0:t1 - t0],
                AF.Copy, bias=0.0, scale=1.0),
                   reads=[c.BP[bank]], writes=[kT.buf])
    free_wslabs(c, wk)
    wv = stream_wslabs(c, c.Wv, DT, 1024, "wv")
    for t in range(TT):
        b0 = 4 + (t % 2) * 2
        mm_group(c, [b0, b0 + 1],
                 [(lambda tt, t=t, d=d, h=h, b0=b0: tt.matmul(
                     c.P[b0 + h][:, 0:512],
                     xnT.bf(0, 128, T * d + 128 * t, T * d + 128 * (t + 1)),
                     wv.bf(0, 128, 1024 * d + 512 * h, 1024 * d + 512 * (h + 1)),
                     start=(d == 0), stop=(d == DT - 1)), [xnT.tbufs[t], wv.sbufs[d]])
                  for d in range(DT) for h in range(2)])
        for h in range(2):
            p.emit("scalar", lambda s, t=t, h=h, b0=b0: s.activation(
                vtok.bf(0, 128, 1024 * t + 512 * h, 1024 * t + 512 * (h + 1)),
                c.P[b0 + h][:, 0:512], AF.Copy, bias=0.0, scale=1.0),
                   reads=[c.BP[b0 + h]], writes=[vtok.buf])
    free_wslabs(c, wv)
    wq = stream_wslabs(c, c.Wq, DT, 1024, "wq")
    qT = ar.alloc(DT * 128, "qTqi")
    SstA = [ar.alloc(2 * T, "Sstrip0"), ar.alloc(2 * T, "Sstrip1")]
    Pst = ar.alloc(T, "Pstrip")
    PTst = ar.alloc(T, "PTstrip")
    smA = [ar.alloc(64, "softsmall0"), ar.alloc(64, "softsmall1")]
    for qi in range(TT):
        Sst = SstA[qi % 2]
        sm = smA[qi % 2]
        for do in range(DT):
            bank = do % 2
            mm_group(c, [bank],
                     [(lambda t, do=do, d=d, qi=qi, bank=bank: t.matmul(
                         c.P[bank][:, 0:128],
                         wq.bf(0, 128, 1024 * d + 128 * do, 1024 * d + 128 * (do + 1)),
                         xnT.bf(0, 128, T * d + 128 * qi, T * d + 128 * (qi + 1)),
                         start=(d == 0), stop=(d == DT - 1)), [wq.sbufs[d], xnT.tbufs[qi]])
                      for d in range(DT)])
            p.emit("scalar", lambda s, do=do, bank=bank: s.activation(
                qT.bf(0, 128, 128 * do, 128 * (do + 1)), c.P[bank][:, 0:128],
                AF.Copy, bias=0.0, scale=1.0),
                   reads=[c.BP[bank]], writes=[qT.buf])
        w = 128 * (qi + 1)
        chunks = []
        s0 = 0
        while s0 < 128 * qi:
            s1 = min(s0 + 512, 128 * qi)
            chunks.append((s0, s1, False))
            s0 = s1
        chunks.append((128 * qi, w, True))
        for ci, (s0, s1, diag) in enumerate(chunks):
            bank = 2 + ci % 2
            mm_group(c, [bank],
                     [(lambda t, d=d, s0=s0, s1=s1, bank=bank: t.matmul(
                         c.P[bank][:, 0:s1 - s0],
                         qT.bf(0, 128, 128 * d, 128 * (d + 1)),
                         kT.bf(0, 128, T * d + s0, T * d + s1),
                         start=(d == 0), stop=(d == DT - 1)), [qT.buf, kT.buf])
                      for d in range(DT)])
            if not diag:
                p.emit("scalar", lambda s, s0=s0, s1=s1, bank=bank: s.activation(
                    Sst.f32(0, 128, s0, s1), c.P[bank][:, 0:s1 - s0],
                    AF.Copy, bias=0.0, scale=np.float32(0.03125)),
                       reads=[c.BP[bank]], writes=[Sst.buf])
            else:
                p.emit("vector", lambda v, s0=s0, s1=s1, bank=bank: v.scalar_tensor_tensor(
                    out=Sst.f32(0, 128, s0, s1), in0=c.P[bank][:, 0:s1 - s0],
                    scalar=0.03125, in1=c.maskS.f32(0, 128, 0, s1 - s0),
                    op0=ALU.mult, op1=ALU.add),
                       reads=[c.BP[bank], c.maskS.buf], writes=[Sst.buf])
        negmax = sm.f32(0, 128, 0, 1)
        rowsum = sm.f32(0, 128, 1, 2)
        recip = sm.f32(0, 128, 2, 3)
        p.emit("vector", lambda v, w=w, negmax=negmax: v.tensor_reduce(
            negmax, Sst.f32(0, 128, 0, w), axis=AX.X, op=ALU.max, negate=True),
               reads=[Sst.buf], writes=[sm.buf])
        p.emit("scalar", lambda s, w=w, negmax=negmax, rowsum=rowsum: s.activation(
            Pst.bf(0, 128, 0, w), Sst.f32(0, 128, 0, w), AF.Exp,
            bias=negmax, scale=1.0, accum_out=rowsum),
               reads=[Sst.buf, sm.buf], writes=[Pst.buf, sm.buf])
        p.emit("vector", lambda v, recip=recip, rowsum=rowsum: v.reciprocal(recip, rowsum),
               reads=[sm.buf], writes=[sm.buf])
        for g0 in range(0, qi + 1, 8):
            g1 = min(g0 + 8, qi + 1)
            dst = bass.AP(PTst.t, PTst.c0 + 128 * g0,
                          [[PTst.t.shape[1], 128], [1, 128 * (g1 - g0)]])
            trans_batch(c, [(Pst.bf(0, 128, 128 * s, 128 * (s + 1)), Pst.buf)
                            for s in range(g0, g1)], 6, dst, PTst.buf)
        for h in range(2):
            bank = 4 + h
            mm_group(c, [bank],
                     [(lambda t, s=s, h=h, qi=qi, bank=bank: t.matmul(
                         c.P[bank][:, 0:512],
                         PTst.bf(0, 128, 128 * s, 128 * (s + 1)),
                         vtok.bf(0, 128, 1024 * s + 512 * h, 1024 * s + 512 * (h + 1)),
                         start=(s == 0), stop=(s == qi)), [PTst.buf, vtok.buf])
                      for s in range(qi + 1)])
            p.emit("vector", lambda v, qi=qi, h=h, recip=recip, bank=bank:
                   v.scalar_tensor_tensor(
                out=c.xres(qi)[:, 512 * h:512 * (h + 1)], in0=c.P[bank][:, 0:512],
                scalar=recip, in1=c.xres(qi)[:, 512 * h:512 * (h + 1)],
                op0=ALU.mult, op1=ALU.add),
                   reads=[c.BP[bank], sm.buf, c.Bx[qi]], writes=[c.Bx[qi]])
        if qi_hook is not None:
            qi_hook(qi)
    free_wslabs(c, wq)
    for r in (qT, SstA[0], SstA[1], Pst, PTst, smA[0], smA[1], kT, vtok):
        c.ar.freer(r)


def router_prep(c, xnT, router_w, with_bias, sigma_scale, brap=None, brbuf=None,
                bank_of=None):
    p, ar, T, TT = c.p, c.ar, c.T, c.TT
    rw = stream_wslabs(c, router_w, DT, NE, "rw")
    scr = ar.alloc(128, "rscr")
    B = scr.buf

    def tile(t):
        bank = (t % 4) if bank_of is None else bank_of(t)
        mm_group(c, [bank],
                 [(lambda tt, t=t, d=d, bank=bank: tt.matmul(
                     c.P[bank][:, 0:NE],
                     xnT.bf(0, 128, T * d + 128 * t, T * d + 128 * (t + 1)),
                     rw.bf(0, 128, NE * d, NE * (d + 1)),
                     start=(d == 0), stop=(d == DT - 1)), [xnT.tbufs[t], rw.sbufs[d]])
                  for d in range(DT)])
        lg = scr.f32(0, 128, 0, NE)
        if with_bias:
            p.emit("vector", lambda v, lg=lg, bank=bank: v.tensor_add(
                lg, c.P[bank][:, 0:NE], brap),
                   reads=[c.BP[bank], brbuf], writes=[B])
        else:
            p.emit("vector", lambda v, lg=lg, bank=bank: v.tensor_copy(
                lg, c.P[bank][:, 0:NE]),
                   reads=[c.BP[bank]], writes=[B])
        m1 = scr.f32(0, 128, NE, NE + 1)
        m2 = scr.f32(0, 128, NE + 1, NE + 2)
        w1 = scr.f32(0, 128, NE + 2, NE + 3)
        w2 = scr.f32(0, 128, NE + 3, NE + 4)
        d12 = scr.f32(0, 128, NE + 4, NE + 5)
        eq1 = scr.f32(0, 128, 16, 24)
        eq2 = scr.f32(0, 128, 24, 32)
        l2 = scr.f32(0, 128, 32, 40)
        p.emit("vector", lambda v, m1=m1, lg=lg: v.tensor_reduce(
            m1, lg, axis=AX.X, op=ALU.max), reads=[B], writes=[B])
        p.emit("vector", lambda v, eq1=eq1, lg=lg, m1=m1: v.tensor_scalar(
            out=eq1, in0=lg, scalar1=m1, scalar2=None, op0=ALU.is_equal),
               reads=[B], writes=[B])
        p.emit("vector", lambda v, l2=l2, eq1=eq1, lg=lg: v.scalar_tensor_tensor(
            out=l2, in0=eq1, scalar=NEG, in1=lg, op0=ALU.mult, op1=ALU.add),
               reads=[B], writes=[B])
        p.emit("vector", lambda v, m2=m2, l2=l2: v.tensor_reduce(
            m2, l2, axis=AX.X, op=ALU.max), reads=[B], writes=[B])
        p.emit("vector", lambda v, eq2=eq2, l2=l2, m2=m2: v.tensor_scalar(
            out=eq2, in0=l2, scalar1=m2, scalar2=None, op0=ALU.is_equal),
               reads=[B], writes=[B])
        p.emit("vector", lambda v, d12=d12, m2=m2, m1=m1: v.tensor_scalar(
            out=d12, in0=m2, scalar1=m1, scalar2=None, op0=ALU.subtract),
               reads=[B], writes=[B])
        p.emit("scalar", lambda s, d12=d12: s.activation(d12, d12, AF.Exp,
                                                         bias=0.0, scale=1.0),
               reads=[B], writes=[B])
        p.emit("vector", lambda v, d12=d12: v.tensor_scalar(
            out=d12, in0=d12, scalar1=1.0, scalar2=None, op0=ALU.add),
               reads=[B], writes=[B])
        p.emit("vector", lambda v, w1=w1, d12=d12: v.reciprocal(w1, d12),
               reads=[B], writes=[B])
        p.emit("vector", lambda v, w1=w1: v.tensor_scalar(
            out=w1, in0=w1, scalar1=sigma_scale, scalar2=None, op0=ALU.mult),
               reads=[B], writes=[B])
        p.emit("vector", lambda v, w2=w2, w1=w1: v.tensor_scalar(
            out=w2, in0=w1, scalar1=-1.0, scalar2=sigma_scale,
            op0=ALU.mult, op1=ALU.add), reads=[B], writes=[B])
        cwc = c.cw.f32(0, 128, NE * t, NE * (t + 1))
        p.emit("vector", lambda v, eq2=eq2, w2=w2: v.tensor_scalar(
            out=eq2, in0=eq2, scalar1=w2, scalar2=None, op0=ALU.mult),
               reads=[B], writes=[B])
        p.emit("vector", lambda v, cwc=cwc, eq1=eq1, w1=w1, eq2=eq2:
               v.scalar_tensor_tensor(
            out=cwc, in0=eq1, scalar=w1, in1=eq2, op0=ALU.mult, op1=ALU.add),
               reads=[B], writes=[c.cw.buf])

    def fini():
        free_wslabs(c, rw)
        c.ar.freer(scr)
    return tile, fini


def router_phase(c, xnT, router_w, with_bias, sigma_scale, brap=None, brbuf=None):
    tile, fini = router_prep(c, xnT, router_w, with_bias, sigma_scale, brap, brbuf)
    for t in range(c.TT):
        tile(t)
    fini()


def moe(c, xnT, dff, wg, wu, wd, router_w, sigma_scale, bias, name):
    p, ar, T, TT = c.p, c.ar, c.T, c.TT
    l0c = None
    if bias:
        l0c = ar.alloc(2 * 520, name + "_bias")
        p.dma("sync", l0c.f32(0, 128, 0, 8), c.consts[:, 0:8], writes=[l0c.buf])
        p.dma("sync", l0c.f32(0, 128, 8, 520), c.consts[:, 8:520],
              writes=[l0c.buf], war=[l0c.buf])
        b3row = ar.alloc(1024, name + "_b3")
    if router_w is not None:
        router_phase(c, xnT, router_w, with_bias=bias, sigma_scale=sigma_scale,
                     brap=(l0c.f32(0, 128, 0, 8) if bias else None),
                     brbuf=(l0c.buf if bias else None))
    hscr = ar.alloc(2 * 512, name + "_hscr")
    nj = dff // 128
    CH = min(1024, T)
    nch = T // CH
    h = ar.alloc(nj * CH, name + "_h")
    wbuf = [ar.alloc(1024, f"{name}_w{i}") for i in range(8)]
    w3buf = [ar.alloc(1024, f"{name}_w3{i}") for i in range(8)]
    wd_res = None
    if dff == F1:
        wd_res = ar.alloc(nj * 1024, name + "_wdres")
    nsub = max(CH // 512, 1)
    sub = CH // nsub
    for e in range(NE):
        if bias:
            p.dma("gpsimd", b3row.bf(0, 1, 0, 1024),
                  c.crow[0:1, 1024 * e:1024 * (e + 1)],
                  writes=[b3row.buf], war=[b3row.buf])
        for ch in range(nch):
            t0 = CH * ch
            for j in range(nj):
                if wd_res is not None and ch == 0 and j == 2:
                    for k in range(nj):
                        wsrc = bass.AP(wd.tensor, wd.offset + e * dff * D + 128 * k * D,
                                       [[D, 128], [1, D]])
                        c.p.dma("gpsimd", wd_res.bf(0, 128, 1024 * k, 1024 * (k + 1)),
                                wsrc, writes=[wd_res.buf], war=[wd_res.buf])
                wb1 = wbuf[(j % 4) * 2]
                wb2 = wbuf[(j % 4) * 2 + 1]
                src1 = bass.AP(wg.tensor, wg.offset + (e * nj + j) * 128 * D,
                               [[D, 128], [1, D]])
                src2 = bass.AP(wu.tensor, wu.offset + (e * nj + j) * 128 * D,
                               [[D, 128], [1, D]])
                if nch > 1 and dff == DFF0:
                    if ch == 0:
                        p.dma("gpsimd", wb1.bf(0, 128, 0, 1024), src1,
                              writes=[wb1.buf], war=[wb1.buf])
                        p.dma("sync", c.w1bf[e * nj + j], wb1.bf(0, 128, 0, 1024),
                              reads=[wb1.buf], writes=[c.Bw1bf[e * nj + j]])
                        p.dma("gpsimd", wb2.bf(0, 128, 0, 1024), src2,
                              writes=[wb2.buf], war=[wb2.buf])
                        p.dma("sync", c.w2bf[e * nj + j], wb2.bf(0, 128, 0, 1024),
                              reads=[wb2.buf], writes=[c.Bw2bf[e * nj + j]])
                    else:
                        p.dma("gpsimd", wb1.bf(0, 128, 0, 1024), c.w1bf[e * nj + j],
                              reads=[c.Bw1bf[e * nj + j]], writes=[wb1.buf], war=[wb1.buf])
                        p.dma("scalar", wb2.bf(0, 128, 0, 1024), c.w2bf[e * nj + j],
                              reads=[c.Bw2bf[e * nj + j]], writes=[wb2.buf], war=[wb2.buf])
                else:
                    p.dma("gpsimd", wb1.bf(0, 128, 0, 1024), src1,
                          writes=[wb1.buf], war=[wb1.buf])
                    p.dma("gpsimd", wb2.bf(0, 128, 0, 1024), src2,
                          writes=[wb2.buf], war=[wb2.buf])
                bk = 0 if (j % 2 == 0) else 4
                for mat, wb in ((0, wb1), (1, wb2)):
                    for si in range(nsub):
                        bank = bk + mat * nsub + si
                        mm_group(c, [bank],
                                 [(lambda t, d=d, t0=t0, si=si, wb=wb, bank=bank: t.matmul(
                                     c.P[bank][:, 0:sub],
                                     wb.bf(0, 128, 128 * d, 128 * (d + 1)),
                                     xnT.bf(0, 128, T * d + t0 + sub * si,
                                            T * d + t0 + sub * (si + 1)),
                                     start=(d == 0), stop=(d == DT - 1)),
                                   [wb.buf] + xn_tiles(xnT, t0 + sub * si,
                                                       t0 + sub * (si + 1)))
                                  for d in range(DT)])
                for si in range(nsub):
                    b1k = bk + si
                    b2k = bk + nsub + si
                    if bias:
                        b1ap = l0c.f32(0, 128, 8 + 32 * e + j, 9 + 32 * e + j)
                        b2ap = l0c.f32(0, 128, 264 + 32 * e + j, 265 + 32 * e + j)
                        p.emit("scalar", lambda s, b1k=b1k, b1ap=b1ap:
                               s.activation(hscr.f32(0, 128, 0, sub),
                                            c.P[b1k][:, 0:sub], AF.Silu,
                                            bias=b1ap, scale=1.0),
                               reads=[c.BP[b1k], l0c.buf], writes=[hscr.buf])
                        p.emit("vector", lambda v, j=j, si=si, b2k=b2k, b2ap=b2ap:
                               v.scalar_tensor_tensor(
                                   out=h.bf(0, 128, CH * j + sub * si,
                                            CH * j + sub * (si + 1)),
                                   in0=c.P[b2k][:, 0:sub], scalar=b2ap,
                                   in1=hscr.f32(0, 128, 0, sub),
                                   op0=ALU.add, op1=ALU.mult),
                               reads=[c.BP[b2k], hscr.buf, l0c.buf], writes=[h.buf])
                    else:
                        p.emit("scalar", lambda s, b1k=b1k: s.activation(
                            hscr.f32(0, 128, 0, sub), c.P[b1k][:, 0:sub],
                            AF.Silu, bias=0.0, scale=1.0),
                               reads=[c.BP[b1k]], writes=[hscr.buf])
                        p.emit("vector", lambda v, j=j, si=si, b2k=b2k: v.tensor_mul(
                            h.bf(0, 128, CH * j + sub * si, CH * j + sub * (si + 1)),
                            hscr.f32(0, 128, 0, sub), c.P[b2k][:, 0:sub]),
                               reads=[c.BP[b2k], hscr.buf], writes=[h.buf])
            PTOK = min(512, CH)
            npass = CH // PTOK
            ntt = PTOK // 128
            for ps in range(npass):
                pt0 = t0 + PTOK * ps
                mms = {b: [] for b in range(2 * ntt)}
                first_use = (ch == 0 and ps == 0)
                for k in range(nj):
                    if wd_res is None:
                        w3b = w3buf[k % 8]
                        cb = c.Bw3bf[e * nj + k]
                        if first_use:
                            p.dma("gpsimd", w3b.bf(0, 128, 0, 1024),
                                  bass.AP(wd.tensor, wd.offset + e * dff * D + 128 * k * D,
                                          [[D, 128], [1, D]]),
                                  writes=[w3b.buf], war=[w3b.buf])
                            p.dma("sync", c.w3bf[e * nj + k], w3b.bf(0, 128, 0, 1024),
                                  reads=[w3b.buf], writes=[cb])
                        else:
                            p.dma("scalar", w3b.bf(0, 128, 0, 1024),
                                  c.w3bf[e * nj + k],
                                  reads=[cb], writes=[w3b.buf], war=[w3b.buf])
                    else:
                        w3b = None
                    for tt4 in range(ntt):
                        for hf in range(2):
                            bank = tt4 * 2 + hf
                            if w3b is not None:
                                wap = w3b.bf(0, 128, 512 * hf, 512 * (hf + 1))
                                wbb = w3b.buf
                            else:
                                wap = wd_res.bf(0, 128, 1024 * k + 512 * hf,
                                                1024 * k + 512 * (hf + 1))
                                wbb = wd_res.buf
                            last_k = (k == nj - 1) and not bias
                            k_last = (tt4 == ntt - 1 and hf == 1)
                            p.emit("tensor", lambda t, k=k, tt4=tt4, wap=wap, bank=bank,
                                   pt0=pt0, t0=t0, last_k=last_k: t.matmul(
                                c.P[bank][:, 0:512],
                                h.bf(0, 128, CH * k + (pt0 - t0) + 128 * tt4,
                                     CH * k + (pt0 - t0) + 128 * (tt4 + 1)),
                                wap, start=(k == 0), stop=last_k),
                                   reads=[h.buf, wbb],
                                   writes=([c.BP[bank]] if last_k else []),
                                   war=([c.BP[bank]] if k == 0 else ()),
                                   sem=("" if (last_k or k_last) else None),
                                   first=(k == 0), last=last_k)
                if bias:
                    for tt4 in range(ntt):
                        for hf in range(2):
                            bank = tt4 * 2 + hf
                            p.emit("tensor", lambda t, hf=hf, bank=bank: t.matmul(
                                c.P[bank][:, 0:512],
                                c.ones.bf(0, 1, 0, 128),
                                b3row.bf(0, 1, 512 * hf, 512 * (hf + 1)),
                                start=False, stop=True),
                                   reads=[c.ones.buf, b3row.buf],
                                   writes=[c.BP[bank]], first=False, last=True)
                for tt4 in range(ntt):
                    tglob = pt0 // 128 + tt4
                    cwap = c.cw.f32(0, 128, NE * tglob + e, NE * tglob + e + 1)
                    for hf in range(2):
                        bank = tt4 * 2 + hf
                        p.emit("vector", lambda v, tglob=tglob, hf=hf, bank=bank,
                               cwap=cwap: v.scalar_tensor_tensor(
                            out=c.xres(tglob)[:, 512 * hf:512 * (hf + 1)],
                            in0=c.P[bank][:, 0:512], scalar=cwap,
                            in1=c.xres(tglob)[:, 512 * hf:512 * (hf + 1)],
                            op0=ALU.mult, op1=ALU.add),
                               reads=[c.BP[bank], c.cw.buf, c.Bx[tglob]],
                               writes=[c.Bx[tglob]])
    for r in [h, hscr] + wbuf + w3buf:
        c.ar.freer(r)
    if wd_res is not None:
        c.ar.freer(wd_res)
    if bias:
        c.ar.freer(l0c)
        c.ar.freer(b3row)


def gate_phase(c, xnT):
    p, ar, T, TT = c.p, c.ar, c.T, c.TT
    c.Bgh = [Buf(f"gh{t}") for t in range(TT)]
    wgt = stream_wslabs(c, c.Wgate, DT, 1024, "wgate")
    bg = ar.alloc(2 * 1024, "bgate")
    p.dma("sync", bg.f32(0, 128, 0, 1024), c.consts[:, 2568:3592], writes=[bg.buf])
    scr = ar.alloc(2 * 1032, "gatescr")
    gt = ar.alloc(2 * 1024, "gtile")
    gtb = [Buf("gt0"), Buf("gt1")]
    for t in range(TT):
        b0 = (t % 2) * 2
        mm_group(c, [b0, b0 + 1],
                 [(lambda tt, t=t, d=d, h=h, b0=b0: tt.matmul(
                     c.P[b0 + h][:, 0:512],
                     xnT.bf(0, 128, T * d + 128 * t, T * d + 128 * (t + 1)),
                     wgt.bf(0, 128, 1024 * d + 512 * h, 1024 * d + 512 * (h + 1)),
                     start=(d == 0), stop=(d == DT - 1)), [xnT.tbufs[t], wgt.sbufs[d]])
                  for d in range(DT) for h in range(2)])
        g0 = scr.f32(0, 128, 0, 1024)
        for h in range(2):
            p.emit("vector", lambda v, h=h, b0=b0, g0=g0: v.tensor_add(
                g0[:, 512 * h:512 * (h + 1)], c.P[b0 + h][:, 0:512],
                bg.f32(0, 128, 512 * h, 512 * (h + 1))),
                   reads=[c.BP[b0 + h], bg.buf], writes=[scr.buf])
        p.emit("scalar", lambda s, g0=g0: s.activation(g0, g0, AF.Sigmoid,
                                                       bias=0.0, scale=1.0),
               reads=[scr.buf], writes=[scr.buf])
        rs = scr.f32(0, 128, 1024, 1025)
        p.emit("vector", lambda v, rs=rs, g0=g0: v.tensor_reduce(
            rs, g0, axis=AX.X, op=ALU.add), reads=[scr.buf], writes=[scr.buf])
        p.emit("vector", lambda v, rs=rs: v.tensor_scalar(
            out=rs, in0=rs, scalar1=1.0 / D, scalar2=EPS, op0=ALU.mult, op1=ALU.add),
               reads=[scr.buf], writes=[scr.buf])
        p.emit("vector", lambda v, rs=rs: v.reciprocal(rs, rs),
               reads=[scr.buf], writes=[scr.buf])
        gi = t % 2
        p.emit("vector", lambda v, gi=gi, g0=g0, rs=rs: v.tensor_scalar(
            out=gt.bf(0, 128, 1024 * gi, 1024 * (gi + 1)), in0=g0, scalar1=rs,
            scalar2=None, op0=ALU.mult),
               reads=[scr.buf], writes=[gtb[gi]])
        p.dma("sync", c.gate_hbm[128 * t:128 * (t + 1), :],
              gt.bf(0, 128, 1024 * gi, 1024 * (gi + 1)), reads=[gtb[gi]],
              writes=[c.Bgh[t]])
    fold_bufs(gt, gtb)
    free_wslabs(c, wgt)
    for r in (bg, scr, gt):
        c.ar.freer(r)


def scan_phase(c):
    p, ar, T, TT = c.p, c.ar, c.T, c.TT
    wqkv = stream_wslabs(c, c.Wqkv, DT, 3072, "wqkv")
    wproj = stream_wslabs(c, c.Wproj, DT, 1024, "wproj")
    brow = ar.alloc(4096, "brow")
    p.dma("gpsimd", brow.bf(0, 1, 0, 4096), c.crow[0:1, 8192:12288], writes=[brow.buf])
    kvacc = ar.alloc(2 * 1024, "kvacc")
    ksacc = ar.alloc(2 * 16, "ksacc")
    kvblk = ar.alloc(1024, "kvblk")
    ksblk = ar.alloc(64, "ksblk")
    for reg, n, f in ((kvacc, 1024, 1), (ksacc, 16, 1)):
        p.emit("gpsimd", lambda g, reg=reg, n=n: g.memset(reg.f32(0, 64, 0, n), 0.0),
               writes=[reg.buf])
    p.emit("gpsimd", lambda g: g.memset(kvblk.bf(0, 128, 0, 1024), 0.0),
           writes=[kvblk.buf])
    p.emit("gpsimd", lambda g: g.memset(ksblk.bf(0, 128, 0, 64), 0.0),
           writes=[ksblk.buf])
    scr = ar.alloc(2 * 2048, "sscr")
    q2A = [ar.alloc(1024, f"q2tok{i}") for i in range(2)]
    k2A = [ar.alloc(1024, f"k2tok{i}") for i in range(2)]
    vvA = [ar.alloc(1024, f"vchunk{i}") for i in range(2)]
    q2TA = [ar.alloc(1024, f"q2T{i}") for i in range(2)]
    k2TA = [ar.alloc(1024, f"k2T{i}") for i in range(2)]
    atA = [ar.alloc(16 * 128, f"at{i}") for i in range(2)]
    outcA = [ar.alloc(1024, f"outc{i}") for i in range(2)]
    outTA = [ar.alloc(1024, f"outT{i}") for i in range(2)]
    dscA = [ar.alloc(64, f"densc{i}") for i in range(2)]
    xncT = ar.alloc(2048, "xncT")
    gtc = ar.alloc(2 * 1024, "gchunk")
    gtcb = [Buf("gc0"), Buf("gc1")]
    xnc2 = ar.alloc(2 * 1024, "xnchunk")
    xncb = [Buf("xc0"), Buf("xc1")]
    nch_lim = int(os.environ.get("KN_CH", "1000000"))
    xncTb = [Buf("xncT0"), Buf("xncT1")]

    def prefetch_chunk(ch):
        gi = ch % 2
        p.dma("sync", gtc.bf(0, 128, 1024 * gi, 1024 * (gi + 1)),
              c.gate_hbm[128 * ch:128 * (ch + 1), :], reads=[c.Bgh[ch]],
              writes=[gtcb[gi]], war=[gtcb[gi]])
        p.dma("sync", xnc2.bf(0, 128, 1024 * gi, 1024 * (gi + 1)),
              c.xnorm_hbm[128 * ch:128 * (ch + 1), :], reads=[c.Bxn[ch]],
              writes=[xncb[gi]], war=[xncb[gi]])
        trans_batch(c, [(xnc2.bf(0, 128, 1024 * gi + 128 * d, 1024 * gi + 128 * (d + 1)),
                         xncb[gi]) for d in range(DT)], 6 + gi,
                    xncT.bf(0, 128, 1024 * gi, 1024 * (gi + 1)), xncTb[gi])

    if min(TT, nch_lim) > 0:
        prefetch_chunk(0)
    for ch in range(min(TT, nch_lim)):
        gi = ch % 2
        B0 = 3 * gi
        TB = 6 + gi
        q2, k2, vv = q2A[gi], k2A[gi], vvA[gi]
        q2T, k2T, at, outc, outT, dsc = (q2TA[gi], k2TA[gi], atA[gi], outcA[gi],
                                         outTA[gi], dscA[gi])
        gap = gtc.bf(0, 128, 1024 * gi, 1024 * (gi + 1))
        gbuf = gtcb[gi]
        for half in range(2):
            mms = []
            for d in range(DT):
                for s3 in range(3):
                    col = 1536 * half + 512 * s3
                    mms.append((lambda t, d=d, col=col, s3=s3, gi=gi, B0=B0: t.matmul(
                        c.P[B0 + s3][:, 0:512],
                        xncT.bf(0, 128, 1024 * gi + 128 * d, 1024 * gi + 128 * (d + 1)),
                        wqkv.bf(0, 128, 3072 * d + col, 3072 * d + col + 512),
                        start=(d == 0), stop=False), [xncTb[gi], wqkv.sbufs[d]]))
            tail = [(lambda t, half=half, s3=s3, B0=B0: t.matmul(
                c.P[B0 + s3][:, 0:512], c.ones.bf(0, 1, 0, 128),
                brow.bf(0, 1, 1536 * half + 512 * s3, 1536 * half + 512 * (s3 + 1)),
                start=False, stop=True), [c.ones.buf, brow.buf]) for s3 in range(3)]
            mm_group(c, [B0, B0 + 1, B0 + 2], mms, tail=tail)
            if half == 0:
                qf = scr.f32(0, 128, 0, 1024)
                for s3 in range(2):
                    p.emit("vector", lambda v, s3=s3, qf=qf, gap=gap, B0=B0: v.tensor_mul(
                        qf[:, 512 * s3:512 * (s3 + 1)], c.P[B0 + s3][:, 0:512],
                        gap[:, 512 * s3:512 * (s3 + 1)]),
                           reads=[c.BP[B0 + s3], gbuf], writes=[scr.buf])
                p.emit("vector", lambda v, qf=qf: v.scalar_tensor_tensor(
                    out=qf, in0=qf, scalar=0.0, in1=qf, op0=ALU.max, op1=ALU.mult),
                       reads=[scr.buf], writes=[scr.buf])
                p.emit("vector", lambda v, qf=qf, q2=q2: v.tensor_scalar(
                    out=q2.bf(0, 128, 0, 1024), in0=qf, scalar1=EPS, scalar2=None,
                    op0=ALU.add), reads=[scr.buf], writes=[q2.buf])
                kf = scr.f32(0, 128, 1024, 1536)
                p.emit("vector", lambda v, kf=kf, gap=gap, B0=B0: v.tensor_mul(
                    kf, c.P[B0 + 2][:, 0:512], gap[:, 0:512]),
                       reads=[c.BP[B0 + 2], gbuf], writes=[scr.buf])
            else:
                kf2 = scr.f32(0, 128, 1536, 2048)
                p.emit("vector", lambda v, kf2=kf2, gap=gap, B0=B0: v.tensor_mul(
                    kf2, c.P[B0][:, 0:512], gap[:, 512:1024]),
                       reads=[c.BP[B0], gbuf], writes=[scr.buf])
                kfull = scr.f32(0, 128, 1024, 2048)
                p.emit("vector", lambda v, kfull=kfull: v.scalar_tensor_tensor(
                    out=kfull, in0=kfull, scalar=0.0, in1=kfull,
                    op0=ALU.max, op1=ALU.mult),
                       reads=[scr.buf], writes=[scr.buf])
                p.emit("vector", lambda v, kfull=kfull, k2=k2: v.tensor_scalar(
                    out=k2.bf(0, 128, 0, 1024), in0=kfull, scalar1=EPS, scalar2=None,
                    op0=ALU.add), reads=[scr.buf], writes=[k2.buf])
                for s3 in range(2):
                    p.emit("scalar", lambda s, s3=s3, vv=vv, B0=B0: s.activation(
                        vv.bf(0, 128, 512 * s3, 512 * (s3 + 1)),
                        c.P[B0 + 1 + s3][:, 0:512], AF.Copy, bias=0.0, scale=1.0),
                           reads=[c.BP[B0 + 1 + s3]], writes=[vv.buf])
        if ch + 1 < min(TT, nch_lim):
            prefetch_chunk(ch + 1)
        trans_batch(c, [(q2.bf(0, 128, 128 * d, 128 * (d + 1)), q2.buf)
                        for d in range(DT)], TB, q2T.bf(0, 128, 0, 1024), q2T.buf)
        trans_batch(c, [(k2.bf(0, 128, 128 * d, 128 * (d + 1)), k2.buf)
                        for d in range(DT)], TB, k2T.bf(0, 128, 0, 1024), k2T.buf)
        for hp in range(NH // 2):
            for odd in range(2):
                hh = 2 * hp + odd
                p.emit("vector", lambda v, hp=hp, odd=odd, hh=hh: v.tensor_copy(
                    kvblk.bf(64 * odd, 64 * odd + 64, 128 * hp + 64 * odd,
                             128 * hp + 64 * odd + 64),
                    kvacc.f32(0, 64, 64 * hh, 64 * (hh + 1))),
                       reads=[kvacc.buf], writes=[kvblk.buf])
                p.emit("vector", lambda v, hh=hh, odd=odd: v.tensor_copy(
                    ksblk.bf(64 * odd, 64 * odd + 64, hh, hh + 1),
                    ksacc.f32(0, 64, hh, hh + 1)),
                       reads=[ksacc.buf], writes=[ksblk.buf])
        for rnd in range(6):
            heads = list(range(3 * rnd, min(3 * rnd + 3, NH)))
            for i, hh in enumerate(heads):
                base = 64 * (hh % 2)
                p.emit("tensor", lambda t, hh=hh, i=i, base=base, B0=B0,
                       q2T=q2T, k2T=k2T: t.matmul(
                    c.P[B0 + i][:, 0:128],
                    k2T.bf(base, base + 64, 128 * (hh // 2), 128 * (hh // 2) + 128),
                    q2T.bf(base, base + 64, 128 * (hh // 2), 128 * (hh // 2) + 128),
                    start=True, stop=True),
                       reads=[k2T.buf, q2T.buf],
                       writes=[c.BP[B0 + i]], war=[c.BP[B0 + i]],
                       first=True, last=True)
            for i, hh in enumerate(heads):
                p.emit("vector", lambda v, hh=hh, i=i, B0=B0, at=at: v.tensor_mul(
                    at.bf(0, 128, 128 * hh, 128 * (hh + 1)),
                    c.P[B0 + i][:, 0:128],
                    c.maskA.bf(0, 128, 0, 128)),
                       reads=[c.BP[B0 + i], c.maskA.buf], writes=[at.buf])
        for half in range(2):
            bank = B0 + half
            mms = [(lambda t, hh=8 * half + i, i=i, bank=bank, at=at, vv=vv: t.matmul(
                c.P[bank][:, 64 * i:64 * (i + 1)],
                at.bf(0, 128, 128 * hh, 128 * (hh + 1)),
                vv.bf(0, 128, 64 * hh, 64 * (hh + 1)),
                start=(i == 0), stop=False), [at.buf, vv.buf]) for i in range(8)]
            tail = [(lambda t, hp=4 * half + j, j=j, bank=bank, q2T=q2T: t.matmul(
                c.P[bank][:, 128 * j:128 * (j + 1)],
                q2T.bf(0, 128, 128 * hp, 128 * (hp + 1)),
                kvblk.bf(0, 128, 128 * hp, 128 * (hp + 1)),
                start=False, stop=(j == 3)), [q2T.buf, kvblk.buf]) for j in range(4)]
            mm_group(c, [bank], mms, tail=tail)
        DB = B0 + 2
        mms = [(lambda t, hh=hh, at=at, DB=DB: t.matmul(
            c.P[DB][:, hh:hh + 1],
            at.bf(0, 128, 128 * hh, 128 * (hh + 1)),
            c.ones.bf(0, 128, 0, 1),
            start=(hh == 0), stop=False), [at.buf, c.ones.buf]) for hh in range(NH)]
        tail = [(lambda t, hp=hp, q2T=q2T, DB=DB: t.matmul(
            c.P[DB][:, 2 * hp:2 * hp + 2],
            q2T.bf(0, 128, 128 * hp, 128 * (hp + 1)),
            ksblk.bf(0, 128, 2 * hp, 2 * hp + 2),
            start=False, stop=(hp == NH // 2 - 1)), [q2T.buf, ksblk.buf])
            for hp in range(NH // 2)]
        mm_group(c, [DB], mms, tail=tail)
        dn = dsc.f32(0, 128, 0, 16)
        p.emit("vector", lambda v, dn=dn, DB=DB: v.tensor_scalar(
            out=dn, in0=c.P[DB][:, 0:16], scalar1=EPS, scalar2=None, op0=ALU.add),
               reads=[c.BP[DB]], writes=[dsc.buf])
        p.emit("vector", lambda v, dn=dn: v.reciprocal(dn, dn),
               reads=[dsc.buf], writes=[dsc.buf])
        for hh in range(NH):
            bank = B0 + hh // 8
            p.emit("vector", lambda v, hh=hh, bank=bank, dn=dn, outc=outc: v.tensor_scalar(
                out=outc.bf(0, 128, 64 * hh, 64 * (hh + 1)),
                in0=c.P[bank][:, 64 * (hh % 8):64 * (hh % 8 + 1)],
                scalar1=dn[:, hh:hh + 1], scalar2=None, op0=ALU.mult),
                   reads=[c.BP[bank], dsc.buf], writes=[outc.buf])
        for half in range(2):
            bank = B0 + half
            mm_group(c, [bank],
                     [(lambda t, hh=8 * half + i, i=i, bank=bank, k2=k2, vv=vv: t.matmul(
                         c.P[bank][0:64, 64 * i:64 * (i + 1)],
                         k2.bf(0, 128, 64 * hh, 64 * (hh + 1)),
                         vv.bf(0, 128, 64 * hh, 64 * (hh + 1)),
                         start=(i == 0), stop=(i == 7)), [k2.buf, vv.buf])
                      for i in range(8)])
        mm_group(c, [DB],
                 [(lambda t, hh=hh, k2=k2, DB=DB: t.matmul(
                     c.P[DB][0:64, 32 + hh:33 + hh],
                     k2.bf(0, 128, 64 * hh, 64 * (hh + 1)),
                     c.ones.bf(0, 128, 0, 1),
                     start=(hh == 0), stop=(hh == NH - 1)), [k2.buf, c.ones.buf])
                  for hh in range(NH)])
        for half in range(2):
            bank = B0 + half
            p.emit("vector", lambda v, half=half, bank=bank: v.tensor_add(
                kvacc.f32(0, 64, 512 * half, 512 * (half + 1)),
                kvacc.f32(0, 64, 512 * half, 512 * (half + 1)),
                c.P[bank][0:64, 0:512]),
                   reads=[c.BP[bank], kvacc.buf], writes=[kvacc.buf])
        p.emit("vector", lambda v, DB=DB: v.tensor_add(
            ksacc.f32(0, 64, 0, 16), ksacc.f32(0, 64, 0, 16), c.P[DB][0:64, 32:48]),
               reads=[c.BP[DB], ksacc.buf], writes=[ksacc.buf])
        trans_batch(c, [(outc.bf(0, 128, 128 * d, 128 * (d + 1)), outc.buf)
                        for d in range(DT)], TB, outT.bf(0, 128, 0, 1024), outT.buf)
        mms = []
        for d in range(DT):
            for h in range(2):
                mms.append((lambda t, d=d, h=h, outT=outT, B0=B0: t.matmul(
                    c.P[B0 + h][:, 0:512],
                    outT.bf(0, 128, 128 * d, 128 * (d + 1)),
                    wproj.bf(0, 128, 1024 * d + 512 * h, 1024 * d + 512 * (h + 1)),
                    start=(d == 0), stop=False), [outT.buf, wproj.sbufs[d]]))
        tail = [(lambda t, h=h, B0=B0: t.matmul(
            c.P[B0 + h][:, 0:512], c.ones.bf(0, 1, 0, 128),
            brow.bf(0, 1, 3072 + 512 * h, 3072 + 512 * (h + 1)),
            start=False, stop=True), [c.ones.buf, brow.buf]) for h in range(2)]
        mm_group(c, [B0, B0 + 1], mms, tail=tail)
        for h in range(2):
            p.emit("vector", lambda v, ch=ch, h=h, B0=B0: v.tensor_add(
                c.xres(ch)[:, 512 * h:512 * (h + 1)],
                c.xres(ch)[:, 512 * h:512 * (h + 1)], c.P[B0 + h][:, 0:512]),
                   reads=[c.BP[B0 + h], c.Bx[ch]], writes=[c.Bx[ch]])
    fold_bufs(gtc, gtcb)
    fold_bufs(xnc2, xncb)
    fold_bufs(xncT, xncTb)
    free_wslabs(c, wqkv)
    free_wslabs(c, wproj)
    for r in ([brow, kvacc, ksacc, kvblk, ksblk, xncT, gtc, xnc2, scr] + q2A +
              k2A + vvA + q2TA + k2TA + atA + outcA + outTA + dscA):
        c.ar.freer(r)


def rebuild_xnT(c):
    p, ar, T, TT = c.p, c.ar, c.T, c.TT
    xnT = ar.alloc(DT * T, "xnT1r")
    scr = ar.alloc(1024, "rbscr")
    st = c.stats
    for t in range(TT):
        m = st.f32(0, 128, t, t + 1)
        r = st.f32(0, 128, TT + t, TT + t + 1)
        xn = scr.bf(0, 128, 0, 1024)
        p.emit("vector", lambda v, t=t, xn=xn, m=m, r=r: v.tensor_scalar(
            out=xn, in0=c.xres(t), scalar1=m, scalar2=r,
            op0=ALU.subtract, op1=ALU.mult),
               reads=[c.Bx[t], st.buf], writes=[scr.buf])
        trans_batch(c, [(xn[:, 128 * d:128 * (d + 1)], scr.buf) for d in range(DT)],
                    7, xnT_dst(c, xnT, t), xnT.buf)
    c.ar.freer(scr)
    return xnT


def swiglu(c, xnT):
    p, ar, T, TT = c.p, c.ar, c.T, c.TT
    njh = F2 // 2 // 128
    hreg = ar.alloc(njh * T, "swh")
    wbuf = [ar.alloc(1024, f"sww{i}") for i in range(8)]
    w3buf = [ar.alloc(1024, f"sww3{i}") for i in range(4)]
    hscr = ar.alloc(2 * 512, "swhscr")
    nseg = max(T // 512, 1)
    seglen = min(512, T)
    for fh in range(2):
        j0 = fh * njh
        for j in range(njh):
            wb1 = wbuf[(j % 4) * 2]
            wb2 = wbuf[(j % 4) * 2 + 1]
            src1 = bass.AP(c.sw_gate.tensor, c.sw_gate.offset + (j0 + j) * 128 * D,
                           [[D, 128], [1, D]])
            src2 = bass.AP(c.sw_up.tensor, c.sw_up.offset + (j0 + j) * 128 * D,
                           [[D, 128], [1, D]])
            p.dma("gpsimd", wb1.bf(0, 128, 0, 1024), src1,
                  writes=[wb1.buf], war=[wb1.buf])
            p.dma("gpsimd", wb2.bf(0, 128, 0, 1024), src2,
                  writes=[wb2.buf], war=[wb2.buf])
            bk = 0 if (j % 2 == 0) else 4
            for seg in range(nseg):
                g0, g1 = seglen * seg, seglen * (seg + 1)
                b1k = bk + (seg % 2)
                b2k = bk + 2 + (seg % 2)
                for bank, wb in ((b1k, wb1), (b2k, wb2)):
                    mm_group(c, [bank],
                             [(lambda t, d=d, g0=g0, g1=g1, wb=wb, bank=bank: t.matmul(
                                 c.P[bank][:, 0:g1 - g0],
                                 wb.bf(0, 128, 128 * d, 128 * (d + 1)),
                                 xnT.bf(0, 128, T * d + g0, T * d + g1),
                                 start=(d == 0), stop=(d == DT - 1)),
                               [wb.buf] + xn_tiles(xnT, g0, g1)) for d in range(DT)])
                p.emit("scalar", lambda s, g1=g1, g0=g0, b1k=b1k: s.activation(
                    hscr.f32(0, 128, 0, g1 - g0), c.P[b1k][:, 0:g1 - g0],
                    AF.Silu, bias=0.0, scale=1.0),
                       reads=[c.BP[b1k]], writes=[hscr.buf])
                p.emit("vector", lambda v, j=j, g0=g0, g1=g1, b2k=b2k: v.tensor_mul(
                    hreg.bf(0, 128, T * j + g0, T * j + g1),
                    hscr.f32(0, 128, 0, g1 - g0), c.P[b2k][:, 0:g1 - g0]),
                       reads=[c.BP[b2k], hscr.buf], writes=[hreg.buf])
        PTOK = min(512, T)
        npass = T // PTOK
        ntt = PTOK // 128
        for ps in range(npass):
            pt0 = PTOK * ps
            for k in range(njh):
                w3b = w3buf[k % 4]
                cb = c.Bswdbf[j0 + k]
                if ps == 0:
                    p.dma("gpsimd", w3b.bf(0, 128, 0, 1024),
                          bass.AP(c.sw_down.tensor,
                                  c.sw_down.offset + 128 * (j0 + k) * D,
                                  [[D, 128], [1, D]]),
                          writes=[w3b.buf], war=[w3b.buf])
                    p.dma("sync", c.swdbf[j0 + k], w3b.bf(0, 128, 0, 1024),
                          reads=[w3b.buf], writes=[cb])
                else:
                    p.dma("scalar", w3b.bf(0, 128, 0, 1024), c.swdbf[j0 + k],
                          reads=[cb], writes=[w3b.buf], war=[w3b.buf])
                for tt4 in range(ntt):
                    for hf in range(2):
                        bank = tt4 * 2 + hf
                        lk = (k == njh - 1)
                        k_last = (tt4 == ntt - 1 and hf == 1)
                        p.emit("tensor", lambda t, k=k, tt4=tt4, hf=hf, w3b=w3b,
                               bank=bank, pt0=pt0: t.matmul(
                            c.P[bank][:, 0:512],
                            hreg.bf(0, 128, T * k + pt0 + 128 * tt4,
                                    T * k + pt0 + 128 * (tt4 + 1)),
                            w3b.bf(0, 128, 512 * hf, 512 * (hf + 1)),
                            start=(k == 0), stop=(k == njh - 1)),
                               reads=[hreg.buf, w3b.buf],
                               writes=([c.BP[bank]] if lk else []),
                               war=([c.BP[bank]] if k == 0 else ()),
                               sem=("" if (lk or k_last) else None),
                               first=(k == 0), last=lk)
            for tt4 in range(ntt):
                tglob = pt0 // 128 + tt4
                for hf in range(2):
                    bank = tt4 * 2 + hf
                    p.emit("vector", lambda v, tglob=tglob, hf=hf, bank=bank:
                           v.tensor_add(
                        c.xres(tglob)[:, 512 * hf:512 * (hf + 1)],
                        c.xres(tglob)[:, 512 * hf:512 * (hf + 1)],
                        c.P[bank][:, 0:512]),
                           reads=[c.BP[bank], c.Bx[tglob]], writes=[c.Bx[tglob]])
    for r in [hreg, hscr] + wbuf + w3buf:
        c.ar.freer(r)


def _consts_np(inputs, parts):
    l1 = any(s.startswith("L1") for s in parts)
    cst = np.zeros((128, 3592), np.float32)
    crow = np.zeros((1, 12288), np.float32)
    if "L0" in parts:
        cst[:, 0:8] = inputs["L0_ffn_br"][None, :]
        cst[:, 8:264] = np.asarray(inputs["L0_ffn_b1"]).reshape(NE, 32, 128).transpose(2, 0, 1).reshape(128, 256)
        cst[:, 264:520] = np.asarray(inputs["L0_ffn_b2"]).reshape(NE, 32, 128).transpose(2, 0, 1).reshape(128, 256)
        crow[0, 0:8192] = np.asarray(inputs["L0_ffn_b3"]).reshape(-1)
    if l1:
        cst[:, 520:1544] = inputs["L1_attn_ln_g"][None, :]
        cst[:, 1544:2568] = inputs["L1_attn_ln_b"][None, :]
        cst[:, 2568:3592] = inputs["L1_attn_b_gate"][None, :]
        crow[0, 8192:11264] = np.asarray(inputs["L1_attn_b_qkv"])
        crow[0, 11264:12288] = np.asarray(inputs["L1_attn_b_proj"])
    return cst, crow


def _in_map(inputs, xcore, parts):
    cst, crow = _consts_np(inputs, parts)
    m = {"x": xcore, "consts": cst, "crow": crow}
    def _packj(w, nj):
        w = np.asarray(w)
        E = w.shape[0]
        return w.reshape(E, DT, 128, nj, 128).transpose(0, 3, 2, 1, 4).reshape(E, nj, 128, DT * 128)

    if "L0" in parts:
        m["Wq"] = inputs["L0_attn_Wq"]
        m["Wk"] = inputs["L0_attn_Wk"]
        m["Wv"] = inputs["L0_attn_Wv"]
        m["Wr"] = inputs["L0_ffn_Wr"]
        m["W1"] = _packj(inputs["L0_ffn_W1"], DFF0 // 128)
        m["W2"] = _packj(inputs["L0_ffn_W2"], DFF0 // 128)
        m["W3"] = inputs["L0_ffn_W3"]
    if any(s.startswith("L1") for s in parts):
        m["Wqkv"] = inputs["L1_attn_W_qkv"]
        m["Wgate"] = inputs["L1_attn_W_gate"]
        m["Wproj"] = inputs["L1_attn_W_proj"]
        m["gwT"] = np.asarray(inputs["L1_ffn_gate_w"]).T
        m["w_gate"] = _packj(inputs["L1_ffn_w_gate"], F1 // 128)
        m["w_up"] = _packj(inputs["L1_ffn_w_up"], F1 // 128)
        m["w_down"] = inputs["L1_ffn_w_down"]
        m["sw_gate"] = _packj(np.asarray(inputs["L1_ffn_sw_gate"])[None], F2 // 128)[0]
        m["sw_up"] = _packj(np.asarray(inputs["L1_ffn_sw_up"])[None], F2 // 128)[0]
        m["sw_down"] = inputs["L1_ffn_sw_down"]
    return {k: np.ascontiguousarray(np.asarray(v, dtype=np.float32)) for k, v in m.items()}


def _tune_compiler():
    pass


def kernel(**inputs):
    _tune_compiler()
    x = np.asarray(inputs["L0_attn_x"], dtype=np.float32)
    B, T, _ = x.shape
    nc = build_core_program(T, ("L0", "L1"))
    in_maps = [_in_map(inputs, x[i], ("L0", "L1")) for i in range(B)]
    res = run_bass_kernel_spmd(nc, in_maps, core_ids=list(range(B)))
    out = np.stack([np.asarray(r["OUT"]) for r in res.results], axis=0)
    return out.astype(np.float32)
